# version 14
# baseline (speedup 1.0000x reference)
import dataclasses
from contextlib import ExitStack

import numpy as np

import concourse.bass as bass
import concourse.mybir as mybir
from concourse.bass_utils import run_bass_kernel_spmd

F32 = mybir.dt.float32
BF16 = mybir.dt.bfloat16
I32 = mybir.dt.int32
AF = mybir.ActivationFunctionType
ALU = mybir.AluOpType

B, L, D, DIN, H, NS = 2, 8192, 1024, 2048, 3072, 16
TC = 512
NCH = 16
NJ = 4
KD = 8
NMI = 4
HM = H // 128
EPS = 1e-5
GROUPS = [[0, 1, 2, 3], [4, 5, 6, 7]]
ENGINES = ("sync", "vector", "scalar", "tensor", "gpsimd")


def bc_mid(ap, rep):
    assert len(ap.ap) == 2
    return dataclasses.replace(ap, ap=[ap.ap[0], [0, rep], ap.ap[1]])


class Ctr:
    def __init__(self, h):
        self.h = h
        self.n = 0


class Bld:
    def __init__(self, nc, ctx):
        self.nc = nc
        self.ops = {e: [] for e in ENGINES}
        E = ctx.enter_context
        self.LD = Ctr(E(nc.semaphore("LD")))
        self.CC = Ctr(E(nc.semaphore("CC")))
        self.VC = Ctr(E(nc.semaphore("VC")))
        self.SC = Ctr(E(nc.semaphore("SC")))
        self.TM = Ctr(E(nc.semaphore("TM")))
        self.PC = Ctr(E(nc.semaphore("PC")))
        self.CD = Ctr(E(nc.semaphore("CD")))
        self.WQ = Ctr(E(nc.semaphore("WQ")))
        self.BAR = Ctr(E(nc.semaphore("BAR")))

    def op(self, eng, build, ctr=None):
        if ctr is None:
            self.ops[eng].append(build)
            return None
        ctr.n += 1
        tgt = ctr.n
        self.ops[eng].append(lambda e, b=build, c=ctr: b(e).then_inc(c.h, 1))
        return tgt

    def dma(self, eng, build, ctr=None):
        ctr = ctr or self.LD
        ctr.n += 16
        tgt = ctr.n
        self.ops[eng].append(lambda e, b=build, c=ctr: b(e).then_inc(c.h, 16))
        return tgt

    def wait(self, eng, ctr, val):
        if val is None:
            return
        self.ops[eng].append(lambda e, c=ctr, v=val: e.wait_ge(c.h, v))

    def barrier(self):
        self.wait("sync", self.LD, self.LD.n)
        self.wait("sync", self.WQ, self.WQ.n)
        tgt = self.BAR.n + len(ENGINES)
        for e in ENGINES:
            self.ops[e].append(lambda E: E.sem_inc(self.BAR.h, 1))
        self.BAR.n = tgt
        for e in ENGINES:
            self.ops[e].append(lambda E, v=tgt: E.wait_ge(self.BAR.h, v))

    def coll(self, kind, op_, ins, outs, wait_ld=None):
        if wait_ld is not None:
            self.wait("gpsimd", self.LD, wait_ld)
        self.CC.n += 1
        tgt = self.CC.n
        self.ops["gpsimd"].append(lambda e, k=kind, o=op_, i=ins, u=outs: e.collective_compute(
            k, o, GROUPS, i, u).then_inc(self.CC.h, 1))
        self.wait("gpsimd", self.CC, tgt)
        self.ops["gpsimd"].append(lambda e: e.drain(fusable=False))
        self.CD.n += 1
        cdt = self.CD.n
        self.ops["gpsimd"].append(lambda e, v=cdt: e.sem_inc(self.CD.h, 1))
        return cdt

    def emit(self, block):
        b = self
        @block.sync
        def _(eng):
            for f in b.ops["sync"]:
                f(eng)
        @block.vector
        def _(eng):
            for f in b.ops["vector"]:
                f(eng)
        @block.scalar
        def _(eng):
            for f in b.ops["scalar"]:
                f(eng)
        @block.tensor
        def _(eng):
            for f in b.ops["tensor"]:
                f(eng)
        @block.gpsimd
        def _(eng):
            for f in b.ops["gpsimd"]:
                f(eng)


def build_program(debug=False, dbg_parts=("p1","scan","wop","rs","p3","dty")):
    nc = bass.Bass("TRN2", target_bir_lowering=False)

    def din(name, shape, dt=F32):
        return nc.dram_tensor(name, shape, dt, kind="ExternalInput").ap()

    x_fm = din("x_fm", [D, NJ * TC])
    winc = din("winc", [D, 1024])
    woc = din("woc", [512, D])
    wxpc = din("wxpc", [512, 33])
    wgT = din("wgT", [D, H], I32)
    wuT = din("wuT", [D, H], I32)
    wdT = din("wdT", [H, D], I32)
    cwc = din("cwc", [128, 16])
    avals = din("avals", [128, 64])
    vecs = din("vecs", [128, 16])
    lngb = din("lngb", [128, 16])
    sel_i = din("sel", [16, 2048])
    ones128 = din("ones128", [128, 128])
    ones1 = din("ones1", [1, 128])

    o_y = nc.dram_tensor("o_y", [D, NJ * TC], F32, kind="ExternalOutput").ap()

    def dint(name, shape, dt=F32):
        return nc.dram_tensor(name, shape, dt, kind="Internal")

    d_wg = dint("d_wg", [D, H], BF16)
    d_wu = dint("d_wu", [D, H], BF16)
    d_wd = dint("d_wd", [H, D], BF16)
    d_winb = dint("d_winb", [D, 1024], BF16)
    d_wob = dint("d_wob", [512, D], BF16)
    d_x1 = dint("d_x1", [D, NJ * TC], F32)
    d_ln2 = dint("d_ln2", [D, NJ * TC], F32)
    d_xn = dint("d_xn", [NJ * D, TC], BF16)
    d_xng = dint("d_xng", [NJ * 4 * D, TC], BF16)
    d_xc = dint("d_xc", [512, L], F32)
    d_z = dint("d_z", [512, L], F32)
    d_xpj = dint("d_xpj", [33, L], F32)
    d_xpjr = dint("d_xpjr", [33, L], F32)
    d_wop_j = [dint(f"d_wop{j}", [4 * D, TC], F32) for j in range(NJ)]
    d_att_j = [dint(f"d_att{j}", [D, TC], F32) for j in range(NJ)]

    dbg = {}
    if debug:
        for name, shape, dt in [
            ("dbg_xn", [D, NJ * TC], BF16), ("dbg_x1", [D, NJ * TC], F32),
            ("dbg_dt", [128, NMI * TC], F32),
            ("dbg_y", [128, NMI * TC], F32), ("dbg_att", [NJ * D, TC], F32),
            ("dbg_u", [128, NMI * TC], F32), ("dbg_a0", [128, NMI * TC], F32),
            ("dbg_b0", [128, NMI * TC], F32), ("dbg_h0", [128, NMI * TC], F32),
            ("dbg_y0", [128, NMI * TC], F32), ("dbg_wop", [4 * D, TC], F32),
            ("dbg_wop2", [4 * D, TC], F32),
        ]:
            dbg[name] = nc.dram_tensor(name, shape, dt, kind="ExternalOutput").ap()

    x_fm_r = x_fm.rearrange("(k p) t -> p k t", p=128)
    d_x1_r = d_x1.ap().rearrange("(k p) t -> p k t", p=128)
    d_ln2_r = d_ln2.ap().rearrange("(k p) t -> p k t", p=128)
    d_xc_r = d_xc.ap().rearrange("(m p) t -> p m t", p=128)
    d_z_r = d_z.ap().rearrange("(m p) t -> p m t", p=128)
    o_y_r = o_y.rearrange("(k p) t -> p k t", p=128)

    with ExitStack() as ctx:
        E = ctx.enter_context
        P = Bld(nc, ctx)
        LD, CC, VC, SC, TM, PC, CD, WQ = P.LD, P.CC, P.VC, P.SC, P.TM, P.PC, P.CD, P.WQ

        def sb(name, shape, dt=F32):
            return E(nc.sbuf_tensor(name, shape, dt))

        mu_t = sb("mu_t", [128, TC])
        ms_t = sb("ms_t", [128, TC])
        var_t = sb("var_t", [128, TC])
        rstd_t = sb("rstd_t", [128, TC])
        f1buf = sb("f1buf", [128, TC])
        cwt = sb("cwt", [128, 16])
        Aneg = sb("Aneg", [128, 64])
        vect = sb("vect", [128, 16])
        lngbt = sb("lngbt", [128, 16])
        onesb = sb("onesb", [128, 128], BF16)
        ones1t = sb("ones1t", [1, 128])
        eps_t = sb("eps_t", [128, 1])
        one_t = sb("one_t", [128, 1])
        selb = sb("selb", [16, 2048], BF16)
        st = sb("st", [128, NMI, NS])

        PS = [E(nc.psum_tensor(f"ps{i}", [128, TC], F32)) for i in range(8)]

        with nc.sbuf_tensor("deqi0", [128, H], I32) as deqi0, \
             nc.sbuf_tensor("deqi1", [128, H], I32) as deqi1, \
             nc.sbuf_tensor("deqb0", [128, H], BF16) as deqb0, \
             nc.sbuf_tensor("deqb1", [128, H], BF16) as deqb1, \
             nc.sbuf_tensor("wfl", [128, 1024], F32) as wfl, \
             nc.sbuf_tensor("p0sel", [16, 2048], F32) as p0sel:
            deqi = [deqi0, deqi1]
            deqb = [deqb0, deqb1]

            P.op("vector", lambda e: e.memset(eps_t[:], float(EPS)))
            vcn = P.op("vector", lambda e: e.memset(one_t[:], 1.0), VC)
            P.wait("scalar", VC, vcn)
            P.dma("sync", lambda e: e.dma_start(out=cwt[:], in_=cwc))
            P.dma("sync", lambda e: e.dma_start(out=vect[:], in_=vecs))
            P.dma("sync", lambda e: e.dma_start(out=lngbt[:], in_=lngb))
            P.dma("sync", lambda e: e.dma_start(out=ones1t[:], in_=ones1))
            t5 = P.dma("sync", lambda e: e.dma_start(out=mu_t[:, 0:64], in_=avals))
            P.wait("scalar", LD, t5)
            s1 = P.op("scalar", lambda e: e.activation(ms_t[:, 0:64], mu_t[:, 0:64], AF.Exp), SC)
            P.wait("vector", SC, s1)
            P.op("vector", lambda e: e.tensor_scalar(Aneg[:], ms_t[:, 0:64], -1.0, None, ALU.mult), VC)
            t6 = P.dma("sync", lambda e: e.dma_start(out=rstd_t[:, 0:128], in_=ones128))
            P.wait("scalar", LD, t6)
            P.op("scalar", lambda e: e.activation(onesb[:], rstd_t[:, 0:128], AF.Copy), SC)
            t7 = P.dma("sync", lambda e: e.dma_start(out=p0sel[:], in_=sel_i))
            P.wait("scalar", LD, t7)
            P.op("scalar", lambda e: e.activation(selb[:], p0sel[:], AF.Copy), SC)

            def dequant(src_ap, dst_t, nblocks, width):
                vvs = [None] * nblocks
                dmas = [None] * nblocks
                for i in range(nblocks):
                    bb = i % 2
                    if i >= 2:
                        P.wait("sync", VC, vvs[i - 2])
                    tl = P.dma("sync", lambda e, i=i, bb=bb, s=src_ap, w=width: e.dma_start(
                        out=deqi[bb][:, 0:w], in_=s[128 * i:128 * (i + 1), 0:w]))
                    P.wait("vector", LD, tl)
                    if i >= 2:
                        P.wait("vector", LD, dmas[i - 2])
                    vvs[i] = P.op("vector", lambda e, bb=bb, w=width: e.tensor_scalar(
                        deqb[bb][:, 0:w], deqi[bb][:, 0:w], 1.0 / 127.0, None, ALU.mult), VC)
                    P.wait("sync", VC, vvs[i])
                    dmas[i] = P.dma("sync", lambda e, i=i, bb=bb, d=dst_t, w=width: e.dma_start(
                        out=d.ap()[128 * i:128 * (i + 1), 0:w], in_=deqb[bb][:, 0:w]))

            dequant(wgT, d_wg, KD, H)
            dequant(wuT, d_wu, KD, H)
            dequant(wdT, d_wd, HM, D)

            def stage(src, dst, nblocks):
                prev = {"sv": None, "dma": None}
                for i in range(nblocks):
                    if prev["sv"] is not None:
                        P.wait("sync", SC, prev["sv"])
                    tl = P.dma("sync", lambda e, i=i, s=src: e.dma_start(
                        out=wfl[:], in_=s[128 * i:128 * (i + 1), :]))
                    P.wait("scalar", LD, tl)
                    if prev["dma"] is not None:
                        P.wait("scalar", LD, prev["dma"])
                    sv = P.op("scalar", lambda e: e.activation(deqb[0][:, 0:1024], wfl[:], AF.Copy), SC)
                    prev["sv"] = sv
                    P.wait("sync", SC, sv)
                    prev["dma"] = P.dma("sync", lambda e, i=i, d=dst: e.dma_start(
                        out=d.ap()[128 * i:128 * (i + 1), :], in_=deqb[0][:, 0:1024]))
            stage(winc, d_winb, KD)
            stage(woc, d_wob, NMI)
            P.barrier()

        def ln_stats(T, src, wait_vc=None):
            if wait_vc is not None:
                P.wait("scalar", VC, wait_vc)
            c1 = P.op("scalar", lambda e: e.activation(T["xbb"][:], src[:], AF.Copy), SC)
            c2 = P.op("scalar", lambda e: e.activation(T["xnb"][:], src[:], AF.Square), SC)
            P.wait("tensor", SC, c2)
            for kb in range(KD):
                P.op("tensor", lambda e, kb=kb: e.matmul(
                    PS[4][:], onesb[:], T["xbb"][:, kb, :], start=(kb == 0), stop=(kb == KD - 1)),
                    TM if kb == KD - 1 else None)
            tm2 = None
            for kb in range(KD):
                tm2 = P.op("tensor", lambda e, kb=kb: e.matmul(
                    PS[5][:], onesb[:], T["xnb"][:, kb, :], start=(kb == 0), stop=(kb == KD - 1)),
                    TM if kb == KD - 1 else None)
            P.wait("vector", TM, tm2)
            P.op("vector", lambda e: e.tensor_scalar(mu_t[:], PS[4][:], 1.0 / D, None, ALU.mult))
            P.op("vector", lambda e: e.tensor_scalar(ms_t[:], PS[5][:], 1.0 / D, None, ALU.mult))
            P.op("vector", lambda e: e.tensor_tensor(var_t[:], mu_t[:], mu_t[:], ALU.mult))
            v1 = P.op("vector", lambda e: e.tensor_tensor(var_t[:], ms_t[:], var_t[:], ALU.subtract), VC)
            P.wait("scalar", VC, v1)
            P.op("scalar", lambda e: e.activation(rstd_t[:], var_t[:], AF.Ln, bias=eps_t[:, 0:1]))
            s2 = P.op("scalar", lambda e: e.activation(rstd_t[:], rstd_t[:], AF.Exp, scale=-0.5), SC)
            return s2

        def ffn(T):
            xs, xnb, fbuf = T["xs"], T["xnb"], T["fbuf"]
            wgt_s, wut_s, wdt_s = T["wgt_s"], T["wut_s"], T["wdt_s"]
            tus = [None] * HM
            fvs = [None] * HM
            f1s = [None] * HM
            for mh in range(HM):
                bw = mh % 2
                pg = PS[0 + 2 * bw]
                pu = PS[1 + 2 * bw]
                if mh >= 2:
                    P.wait("sync", TM, tus[mh - 2])
                P.dma("sync", lambda e, mh=mh, bw=bw: e.dma_start(
                    out=wgt_s[bw][:],
                    in_=d_wg.ap()[:, 128 * mh:128 * (mh + 1)].rearrange("(k p) m -> p k m", p=128)), ctr=WQ)
                lw2 = P.dma("sync", lambda e, mh=mh, bw=bw: e.dma_start(
                    out=wut_s[bw][:],
                    in_=d_wu.ap()[:, 128 * mh:128 * (mh + 1)].rearrange("(k p) m -> p k m", p=128)), ctr=WQ)
                P.wait("tensor", WQ, lw2)
                if mh >= 2:
                    P.wait("tensor", SC, f1s[mh - 2])
                    P.wait("tensor", VC, fvs[mh - 2])
                tg = None
                for kb in range(KD):
                    tg = P.op("tensor", lambda e, kb=kb, bw=bw, pg=pg: e.matmul(
                        pg[:], wgt_s[bw][:, kb, :], xnb[:, kb, :],
                        start=(kb == 0), stop=(kb == KD - 1)), TM if kb == KD - 1 else None)
                for kb in range(KD):
                    tus[mh] = P.op("tensor", lambda e, kb=kb, bw=bw, pu=pu: e.matmul(
                        pu[:], wut_s[bw][:, kb, :], xnb[:, kb, :],
                        start=(kb == 0), stop=(kb == KD - 1)), TM if kb == KD - 1 else None)
                P.wait("scalar", TM, tg)
                if mh >= 1:
                    P.wait("scalar", VC, fvs[mh - 1])
                f1s[mh] = P.op("scalar", lambda e, pg=pg: e.activation(f1buf[:], pg[:], AF.Silu), SC)
                P.wait("vector", SC, f1s[mh])
                P.wait("vector", TM, tus[mh])
                fvs[mh] = P.op("vector", lambda e, mh=mh, pu=pu: e.tensor_tensor(
                    fbuf[:, mh, :], f1buf[:], pu[:], ALU.mult), VC)
            xv = None
            tds = [None] * KD
            for mo in range(KD):
                bw = mo % 2
                pd = PS[6 + bw]
                if mo >= 2:
                    P.wait("sync", TM, tds[mo - 2])
                lw = P.dma("sync", lambda e, mo=mo, bw=bw: e.dma_start(
                    out=wdt_s[bw][:],
                    in_=d_wd.ap()[:, 128 * mo:128 * (mo + 1)].rearrange("(k p) m -> p k m", p=128)), ctr=WQ)
                P.wait("tensor", WQ, lw)
                if mo == 0:
                    P.wait("tensor", VC, fvs[-1])
                if mo >= 2:
                    P.wait("tensor", VC, xv)
                for kh in range(HM):
                    tds[mo] = P.op("tensor", lambda e, kh=kh, bw=bw, pd=pd: e.matmul(
                        pd[:], wdt_s[bw][:, kh, :], fbuf[:, kh, :],
                        start=(kh == 0), stop=(kh == HM - 1)), TM if kh == HM - 1 else None)
                P.wait("vector", TM, tds[mo])
                xv = P.op("vector", lambda e, mo=mo, pd=pd: e.scalar_tensor_tensor(
                    xs[:, mo, :], pd[:], 0.5, xs[:, mo, :], ALU.mult, ALU.add), VC)
            return xv

        def ln_apply_plain(T, src, dst_bf):
            P.op("vector", lambda e: e.tensor_tensor(
                T["cen"][:], src[:], bc_mid(mu_t[:], KD), ALU.subtract))
            return P.op("vector", lambda e: e.tensor_tensor(
                dst_bf[:], T["cen"][:], bc_mid(rstd_t[:], KD), ALU.mult), VC)

        def alloc_ffn_scope(stack, tag):
            T = {}
            ES = stack.enter_context
            T["xs"] = ES(nc.sbuf_tensor(f"xs{tag}", [128, KD, TC], F32))
            T["ln2t"] = ES(nc.sbuf_tensor(f"ln2t{tag}", [128, KD, TC], F32))
            T["cen"] = ES(nc.sbuf_tensor(f"cen{tag}", [128, KD, TC], F32))
            T["xbb"] = ES(nc.sbuf_tensor(f"xbb{tag}", [128, KD, TC], BF16))
            T["xnb"] = ES(nc.sbuf_tensor(f"xnb{tag}", [128, KD, TC], BF16))
            T["fbuf"] = ES(nc.sbuf_tensor(f"fbuf{tag}", [128, HM, TC], BF16))
            T["wgt_s"] = [ES(nc.sbuf_tensor(f"wgt{tag}{i}", [128, KD, 128], BF16)) for i in range(2)]
            T["wut_s"] = [ES(nc.sbuf_tensor(f"wut{tag}{i}", [128, KD, 128], BF16)) for i in range(2)]
            T["wdt_s"] = [ES(nc.sbuf_tensor(f"wdt{tag}{i}", [128, HM, 128], BF16)) for i in range(2)]
            return T

        ag_done = [None] * NJ
        with ExitStack() as sA:
            T = alloc_ffn_scope(sA, "a")
            for j in range(NJ):
                jt = slice(TC * j, TC * (j + 1))
                lx = P.dma("sync", lambda e, jt=jt: e.dma_start(out=T["xs"][:], in_=x_fm_r[:, :, jt]))
                P.wait("scalar", LD, lx)
                P.wait("vector", LD, lx)
                s_r = ln_stats(T, T["xs"])
                P.wait("vector", SC, s_r)
                vx = ln_apply_plain(T, T["xs"], T["xnb"])
                P.wait("tensor", VC, vx)
                xv = ffn(T)
                P.wait("sync", VC, xv)
                P.dma("sync", lambda e, jt=jt: e.dma_start(out=d_x1_r[:, :, jt], in_=T["xs"][:]))
                s_r = ln_stats(T, T["xs"], wait_vc=xv)
                P.wait("vector", SC, s_r)
                P.op("vector", lambda e: e.tensor_tensor(
                    T["ln2t"][:], T["xs"][:], bc_mid(mu_t[:], KD), ALU.subtract))
                v2 = P.op("vector", lambda e: e.tensor_tensor(
                    T["ln2t"][:], T["ln2t"][:], bc_mid(rstd_t[:], KD), ALU.mult), VC)
                P.wait("sync", VC, v2)
                P.dma("sync", lambda e, jt=jt: e.dma_start(out=d_ln2_r[:, :, jt], in_=T["ln2t"][:]))
                s_r = ln_stats(T, T["ln2t"], wait_vc=v2)
                P.wait("vector", SC, s_r)
                P.op("vector", lambda e: e.tensor_tensor(
                    T["cen"][:], T["ln2t"][:], bc_mid(mu_t[:], KD), ALU.subtract))
                P.op("vector", lambda e: e.tensor_tensor(
                    T["cen"][:], T["cen"][:], bc_mid(rstd_t[:], KD), ALU.mult))
                v3 = None
                for kb in range(KD):
                    v3 = P.op("vector", lambda e, kb=kb: e.tensor_scalar(
                        T["xnb"][:, kb, :], T["cen"][:, kb, :],
                        lngbt[:, 2 * kb:2 * kb + 1], lngbt[:, 2 * kb + 1:2 * kb + 2],
                        ALU.mult, ALU.add), VC)
                P.wait("sync", VC, v3)
                xnd = P.dma("sync", lambda e, j=j: e.dma_start(
                    out=d_xn.ap()[D * j:D * (j + 1), :].rearrange("(k p) t -> p k t", p=128),
                    in_=T["xnb"][:]))
                xnd_j = xnd
                if debug and "p1" in dbg_parts:
                    P.dma("sync", lambda e, jt=jt: e.dma_start(
                        out=dbg["dbg_xn"].rearrange("(k p) t -> p k t", p=128)[:, :, jt], in_=T["xnb"][:]))
                    P.dma("sync", lambda e, jt=jt: e.dma_start(
                        out=dbg["dbg_x1"].rearrange("(k p) t -> p k t", p=128)[:, :, jt], in_=T["xs"][:]))
                P.barrier()
            for j in range(NJ):
                ag_done[j] = P.coll(
                    "AllGather", ALU.bypass,
                    [d_xn.ap()[D * j:D * (j + 1), :].opt()],
                    [d_xng.ap()[4 * D * j:4 * D * (j + 1), :].opt()],
                    wait_ld=xnd_j if j == 0 else None)

        with ExitStack() as s2a:
            E2 = s2a.enter_context
            Winb = E2(nc.sbuf_tensor("Winb", [128, KD, 1024], BF16))
            xng0 = E2(nc.sbuf_tensor("xng0", [128, KD, TC], BF16))
            xng1 = E2(nc.sbuf_tensor("xng1", [128, KD, TC], BF16))
            xhh = E2(nc.sbuf_tensor("xhh", [128, NMI, TC + 8], F32))
            zst = E2(nc.sbuf_tensor("zst", [128, NMI, TC], F32))
            xct2 = E2(nc.sbuf_tensor("xct2", [128, NMI, TC], F32))
            xcb2 = E2(nc.sbuf_tensor("xcb2", [128, NMI, TC], BF16))
            cvac = E2(nc.sbuf_tensor("cvac", [128, NMI, TC], F32))
            Wxpb = E2(nc.sbuf_tensor("Wxpb", [128, NMI, 33], BF16))
            xpjt = E2(nc.sbuf_tensor("xpjt", [33, TC], F32))
            xng = [xng0, xng1]
            lw = P.dma("sync", lambda e: e.dma_start(
                out=Winb[:], in_=d_winb.ap().rearrange("(k p) m -> p k m", p=128)))
            for mi in range(NMI):
                tw = P.dma("sync", lambda e, mi=mi: e.dma_start(
                    out=var_t[:, 0:33], in_=wxpc[128 * mi:128 * (mi + 1), :]))
                P.wait("scalar", LD, tw)
                sw = P.op("scalar", lambda e, mi=mi: e.activation(Wxpb[:, mi, :], var_t[:, 0:33], AF.Copy), SC)
                P.wait("sync", SC, sw)
            P.wait("tensor", LD, lw)
            P.op("gpsimd", lambda e: e.memset(xhh[:, :, 0:3], 0.0), PC)
            P.wait("scalar", PC, PC.n)
            conv_done = None
            sil_done = None
            cast_done = None
            tpj_prev = None
            spj_prev = None
            zs_spill = None
            xc_spill = None
            pj_spill = None
            bank_ev = {}
            mm_last = [None, None]
            for k in range(NCH):
                bk = k % 2
                r0 = 4 * D * (k // 4) + D * (k % 4)
                c0 = TC * k
                if mm_last[bk] is not None:
                    P.wait("sync", TM, mm_last[bk])
                P.wait("sync", CD, ag_done[k // 4])
                lx = P.dma("sync", lambda e, r0=r0, bk=bk: e.dma_start(
                    out=xng[bk][:],
                    in_=d_xng.ap()[r0:r0 + D, :].rearrange("(k p) t -> p k t", p=128)))
                if k > 0:
                    P.wait("scalar", VC, conv_done)
                    P.op("scalar", lambda e: e.activation(xhh[:, :, 0:3], xhh[:, :, TC:TC + 3], AF.Copy))
                P.wait("tensor", LD, lx)
                ev = []
                for mi in range(KD):
                    psn = mi % 4
                    if bank_ev.get(psn) is not None:
                        P.wait("tensor", SC, bank_ev[psn])
                    tmm = None
                    for kb in range(KD):
                        tmm = P.op("tensor", lambda e, kb=kb, mi=mi, bk=bk, psn=psn: e.matmul(
                            PS[psn][:], Winb[:, kb, 128 * mi:128 * (mi + 1)], xng[bk][:, kb, :],
                            start=(kb == 0), stop=(kb == KD - 1)), TM if kb == KD - 1 else None)
                    if mi == KD - 1:
                        mm_last[bk] = tmm
                    P.wait("scalar", TM, tmm)
                    if mi < NMI:
                        evv = P.op("scalar", lambda e, mi=mi, psn=psn: e.activation(
                            xhh[:, mi, 3:3 + TC], PS[psn][:], AF.Copy), SC)
                    else:
                        if zs_spill is not None and mi == NMI:
                            P.wait("scalar", LD, zs_spill)
                        evv = P.op("scalar", lambda e, mi=mi, psn=psn: e.activation(
                            zst[:, mi - NMI, :], PS[psn][:], AF.Silu), SC)
                    ev.append(evv)
                    bank_ev[psn] = evv
                P.wait("sync", SC, ev[-1])
                zs_spill = P.dma("sync", lambda e, c0=c0: e.dma_start(
                    out=d_z_r[:, :, c0:c0 + TC], in_=zst[:]))
                P.wait("vector", SC, ev[NMI - 1])
                if sil_done is not None:
                    P.wait("vector", SC, sil_done)
                cv = None
                for mi in range(NMI):
                    P.op("vector", lambda e, mi=mi: e.tensor_scalar(
                        cvac[:, mi, :], xhh[:, mi, 0:TC],
                        cwt[:, 4 * mi + 0:4 * mi + 1], vect[:, 4 * mi + 0:4 * mi + 1],
                        ALU.mult, ALU.add))
                    for kk in range(1, 4):
                        cv = P.op("vector", lambda e, mi=mi, kk=kk: e.scalar_tensor_tensor(
                            cvac[:, mi, :], xhh[:, mi, kk:kk + TC],
                            cwt[:, 4 * mi + kk:4 * mi + kk + 1], cvac[:, mi, :],
                            ALU.mult, ALU.add), VC if (mi == NMI - 1 and kk == 3) else None)
                conv_done = cv
                P.wait("scalar", VC, cv)
                if xc_spill is not None:
                    P.wait("scalar", LD, xc_spill)
                sil_done = P.op("scalar", lambda e: e.activation(
                    xct2[:].rearrange("p m t -> p (m t)"), cvac[:].rearrange("p m t -> p (m t)"),
                    AF.Silu), SC)
                if tpj_prev is not None:
                    P.wait("scalar", TM, tpj_prev)
                cast_done = P.op("scalar", lambda e: e.activation(
                    xcb2[:].rearrange("p m t -> p (m t)"), xct2[:].rearrange("p m t -> p (m t)"),
                    AF.Copy), SC)
                P.wait("sync", SC, sil_done)
                xc_spill = P.dma("sync", lambda e, c0=c0: e.dma_start(
                    out=d_xc_r[:, :, c0:c0 + TC], in_=xct2[:]))
                P.wait("tensor", SC, cast_done)
                if spj_prev is not None:
                    P.wait("tensor", SC, spj_prev)
                tpj = None
                for mi in range(NMI):
                    tpj = P.op("tensor", lambda e, mi=mi: e.matmul(
                        PS[4][0:33, :], Wxpb[:, mi, :], xcb2[:, mi, :],
                        start=(mi == 0), stop=(mi == NMI - 1)), TM if mi == NMI - 1 else None)
                tpj_prev = tpj
                P.wait("scalar", TM, tpj)
                if pj_spill is not None:
                    P.wait("scalar", LD, pj_spill)
                spj = P.op("scalar", lambda e: e.activation(xpjt[:], PS[4][0:33, :], AF.Copy), SC)
                spj_prev = spj
                P.wait("sync", SC, spj)
                pj_spill = P.dma("sync", lambda e, c0=c0: e.dma_start(
                    out=d_xpj.ap()[:, c0:c0 + TC], in_=xpjt[:]))
            P.barrier()

        art = P.coll("AllReduce", ALU.add, [d_xpj.ap().opt()], [d_xpjr.ap().opt()],
                     wait_ld=P.LD.n)
        P.barrier()

        rs_done = [None] * NJ
        with ExitStack() as s2b:
            E3 = s2b.enter_context
            Wob = E3(nc.sbuf_tensor("Wob", [128, NMI, 1024], BF16))
            xct3 = [E3(nc.sbuf_tensor(f"xct3{i}", [128, NMI, TC], F32)) for i in range(2)]
            zsl = [E3(nc.sbuf_tensor(f"zsl{i}", [128, NMI, TC], F32)) for i in range(2)]
            dtt = E3(nc.sbuf_tensor("dtt", [128, NMI, TC], F32))
            ut = E3(nc.sbuf_tensor("ut", [128, NMI, TC], F32))
            Bxt = E3(nc.sbuf_tensor("Bxt", [16, TC], F32))
            Cxt = E3(nc.sbuf_tensor("Cxt", [16, TC], F32))
            dtr = E3(nc.sbuf_tensor("dtr", [1, TC], F32))
            Bxb = E3(nc.sbuf_tensor("Bxb", [16, TC], BF16))
            Cxb = E3(nc.sbuf_tensor("Cxb", [16, TC], BF16))
            Cb = [E3(nc.sbuf_tensor(f"Cb{i}", [128, TC], F32)) for i in range(2)]
            ab = [E3(nc.sbuf_tensor(f"ab{i}", [128, NMI * TC], F32)) for i in range(2)]
            bb3 = E3(nc.sbuf_tensor("bb3", [128, NMI * TC], F32))
            hta = E3(nc.sbuf_tensor("hta", [128, NMI, TC], F32))
            tmpb = [E3(nc.sbuf_tensor(f"tmpb{i}", [128, NMI * TC], F32)) for i in range(2)]
            yt = [E3(nc.sbuf_tensor(f"yt{i}", [128, NMI, TC], F32)) for i in range(2)]
            gt = [E3(nc.sbuf_tensor(f"gt{i}", [128, NMI, TC], BF16)) for i in range(2)]
            eb2b = E3(nc.sbuf_tensor("eb2b", [128, NMI * TC], F32))
            rb2b = E3(nc.sbuf_tensor("rb2b", [128, NMI * TC], F32))
            wot = [E3(nc.sbuf_tensor(f"wot{i}", [128, TC], F32)) for i in range(2)]

            lw = P.dma("sync", lambda e: e.dma_start(
                out=Wob[:], in_=d_wob.ap().rearrange("(k p) m -> p k m", p=128)))
            P.op("vector", lambda e: e.memset(st[:], 0.0), VC)
            P.wait("tensor", LD, lw)

            roll = {"a3": {}, "dve": {}, "psB_mm": {}, "psC_ev": {}, "tmp": {}, "tmp_rd": {}}
            vg_k = [None, None]
            wo_last_k = [None, None]
            wo_bank_ev = {0: None, 1: None}
            dt_vsp = None
            u_done = None
            u_done_k = [None, None]
            b3_last = None
            bxb_done = None
            dtb_tm = [None, None]
            prev_tdt = None
            gl_n = 0
            for k in range(NCH):
                c0 = TC * k
                kt = slice(c0, c0 + TC)
                pk = k % 2
                if vg_k[pk] is not None:
                    P.wait("sync", VC, vg_k[pk])
                    P.wait("sync", PC, u_done_k[pk])
                l1 = P.dma("sync", lambda e, kt=kt, pk=pk: e.dma_start(out=xct3[pk][:], in_=d_xc_r[:, :, kt]))
                l2 = P.dma("sync", lambda e, kt=kt, pk=pk: e.dma_start(out=zsl[pk][:], in_=d_z_r[:, :, kt]))
                if bxb_done is not None:
                    P.wait("sync", SC, bxb_done)
                if prev_tdt is not None:
                    P.wait("sync", TM, prev_tdt)
                lB = P.dma("sync", lambda e, kt=kt: e.dma_start(out=Bxt[:], in_=d_xpjr.ap()[0:16, kt]))
                lC = P.dma("sync", lambda e, kt=kt: e.dma_start(out=Cxt[:], in_=d_xpjr.ap()[16:32, kt]))
                lD = P.dma("sync", lambda e, kt=kt: e.dma_start(out=dtr[:], in_=d_xpjr.ap()[32:33, kt]))
                P.wait("tensor", LD, lD)
                if dtb_tm[pk] is not None:
                    P.wait("tensor", VC, dt_vsp)
                tdt = P.op("tensor", lambda e, pk=pk: e.matmul(
                    PS[6 + pk][:], ones1t[:], dtr[:], start=True, stop=True), TM)
                dtb_tm[pk] = tdt
                prev_tdt = tdt
                P.wait("vector", TM, tdt)
                for mi in range(NMI):
                    P.op("vector", lambda e, mi=mi, pk=pk: e.tensor_scalar(
                        dtt[:, mi, :], PS[6 + pk][:],
                        vect[:, 4 * mi + 2:4 * mi + 3], vect[:, 4 * mi + 1:4 * mi + 2],
                        ALU.mult, ALU.add))
                vrl = P.op("vector", lambda e: e.tensor_scalar(
                    rb2b[:], dtt[:].rearrange("p m t -> p (m t)"), 0.0, None, ALU.max), VC)
                P.wait("scalar", VC, vrl)
                P.op("scalar", lambda e: e.activation(eb2b[:], dtt[:].rearrange("p m t -> p (m t)"), AF.Abs))
                P.op("scalar", lambda e: e.activation(eb2b[:], eb2b[:], AF.Exp, scale=-1.0))
                ssp = P.op("scalar", lambda e: e.activation(eb2b[:], eb2b[:], AF.Ln, bias=one_t[:, 0:1]), SC)
                P.wait("vector", SC, ssp)
                dt_vsp = P.op("vector", lambda e: e.tensor_tensor(
                    dtt[:].rearrange("p m t -> p (m t)"), rb2b[:], eb2b[:], ALU.add), VC)
                P.wait("gpsimd", VC, dt_vsp)
                P.wait("gpsimd", LD, l1)
                if b3_last is not None:
                    P.wait("gpsimd", VC, b3_last)
                u_done = P.op("gpsimd", lambda e, pk=pk: e.tensor_tensor(
                    ut[:].rearrange("p m t -> p (m t)"), dtt[:].rearrange("p m t -> p (m t)"),
                    xct3[pk][:].rearrange("p m t -> p (m t)"), ALU.mult), PC)
                u_done_k[pk] = u_done
                P.wait("scalar", LD, lC)
                if gl_n > 0:
                    P.wait("scalar", TM, roll["psB_mm"][gl_n - 1])
                P.op("scalar", lambda e: e.activation(Bxb[:], Bxt[:], AF.Copy))
                bxb_done = P.op("scalar", lambda e: e.activation(Cxb[:], Cxt[:], AF.Copy), SC)
                if debug and "dty" in dbg_parts and k == 0:
                    P.wait("sync", VC, dt_vsp)
                    P.dma("sync", lambda e: e.dma_start(out=dbg["dbg_dt"], in_=dtt[:].rearrange("p m t -> p (m t)")))
                first_v = True
                for n in range(NS):
                    g = gl_n
                    gl_n += 1
                    pp = g % 2
                    psB = PS[0 + 2 * pp]
                    psC = PS[1 + 2 * pp]
                    P.wait("tensor", SC, bxb_done)
                    if roll["dve"].get(g - 2) is not None:
                        P.wait("tensor", VC, roll["dve"][g - 2])
                    if roll["psC_ev"].get(g - 2) is not None:
                        P.wait("tensor", SC, roll["psC_ev"][g - 2])
                    P.op("tensor", lambda e, n=n, psB=psB: e.matmul(
                        psB[:], selb[:, 128 * n:128 * (n + 1)], Bxb[:], start=True, stop=True))
                    tbc = P.op("tensor", lambda e, n=n, psC=psC: e.matmul(
                        psC[:], selb[:, 128 * n:128 * (n + 1)], Cxb[:], start=True, stop=True), TM)
                    roll["psB_mm"][g] = tbc
                    P.wait("scalar", TM, tbc)
                    if roll["tmp"].get(g - 2) is not None:
                        P.wait("scalar", PC, roll["tmp"][g - 2])
                    cev = P.op("scalar", lambda e, pp=pp, psC=psC: e.activation(Cb[pp][:], psC[:], AF.Copy), SC)
                    roll["psC_ev"][g] = cev
                    if roll["dve"].get(g - 2) is not None:
                        P.wait("scalar", VC, roll["dve"][g - 2])
                    if n == 0:
                        P.wait("scalar", VC, dt_vsp)
                    sa = None
                    for mi in range(NMI):
                        sa = P.op("scalar", lambda e, mi=mi, n=n, pp=pp: e.activation(
                            ab[pp][:, TC * mi:TC * (mi + 1)], dtt[:, mi, :], AF.Exp,
                            scale=Aneg[:, 16 * mi + n:16 * mi + n + 1]),
                            SC if mi == NMI - 1 else None)
                    roll["a3"][g] = sa
                    if n == 0:
                        P.wait("vector", PC, u_done)
                    P.wait("vector", TM, roll["psB_mm"][g])
                    if roll["tmp"].get(g - 1) is not None:
                        P.wait("vector", PC, roll["tmp"][g - 1])
                    b3m = P.op("vector", lambda e, pp=pp, psB=psB: e.tensor_tensor(
                        bb3[:].rearrange("p (m t) -> p m t", m=NMI),
                        ut[:], bc_mid(psB[:], NMI), ALU.mult), VC)
                    if n == NS - 1:
                        b3_last = b3m
                    P.wait("vector", SC, roll["a3"][g])
                    for mi in range(NMI):
                        P.op("vector", lambda e, mi=mi, n=n, pp=pp: e.tensor_tensor_scan(
                            hta[:, mi, :],
                            ab[pp][:, TC * mi:TC * (mi + 1)],
                            bb3[:, TC * mi:TC * (mi + 1)],
                            st[:, mi, n:n + 1], ALU.mult, ALU.add))
                    vsc = P.op("vector", lambda e, n=n: e.tensor_copy(
                        st[:, :, n:n + 1], hta[:, :, TC - 1:TC]), VC)
                    roll["dve"][g] = vsc
                    P.wait("gpsimd", VC, vsc)
                    P.wait("gpsimd", SC, cev)
                    if roll["tmp_rd"].get(g - 2) is not None:
                        P.wait("gpsimd", VC, roll["tmp_rd"][g - 2])
                    tm_ = P.op("gpsimd", lambda e, pp=pp: e.tensor_tensor(
                        tmpb[pp][:].rearrange("p (m t) -> p m t", m=NMI),
                        hta[:], bc_mid(Cb[pp][:], NMI), ALU.mult), PC)
                    roll["tmp"][g] = tm_
                    P.wait("vector", PC, tm_)
                    if n == 0:
                        yadd = P.op("vector", lambda e, pp=pp, pk=pk: e.tensor_scalar(
                            yt[pk][:].rearrange("p m t -> p (m t)"),
                            tmpb[pp][:], 0.0, None, ALU.add), VC)
                    else:
                        yadd = P.op("vector", lambda e, pp=pp, pk=pk: e.tensor_tensor(
                            yt[pk][:].rearrange("p m t -> p (m t)"),
                            yt[pk][:].rearrange("p m t -> p (m t)"),
                            tmpb[pp][:], ALU.add), VC)
                    roll["tmp_rd"][g] = yadd
                    if debug and "scan" in dbg_parts and k == 0 and n == 0:
                        P.wait("sync", PC, u_done)
                        du = P.dma("sync", lambda e: e.dma_start(out=dbg["dbg_u"], in_=ut[:].rearrange("p m t -> p (m t)")))
                        P.wait("sync", SC, roll["a3"][g])
                        da = P.dma("sync", lambda e, pp=pp: e.dma_start(out=dbg["dbg_a0"], in_=ab[pp][:]))
                        P.wait("sync", VC, roll["dve"][g])
                        dbm = P.dma("sync", lambda e: e.dma_start(out=dbg["dbg_b0"], in_=bb3[:]))
                        dh = P.dma("sync", lambda e: e.dma_start(out=dbg["dbg_h0"], in_=hta[:].rearrange("p m t -> p (m t)")))
                        P.wait("sync", VC, yadd)
                        dy0 = P.dma("sync", lambda e, pk=pk: e.dma_start(out=dbg["dbg_y0"], in_=yt[pk][:].rearrange("p m t -> p (m t)")))
                        P.wait("sync", LD, dy0)
                        P.barrier()
                P.wait("vector", LD, l1)
                for mi in range(NMI):
                    P.op("vector", lambda e, mi=mi, pk=pk: e.scalar_tensor_tensor(
                        yt[pk][:, mi, :], xct3[pk][:, mi, :],
                        vect[:, 4 * mi + 3:4 * mi + 4], yt[pk][:, mi, :], ALU.mult, ALU.add))
                P.wait("vector", LD, l2)
                if wo_last_k[pk] is not None:
                    P.wait("vector", TM, wo_last_k[pk])
                vg = P.op("vector", lambda e, pk=pk: e.tensor_tensor(
                    gt[pk][:].rearrange("p m t -> p (m t)"), yt[pk][:].rearrange("p m t -> p (m t)"),
                    zsl[pk][:].rearrange("p m t -> p (m t)"), ALU.mult), VC)
                vg_k[pk] = vg
                if debug and "dty" in dbg_parts and k == 0:
                    P.wait("sync", VC, vg)
                    P.dma("sync", lambda e: e.dma_start(out=dbg["dbg_y"], in_=yt[0][:].rearrange("p m t -> p (m t)")))
                slot = k
                P.wait("tensor", VC, vg)
                wo_dma = {}
                for mo in range(KD):
                    pw = mo % 2
                    psW = PS[4 + pw]
                    if wo_bank_ev[pw] is not None:
                        P.wait("tensor", SC, wo_bank_ev[pw])
                    two = None
                    for kd in range(NMI):
                        two = P.op("tensor", lambda e, kd=kd, mo=mo, pk=pk, psW=psW: e.matmul(
                            psW[:], Wob[:, kd, 128 * mo:128 * (mo + 1)], gt[pk][:, kd, :],
                            start=(kd == 0), stop=(kd == NMI - 1)), TM if kd == NMI - 1 else None)
                    if mo == KD - 1:
                        wo_last_k[pk] = two
                    P.wait("scalar", TM, two)
                    if wo_dma.get(mo - 2) is not None:
                        P.wait("scalar", LD, wo_dma[mo - 2])
                    swo = P.op("scalar", lambda e, pw=pw, psW=psW: e.activation(wot[pw][:], psW[:], AF.Copy), SC)
                    wo_bank_ev[pw] = swo
                    P.wait("sync", SC, swo)
                    wo_dma[mo] = P.dma("sync", lambda e, k=k, mo=mo, pw=pw: e.dma_start(
                        out=d_wop_j[k // 4].ap()[1024 * (k % 4) + 128 * mo:1024 * (k % 4) + 128 * (mo + 1), :],
                        in_=wot[pw][:]))
                if debug and "wop" in dbg_parts and k == 3:
                    P.wait("sync", LD, P.LD.n)
                    P.dma("sync", lambda e: e.dma_start(out=dbg["dbg_wop"], in_=d_wop_j[0].ap()))
            P.barrier()
        if debug and "wop" in dbg_parts:
            P.dma("sync", lambda e: e.dma_start(out=dbg["dbg_wop2"], in_=d_wop_j[0].ap()))
            P.barrier()
        for j in range(NJ):
            rs_done[j] = P.coll(
                "ReduceScatter", ALU.add,
                [d_wop_j[j].ap().opt()], [d_att_j[j].ap().opt()])
        P.barrier()
        if debug and "rs" in dbg_parts:
            P.wait("sync", CD, rs_done[0])
            P.dma("sync", lambda e: e.dma_start(out=dbg["dbg_wop"][0:D, :], in_=d_att_j[0].ap()))
            P.barrier()

        with ExitStack() as sB:
            T = alloc_ffn_scope(sB, "b")
            for j in range(NJ):
                jt = slice(TC * j, TC * (j + 1))
                l1 = P.dma("sync", lambda e, jt=jt: e.dma_start(out=T["xs"][:], in_=d_x1_r[:, :, jt]))
                l2 = P.dma("sync", lambda e, jt=jt: e.dma_start(out=T["ln2t"][:], in_=d_ln2_r[:, :, jt]))
                P.wait("sync", CD, rs_done[j])
                l3 = P.dma("sync", lambda e, j=j: e.dma_start(
                    out=T["cen"][:],
                    in_=d_att_j[j].ap().rearrange("(k p) t -> p k t", p=128)))
                P.wait("vector", LD, l3)
                P.op("vector", lambda e: e.tensor_tensor(
                    T["xs"][:].rearrange("p k t -> p (k t)"), T["xs"][:].rearrange("p k t -> p (k t)"),
                    T["ln2t"][:].rearrange("p k t -> p (k t)"), ALU.add))
                vx2 = P.op("vector", lambda e: e.tensor_tensor(
                    T["xs"][:].rearrange("p k t -> p (k t)"), T["xs"][:].rearrange("p k t -> p (k t)"),
                    T["cen"][:].rearrange("p k t -> p (k t)"), ALU.add), VC)
                if debug and "p3" in dbg_parts:
                    P.wait("sync", LD, l3)
                    P.dma("sync", lambda e, j=j: e.dma_start(
                        out=dbg["dbg_att"][D * j:D * (j + 1), :].rearrange("(k p) t -> p k t", p=128),
                        in_=T["cen"][:]))
                s_r = ln_stats(T, T["xs"], wait_vc=vx2)
                P.wait("vector", SC, s_r)
                vx = ln_apply_plain(T, T["xs"], T["xnb"])
                P.wait("tensor", VC, vx)
                xv = ffn(T)
                P.wait("sync", VC, xv)
                P.dma("sync", lambda e, jt=jt: e.dma_start(out=o_y_r[:, :, jt], in_=T["xs"][:]))
                P.barrier()

        P.wait("sync", LD, P.LD.n)

        block = E(nc.Block())
        P.emit(block)

    return nc


_DEBUG = False
_NC_CACHE = {}


_DBG_PARTS = ("p3",)


def _get_nc():
    key = (bool(_DEBUG), tuple(_DBG_PARTS))
    if key not in _NC_CACHE:
        _NC_CACHE[key] = build_program(debug=key[0], dbg_parts=key[1])
    return _NC_CACHE[key]


def kernel(L0_attn_x, L0_attn_ln_g, L0_attn_ln_b, L0_attn_W_in, L0_attn_conv_w,
           L0_attn_conv_b, L0_attn_W_xp, L0_attn_W_dt, L0_attn_b_dt, L0_attn_A_log,
           L0_attn_D, L0_attn_W_out, L0_ffn_w_gate, L0_ffn_w_up, L0_ffn_w_down):
    f32 = np.float32
    x = np.asarray(L0_attn_x, f32)
    W_in = np.asarray(L0_attn_W_in, f32)
    W_out = np.asarray(L0_attn_W_out, f32)
    W_xp = np.asarray(L0_attn_W_xp, f32)
    conv_w = np.asarray(L0_attn_conv_w, f32)
    conv_b = np.asarray(L0_attn_conv_b, f32)
    W_dt = np.asarray(L0_attn_W_dt, f32)
    b_dt = np.asarray(L0_attn_b_dt, f32)
    A_log = np.asarray(L0_attn_A_log, f32)
    Dvec = np.asarray(L0_attn_D, f32)
    ln_g = np.asarray(L0_attn_ln_g, f32)
    ln_b = np.asarray(L0_attn_ln_b, f32)
    wg = np.ascontiguousarray(np.asarray(L0_ffn_w_gate, np.int32).T)
    wu = np.ascontiguousarray(np.asarray(L0_ffn_w_up, np.int32).T)
    wd = np.ascontiguousarray(np.asarray(L0_ffn_w_down, np.int32).T)

    sel = np.zeros((16, 2048), f32)
    for n in range(16):
        sel[n, 128 * n:128 * (n + 1)] = 1.0
    ones128 = np.ones((128, 128), f32)
    ones1 = np.ones((1, 128), f32)
    lngb = np.zeros((128, 16), f32)
    for kb in range(KD):
        lngb[:, 2 * kb] = ln_g[128 * kb:128 * (kb + 1)]
        lngb[:, 2 * kb + 1] = ln_b[128 * kb:128 * (kb + 1)]

    in_maps = []
    for c in range(8):
        b, r = divmod(c, 4)
        sl = slice(512 * r, 512 * (r + 1))
        xc = np.concatenate(
            [x[b, TC * (4 * j + r):TC * (4 * j + r + 1), :].T for j in range(NJ)], axis=1)
        winc_c = np.concatenate([W_in[:, sl], W_in[:, 2048 + sl.start:2048 + sl.stop]], axis=1)
        cwc = conv_w[sl].reshape(4, 128, 4).transpose(1, 0, 2).reshape(128, 16)
        avals = A_log[sl].reshape(4, 128, 16).transpose(1, 0, 2).reshape(128, 64)
        vv = np.stack([conv_b[sl], b_dt[sl], W_dt[0, sl], Dvec[sl]], axis=-1)
        vv = vv.reshape(4, 128, 4).transpose(1, 0, 2).reshape(128, 16)
        in_maps.append(dict(
            x_fm=np.ascontiguousarray(xc),
            winc=np.ascontiguousarray(winc_c),
            woc=np.ascontiguousarray(W_out[sl]),
            wxpc=np.ascontiguousarray(W_xp[sl]),
            wgT=wg, wuT=wu, wdT=wd,
            cwc=np.ascontiguousarray(cwc),
            avals=np.ascontiguousarray(avals),
            vecs=np.ascontiguousarray(vv),
            lngb=lngb, sel=sel, ones128=ones128, ones1=ones1,
        ))

    nc = _get_nc()
    res = run_bass_kernel_spmd(nc, in_maps, core_ids=list(range(8)))

    out = np.empty((B, L, D), f32)
    for c in range(8):
        b, r = divmod(c, 4)
        oy = np.asarray(res.results[c]["o_y"], f32)
        for j in range(NJ):
            out[b, TC * (4 * j + r):TC * (4 * j + r + 1), :] = oy[:, TC * j:TC * (j + 1)].T
    if _DEBUG:
        kernel._dbg = res.results
    return out


# version 15
# speedup vs baseline: 1.0815x; 1.0815x over previous
import dataclasses
from contextlib import ExitStack

import numpy as np

import concourse.bass as bass
import concourse.mybir as mybir
from concourse.bass_utils import run_bass_kernel_spmd

F32 = mybir.dt.float32
BF16 = mybir.dt.bfloat16
I32 = mybir.dt.int32
AF = mybir.ActivationFunctionType
ALU = mybir.AluOpType

B, L, D, DIN, H, NS = 2, 8192, 1024, 2048, 3072, 16
TC = 512
NCH = 16
NJ = 4
KD = 8
NMI = 4
HM = H // 128
EPS = 1e-5
GROUPS = [[0, 1, 2, 3], [4, 5, 6, 7]]
ENGINES = ("sync", "vector", "scalar", "tensor", "gpsimd")


def bc_mid(ap, rep):
    assert len(ap.ap) == 2
    return dataclasses.replace(ap, ap=[ap.ap[0], [0, rep], ap.ap[1]])


class Ctr:
    def __init__(self, h):
        self.h = h
        self.n = 0


class Bld:
    def __init__(self, nc, ctx):
        self.nc = nc
        self.ops = {e: [] for e in ENGINES}
        E = ctx.enter_context
        self.LD = Ctr(E(nc.semaphore("LD")))
        self.CC = Ctr(E(nc.semaphore("CC")))
        self.VC = Ctr(E(nc.semaphore("VC")))
        self.SC = Ctr(E(nc.semaphore("SC")))
        self.TM = Ctr(E(nc.semaphore("TM")))
        self.PC = Ctr(E(nc.semaphore("PC")))
        self.CD = Ctr(E(nc.semaphore("CD")))
        self.WQ = Ctr(E(nc.semaphore("WQ")))
        self.BAR = Ctr(E(nc.semaphore("BAR")))

    def op(self, eng, build, ctr=None):
        if ctr is None:
            self.ops[eng].append(build)
            return None
        ctr.n += 1
        tgt = ctr.n
        self.ops[eng].append(lambda e, b=build, c=ctr: b(e).then_inc(c.h, 1))
        return tgt

    def dma(self, eng, build, ctr=None):
        ctr = ctr or self.LD
        ctr.n += 16
        tgt = ctr.n
        self.ops[eng].append(lambda e, b=build, c=ctr: b(e).then_inc(c.h, 16))
        return tgt

    def wait(self, eng, ctr, val):
        if val is None:
            return
        self.ops[eng].append(lambda e, c=ctr, v=val: e.wait_ge(c.h, v))

    def barrier(self):
        self.wait("sync", self.LD, self.LD.n)
        self.wait("sync", self.WQ, self.WQ.n)
        tgt = self.BAR.n + len(ENGINES)
        for e in ENGINES:
            self.ops[e].append(lambda E: E.sem_inc(self.BAR.h, 1))
        self.BAR.n = tgt
        for e in ENGINES:
            self.ops[e].append(lambda E, v=tgt: E.wait_ge(self.BAR.h, v))

    def coll(self, kind, op_, ins, outs, wait_ld=None):
        if wait_ld is not None:
            self.wait("gpsimd", self.LD, wait_ld)
        self.CC.n += 1
        tgt = self.CC.n
        self.ops["gpsimd"].append(lambda e, k=kind, o=op_, i=ins, u=outs: e.collective_compute(
            k, o, GROUPS, i, u).then_inc(self.CC.h, 1))
        self.wait("gpsimd", self.CC, tgt)
        self.ops["gpsimd"].append(lambda e: e.drain(fusable=False))
        self.CD.n += 1
        cdt = self.CD.n
        self.ops["gpsimd"].append(lambda e, v=cdt: e.sem_inc(self.CD.h, 1))
        return cdt

    def emit(self, block):
        b = self
        @block.sync
        def _(eng):
            for f in b.ops["sync"]:
                f(eng)
        @block.vector
        def _(eng):
            for f in b.ops["vector"]:
                f(eng)
        @block.scalar
        def _(eng):
            for f in b.ops["scalar"]:
                f(eng)
        @block.tensor
        def _(eng):
            for f in b.ops["tensor"]:
                f(eng)
        @block.gpsimd
        def _(eng):
            for f in b.ops["gpsimd"]:
                f(eng)


def build_program(debug=False, dbg_parts=("p1","scan","wop","rs","p3","dty")):
    nc = bass.Bass("TRN2", target_bir_lowering=False)

    def din(name, shape, dt=F32):
        return nc.dram_tensor(name, shape, dt, kind="ExternalInput").ap()

    x_fm = din("x_fm", [D, NJ * TC])
    winc = din("winc", [D, 1024])
    woc = din("woc", [512, D])
    wxpc = din("wxpc", [512, 33])
    wgT = din("wgT", [D, H], I32)
    wuT = din("wuT", [D, H], I32)
    wdT = din("wdT", [H, D], I32)
    cwc = din("cwc", [128, 16])
    avals = din("avals", [128, 64])
    vecs = din("vecs", [128, 16])
    lngb = din("lngb", [128, 16])
    sel_i = din("sel", [16, 2048])
    ones128 = din("ones128", [128, 128])
    ones1 = din("ones1", [1, 128])

    o_y = nc.dram_tensor("o_y", [D, NJ * TC], F32, kind="ExternalOutput").ap()

    def dint(name, shape, dt=F32):
        return nc.dram_tensor(name, shape, dt, kind="Internal")

    d_wg = dint("d_wg", [D, H], BF16)
    d_wu = dint("d_wu", [D, H], BF16)
    d_wd = dint("d_wd", [H, D], BF16)
    d_winb = dint("d_winb", [D, 1024], BF16)
    d_wob = dint("d_wob", [512, D], BF16)
    d_x1 = dint("d_x1", [D, NJ * TC], F32)
    d_ln2 = dint("d_ln2", [D, NJ * TC], F32)
    d_xn = dint("d_xn", [NJ * D, TC], BF16)
    d_xng = dint("d_xng", [NJ * 4 * D, TC], BF16)
    d_xc = dint("d_xc", [512, L], F32)
    d_z = dint("d_z", [512, L], F32)
    d_xpj = dint("d_xpj", [33, L], F32)
    d_xpjr = dint("d_xpjr", [33, L], F32)
    d_wop_j = [dint(f"d_wop{j}", [4 * D, TC], F32) for j in range(NJ)]
    d_att_j = [dint(f"d_att{j}", [D, TC], F32) for j in range(NJ)]

    dbg = {}
    if debug:
        for name, shape, dt in [
            ("dbg_xn", [D, NJ * TC], BF16), ("dbg_x1", [D, NJ * TC], F32),
            ("dbg_dt", [128, NMI * TC], F32),
            ("dbg_y", [128, NMI * TC], F32), ("dbg_att", [NJ * D, TC], F32),
            ("dbg_u", [128, NMI * TC], F32), ("dbg_a0", [128, NMI * TC], F32),
            ("dbg_b0", [128, NMI * TC], F32), ("dbg_h0", [128, NMI * TC], F32),
            ("dbg_y0", [128, NMI * TC], F32), ("dbg_wop", [4 * D, TC], F32),
            ("dbg_wop2", [4 * D, TC], F32),
        ]:
            dbg[name] = nc.dram_tensor(name, shape, dt, kind="ExternalOutput").ap()

    x_fm_r = x_fm.rearrange("(k p) t -> p k t", p=128)
    d_x1_r = d_x1.ap().rearrange("(k p) t -> p k t", p=128)
    d_ln2_r = d_ln2.ap().rearrange("(k p) t -> p k t", p=128)
    d_xc_r = d_xc.ap().rearrange("(m p) t -> p m t", p=128)
    d_z_r = d_z.ap().rearrange("(m p) t -> p m t", p=128)
    o_y_r = o_y.rearrange("(k p) t -> p k t", p=128)

    with ExitStack() as ctx:
        E = ctx.enter_context
        P = Bld(nc, ctx)
        LD, CC, VC, SC, TM, PC, CD, WQ = P.LD, P.CC, P.VC, P.SC, P.TM, P.PC, P.CD, P.WQ

        def sb(name, shape, dt=F32):
            return E(nc.sbuf_tensor(name, shape, dt))

        mu_t = sb("mu_t", [128, TC])
        ms_t = sb("ms_t", [128, TC])
        var_t = sb("var_t", [128, TC])
        rstd_t = sb("rstd_t", [128, TC])
        f1buf = sb("f1buf", [128, TC])
        cwt = sb("cwt", [128, 16])
        Aneg = sb("Aneg", [128, 64])
        vect = sb("vect", [128, 16])
        lngbt = sb("lngbt", [128, 16])
        onesb = sb("onesb", [128, 128], BF16)
        ones1t = sb("ones1t", [1, 128])
        eps_t = sb("eps_t", [128, 1])
        one_t = sb("one_t", [128, 1])
        selb = sb("selb", [16, 2048], BF16)
        st = sb("st", [128, NMI, NS])

        PS = [E(nc.psum_tensor(f"ps{i}", [128, TC], F32)) for i in range(8)]

        with nc.sbuf_tensor("deqi0", [128, H], I32) as deqi0, \
             nc.sbuf_tensor("deqi1", [128, H], I32) as deqi1, \
             nc.sbuf_tensor("deqb0", [128, H], BF16) as deqb0, \
             nc.sbuf_tensor("deqb1", [128, H], BF16) as deqb1, \
             nc.sbuf_tensor("wfl", [128, 1024], F32) as wfl, \
             nc.sbuf_tensor("p0sel", [16, 2048], F32) as p0sel:
            deqi = [deqi0, deqi1]
            deqb = [deqb0, deqb1]

            P.op("vector", lambda e: e.memset(eps_t[:], float(EPS)))
            vcn = P.op("vector", lambda e: e.memset(one_t[:], 1.0), VC)
            P.wait("scalar", VC, vcn)
            P.dma("sync", lambda e: e.dma_start(out=cwt[:], in_=cwc))
            P.dma("sync", lambda e: e.dma_start(out=vect[:], in_=vecs))
            P.dma("sync", lambda e: e.dma_start(out=lngbt[:], in_=lngb))
            P.dma("sync", lambda e: e.dma_start(out=ones1t[:], in_=ones1))
            t5 = P.dma("sync", lambda e: e.dma_start(out=mu_t[:, 0:64], in_=avals))
            P.wait("scalar", LD, t5)
            s1 = P.op("scalar", lambda e: e.activation(ms_t[:, 0:64], mu_t[:, 0:64], AF.Exp), SC)
            P.wait("vector", SC, s1)
            P.op("vector", lambda e: e.tensor_scalar(Aneg[:], ms_t[:, 0:64], -1.0, None, ALU.mult), VC)
            t6 = P.dma("sync", lambda e: e.dma_start(out=rstd_t[:, 0:128], in_=ones128))
            P.wait("scalar", LD, t6)
            P.op("scalar", lambda e: e.activation(onesb[:], rstd_t[:, 0:128], AF.Copy), SC)
            t7 = P.dma("sync", lambda e: e.dma_start(out=p0sel[:], in_=sel_i))
            P.wait("scalar", LD, t7)
            P.op("scalar", lambda e: e.activation(selb[:], p0sel[:], AF.Copy), SC)

            def dequant(src_ap, dst_t, nblocks, width):
                vvs = [None] * nblocks
                dmas = [None] * nblocks
                for i in range(nblocks):
                    bb = i % 2
                    if i >= 2:
                        P.wait("sync", VC, vvs[i - 2])
                    tl = P.dma("sync", lambda e, i=i, bb=bb, s=src_ap, w=width: e.dma_start(
                        out=deqi[bb][:, 0:w], in_=s[128 * i:128 * (i + 1), 0:w]))
                    P.wait("vector", LD, tl)
                    if i >= 2:
                        P.wait("vector", LD, dmas[i - 2])
                    vvs[i] = P.op("vector", lambda e, bb=bb, w=width: e.tensor_scalar(
                        deqb[bb][:, 0:w], deqi[bb][:, 0:w], 1.0 / 127.0, None, ALU.mult), VC)
                    P.wait("sync", VC, vvs[i])
                    dmas[i] = P.dma("sync", lambda e, i=i, bb=bb, d=dst_t, w=width: e.dma_start(
                        out=d.ap()[128 * i:128 * (i + 1), 0:w], in_=deqb[bb][:, 0:w]))

            dequant(wgT, d_wg, KD, H)
            dequant(wuT, d_wu, KD, H)
            dequant(wdT, d_wd, HM, D)

            def stage(src, dst, nblocks):
                prev = {"sv": None, "dma": None}
                for i in range(nblocks):
                    if prev["sv"] is not None:
                        P.wait("sync", SC, prev["sv"])
                    tl = P.dma("sync", lambda e, i=i, s=src: e.dma_start(
                        out=wfl[:], in_=s[128 * i:128 * (i + 1), :]))
                    P.wait("scalar", LD, tl)
                    if prev["dma"] is not None:
                        P.wait("scalar", LD, prev["dma"])
                    sv = P.op("scalar", lambda e: e.activation(deqb[0][:, 0:1024], wfl[:], AF.Copy), SC)
                    prev["sv"] = sv
                    P.wait("sync", SC, sv)
                    prev["dma"] = P.dma("sync", lambda e, i=i, d=dst: e.dma_start(
                        out=d.ap()[128 * i:128 * (i + 1), :], in_=deqb[0][:, 0:1024]))
            stage(winc, d_winb, KD)
            stage(woc, d_wob, NMI)
            P.barrier()

        def ln_stats(T, src, wait_vc=None):
            if wait_vc is not None:
                P.wait("scalar", VC, wait_vc)
            c1 = P.op("scalar", lambda e: e.activation(T["xbb"][:], src[:], AF.Copy), SC)
            c2 = P.op("scalar", lambda e: e.activation(T["xnb"][:], src[:], AF.Square), SC)
            P.wait("tensor", SC, c2)
            for kb in range(KD):
                P.op("tensor", lambda e, kb=kb: e.matmul(
                    PS[4][:], onesb[:], T["xbb"][:, kb, :], start=(kb == 0), stop=(kb == KD - 1)),
                    TM if kb == KD - 1 else None)
            tm2 = None
            for kb in range(KD):
                tm2 = P.op("tensor", lambda e, kb=kb: e.matmul(
                    PS[5][:], onesb[:], T["xnb"][:, kb, :], start=(kb == 0), stop=(kb == KD - 1)),
                    TM if kb == KD - 1 else None)
            P.wait("vector", TM, tm2)
            P.op("vector", lambda e: e.tensor_scalar(mu_t[:], PS[4][:], 1.0 / D, None, ALU.mult))
            P.op("vector", lambda e: e.tensor_scalar(ms_t[:], PS[5][:], 1.0 / D, None, ALU.mult))
            P.op("vector", lambda e: e.tensor_tensor(var_t[:], mu_t[:], mu_t[:], ALU.mult))
            v1 = P.op("vector", lambda e: e.tensor_tensor(var_t[:], ms_t[:], var_t[:], ALU.subtract), VC)
            P.wait("scalar", VC, v1)
            P.op("scalar", lambda e: e.activation(rstd_t[:], var_t[:], AF.Ln, bias=eps_t[:, 0:1]))
            s2 = P.op("scalar", lambda e: e.activation(rstd_t[:], rstd_t[:], AF.Exp, scale=-0.5), SC)
            return s2

        def ffn(T):
            xs, xnb, fbuf = T["xs"], T["xnb"], T["fbuf"]
            wgt_s, wut_s, wdt_s = T["wgt_s"], T["wut_s"], T["wdt_s"]
            tus = [None] * HM
            fvs = [None] * HM
            f1s = [None] * HM
            for mh in range(HM):
                bw = mh % 2
                pg = PS[0 + 2 * bw]
                pu = PS[1 + 2 * bw]
                if mh >= 2:
                    P.wait("sync", TM, tus[mh - 2])
                P.dma("sync", lambda e, mh=mh, bw=bw: e.dma_start(
                    out=wgt_s[bw][:],
                    in_=d_wg.ap()[:, 128 * mh:128 * (mh + 1)].rearrange("(k p) m -> p k m", p=128)), ctr=WQ)
                lw2 = P.dma("sync", lambda e, mh=mh, bw=bw: e.dma_start(
                    out=wut_s[bw][:],
                    in_=d_wu.ap()[:, 128 * mh:128 * (mh + 1)].rearrange("(k p) m -> p k m", p=128)), ctr=WQ)
                P.wait("tensor", WQ, lw2)
                if mh >= 2:
                    P.wait("tensor", SC, f1s[mh - 2])
                    P.wait("tensor", VC, fvs[mh - 2])
                tg = None
                for kb in range(KD):
                    tg = P.op("tensor", lambda e, kb=kb, bw=bw, pg=pg: e.matmul(
                        pg[:], wgt_s[bw][:, kb, :], xnb[:, kb, :],
                        start=(kb == 0), stop=(kb == KD - 1)), TM if kb == KD - 1 else None)
                for kb in range(KD):
                    tus[mh] = P.op("tensor", lambda e, kb=kb, bw=bw, pu=pu: e.matmul(
                        pu[:], wut_s[bw][:, kb, :], xnb[:, kb, :],
                        start=(kb == 0), stop=(kb == KD - 1)), TM if kb == KD - 1 else None)
                P.wait("scalar", TM, tg)
                if mh >= 1:
                    P.wait("scalar", VC, fvs[mh - 1])
                f1s[mh] = P.op("scalar", lambda e, pg=pg: e.activation(f1buf[:], pg[:], AF.Silu), SC)
                P.wait("vector", SC, f1s[mh])
                P.wait("vector", TM, tus[mh])
                fvs[mh] = P.op("vector", lambda e, mh=mh, pu=pu: e.tensor_tensor(
                    fbuf[:, mh, :], f1buf[:], pu[:], ALU.mult), VC)
            xv = None
            tds = [None] * KD
            for mo in range(KD):
                bw = mo % 2
                pd = PS[6 + bw]
                if mo >= 2:
                    P.wait("sync", TM, tds[mo - 2])
                lw = P.dma("sync", lambda e, mo=mo, bw=bw: e.dma_start(
                    out=wdt_s[bw][:],
                    in_=d_wd.ap()[:, 128 * mo:128 * (mo + 1)].rearrange("(k p) m -> p k m", p=128)), ctr=WQ)
                P.wait("tensor", WQ, lw)
                if mo == 0:
                    P.wait("tensor", VC, fvs[-1])
                if mo >= 2:
                    P.wait("tensor", VC, xv)
                for kh in range(HM):
                    tds[mo] = P.op("tensor", lambda e, kh=kh, bw=bw, pd=pd: e.matmul(
                        pd[:], wdt_s[bw][:, kh, :], fbuf[:, kh, :],
                        start=(kh == 0), stop=(kh == HM - 1)), TM if kh == HM - 1 else None)
                P.wait("vector", TM, tds[mo])
                xv = P.op("vector", lambda e, mo=mo, pd=pd: e.scalar_tensor_tensor(
                    xs[:, mo, :], pd[:], 0.5, xs[:, mo, :], ALU.mult, ALU.add), VC)
            return xv

        def ln_apply_plain(T, src, dst_bf):
            P.op("vector", lambda e: e.tensor_tensor(
                T["cen"][:], src[:], bc_mid(mu_t[:], KD), ALU.subtract))
            return P.op("vector", lambda e: e.tensor_tensor(
                dst_bf[:], T["cen"][:], bc_mid(rstd_t[:], KD), ALU.mult), VC)

        def alloc_ffn_scope(stack, tag):
            T = {}
            ES = stack.enter_context
            T["xs"] = ES(nc.sbuf_tensor(f"xs{tag}", [128, KD, TC], F32))
            T["ln2t"] = ES(nc.sbuf_tensor(f"ln2t{tag}", [128, KD, TC], F32))
            T["cen"] = ES(nc.sbuf_tensor(f"cen{tag}", [128, KD, TC], F32))
            T["xbb"] = ES(nc.sbuf_tensor(f"xbb{tag}", [128, KD, TC], BF16))
            T["xnb"] = ES(nc.sbuf_tensor(f"xnb{tag}", [128, KD, TC], BF16))
            T["fbuf"] = ES(nc.sbuf_tensor(f"fbuf{tag}", [128, HM, TC], BF16))
            T["wgt_s"] = [ES(nc.sbuf_tensor(f"wgt{tag}{i}", [128, KD, 128], BF16)) for i in range(2)]
            T["wut_s"] = [ES(nc.sbuf_tensor(f"wut{tag}{i}", [128, KD, 128], BF16)) for i in range(2)]
            T["wdt_s"] = [ES(nc.sbuf_tensor(f"wdt{tag}{i}", [128, HM, 128], BF16)) for i in range(2)]
            return T

        ag_done = [None] * NJ
        with ExitStack() as sA:
            T = alloc_ffn_scope(sA, "a")
            for j in range(NJ):
                jt = slice(TC * j, TC * (j + 1))
                lx = P.dma("sync", lambda e, jt=jt: e.dma_start(out=T["xs"][:], in_=x_fm_r[:, :, jt]))
                P.wait("scalar", LD, lx)
                P.wait("vector", LD, lx)
                s_r = ln_stats(T, T["xs"])
                P.wait("vector", SC, s_r)
                vx = ln_apply_plain(T, T["xs"], T["xnb"])
                P.wait("tensor", VC, vx)
                xv = ffn(T)
                P.wait("sync", VC, xv)
                P.dma("sync", lambda e, jt=jt: e.dma_start(out=d_x1_r[:, :, jt], in_=T["xs"][:]))
                s_r = ln_stats(T, T["xs"], wait_vc=xv)
                P.wait("vector", SC, s_r)
                P.op("vector", lambda e: e.tensor_tensor(
                    T["ln2t"][:], T["xs"][:], bc_mid(mu_t[:], KD), ALU.subtract))
                v2 = P.op("vector", lambda e: e.tensor_tensor(
                    T["ln2t"][:], T["ln2t"][:], bc_mid(rstd_t[:], KD), ALU.mult), VC)
                P.wait("sync", VC, v2)
                P.dma("sync", lambda e, jt=jt: e.dma_start(out=d_ln2_r[:, :, jt], in_=T["ln2t"][:]))
                s_r = ln_stats(T, T["ln2t"], wait_vc=v2)
                P.wait("vector", SC, s_r)
                P.op("vector", lambda e: e.tensor_tensor(
                    T["cen"][:], T["ln2t"][:], bc_mid(mu_t[:], KD), ALU.subtract))
                P.op("vector", lambda e: e.tensor_tensor(
                    T["cen"][:], T["cen"][:], bc_mid(rstd_t[:], KD), ALU.mult))
                v3 = None
                for kb in range(KD):
                    v3 = P.op("vector", lambda e, kb=kb: e.tensor_scalar(
                        T["xnb"][:, kb, :], T["cen"][:, kb, :],
                        lngbt[:, 2 * kb:2 * kb + 1], lngbt[:, 2 * kb + 1:2 * kb + 2],
                        ALU.mult, ALU.add), VC)
                P.wait("sync", VC, v3)
                xnd = P.dma("sync", lambda e, j=j: e.dma_start(
                    out=d_xn.ap()[D * j:D * (j + 1), :].rearrange("(k p) t -> p k t", p=128),
                    in_=T["xnb"][:]))
                xnd_j = xnd
                if debug and "p1" in dbg_parts:
                    P.dma("sync", lambda e, jt=jt: e.dma_start(
                        out=dbg["dbg_xn"].rearrange("(k p) t -> p k t", p=128)[:, :, jt], in_=T["xnb"][:]))
                    P.dma("sync", lambda e, jt=jt: e.dma_start(
                        out=dbg["dbg_x1"].rearrange("(k p) t -> p k t", p=128)[:, :, jt], in_=T["xs"][:]))
                P.barrier()
            for j in range(NJ):
                ag_done[j] = P.coll(
                    "AllGather", ALU.bypass,
                    [d_xn.ap()[D * j:D * (j + 1), :].opt()],
                    [d_xng.ap()[4 * D * j:4 * D * (j + 1), :].opt()],
                    wait_ld=xnd_j if j == 0 else None)

        with ExitStack() as s2a:
            E2 = s2a.enter_context
            Winb = E2(nc.sbuf_tensor("Winb", [128, KD, 1024], BF16))
            xng0 = E2(nc.sbuf_tensor("xng0", [128, KD, TC], BF16))
            xng1 = E2(nc.sbuf_tensor("xng1", [128, KD, TC], BF16))
            xhh = E2(nc.sbuf_tensor("xhh", [128, NMI, TC + 8], F32))
            zst = E2(nc.sbuf_tensor("zst", [128, NMI, TC], F32))
            xct2 = E2(nc.sbuf_tensor("xct2", [128, NMI, TC], F32))
            xcb2 = E2(nc.sbuf_tensor("xcb2", [128, NMI, TC], BF16))
            cvac = E2(nc.sbuf_tensor("cvac", [128, NMI, TC], F32))
            Wxpb = E2(nc.sbuf_tensor("Wxpb", [128, NMI, 33], BF16))
            xpjt = E2(nc.sbuf_tensor("xpjt", [33, TC], F32))
            xng = [xng0, xng1]
            lw = P.dma("sync", lambda e: e.dma_start(
                out=Winb[:], in_=d_winb.ap().rearrange("(k p) m -> p k m", p=128)))
            for mi in range(NMI):
                tw = P.dma("sync", lambda e, mi=mi: e.dma_start(
                    out=var_t[:, 0:33], in_=wxpc[128 * mi:128 * (mi + 1), :]))
                P.wait("scalar", LD, tw)
                sw = P.op("scalar", lambda e, mi=mi: e.activation(Wxpb[:, mi, :], var_t[:, 0:33], AF.Copy), SC)
                P.wait("sync", SC, sw)
            P.wait("tensor", LD, lw)
            P.op("gpsimd", lambda e: e.memset(xhh[:, :, 0:3], 0.0), PC)
            P.wait("scalar", PC, PC.n)
            conv_done = None
            sil_done = None
            cast_done = None
            tpj_prev = None
            spj_prev = None
            zs_spill = None
            xc_spill = None
            pj_spill = None
            bank_ev = {}
            mm_last = [None, None]
            for k in range(NCH):
                bk = k % 2
                r0 = 4 * D * (k // 4) + D * (k % 4)
                c0 = TC * k
                if mm_last[bk] is not None:
                    P.wait("sync", TM, mm_last[bk])
                P.wait("sync", CD, ag_done[k // 4])
                lx = P.dma("sync", lambda e, r0=r0, bk=bk: e.dma_start(
                    out=xng[bk][:],
                    in_=d_xng.ap()[r0:r0 + D, :].rearrange("(k p) t -> p k t", p=128)))
                if k > 0:
                    P.wait("scalar", VC, conv_done)
                    P.op("scalar", lambda e: e.activation(xhh[:, :, 0:3], xhh[:, :, TC:TC + 3], AF.Copy))
                P.wait("tensor", LD, lx)
                ev = []
                for mi in range(KD):
                    psn = mi % 4
                    if bank_ev.get(psn) is not None:
                        P.wait("tensor", SC, bank_ev[psn])
                    tmm = None
                    for kb in range(KD):
                        tmm = P.op("tensor", lambda e, kb=kb, mi=mi, bk=bk, psn=psn: e.matmul(
                            PS[psn][:], Winb[:, kb, 128 * mi:128 * (mi + 1)], xng[bk][:, kb, :],
                            start=(kb == 0), stop=(kb == KD - 1)), TM if kb == KD - 1 else None)
                    if mi == KD - 1:
                        mm_last[bk] = tmm
                    P.wait("scalar", TM, tmm)
                    if mi < NMI:
                        evv = P.op("scalar", lambda e, mi=mi, psn=psn: e.activation(
                            xhh[:, mi, 3:3 + TC], PS[psn][:], AF.Copy), SC)
                    else:
                        if zs_spill is not None and mi == NMI:
                            P.wait("scalar", LD, zs_spill)
                        evv = P.op("scalar", lambda e, mi=mi, psn=psn: e.activation(
                            zst[:, mi - NMI, :], PS[psn][:], AF.Silu), SC)
                    ev.append(evv)
                    bank_ev[psn] = evv
                P.wait("sync", SC, ev[-1])
                zs_spill = P.dma("sync", lambda e, c0=c0: e.dma_start(
                    out=d_z_r[:, :, c0:c0 + TC], in_=zst[:]))
                P.wait("vector", SC, ev[NMI - 1])
                if sil_done is not None:
                    P.wait("vector", SC, sil_done)
                cv = None
                for mi in range(NMI):
                    P.op("vector", lambda e, mi=mi: e.tensor_scalar(
                        cvac[:, mi, :], xhh[:, mi, 0:TC],
                        cwt[:, 4 * mi + 0:4 * mi + 1], vect[:, 4 * mi + 0:4 * mi + 1],
                        ALU.mult, ALU.add))
                    for kk in range(1, 4):
                        cv = P.op("vector", lambda e, mi=mi, kk=kk: e.scalar_tensor_tensor(
                            cvac[:, mi, :], xhh[:, mi, kk:kk + TC],
                            cwt[:, 4 * mi + kk:4 * mi + kk + 1], cvac[:, mi, :],
                            ALU.mult, ALU.add), VC if (mi == NMI - 1 and kk == 3) else None)
                conv_done = cv
                P.wait("scalar", VC, cv)
                if xc_spill is not None:
                    P.wait("scalar", LD, xc_spill)
                sil_done = P.op("scalar", lambda e: e.activation(
                    xct2[:].rearrange("p m t -> p (m t)"), cvac[:].rearrange("p m t -> p (m t)"),
                    AF.Silu), SC)
                if tpj_prev is not None:
                    P.wait("scalar", TM, tpj_prev)
                cast_done = P.op("scalar", lambda e: e.activation(
                    xcb2[:].rearrange("p m t -> p (m t)"), xct2[:].rearrange("p m t -> p (m t)"),
                    AF.Copy), SC)
                P.wait("sync", SC, sil_done)
                xc_spill = P.dma("sync", lambda e, c0=c0: e.dma_start(
                    out=d_xc_r[:, :, c0:c0 + TC], in_=xct2[:]))
                P.wait("tensor", SC, cast_done)
                if spj_prev is not None:
                    P.wait("tensor", SC, spj_prev)
                tpj = None
                for mi in range(NMI):
                    tpj = P.op("tensor", lambda e, mi=mi: e.matmul(
                        PS[4][0:33, :], Wxpb[:, mi, :], xcb2[:, mi, :],
                        start=(mi == 0), stop=(mi == NMI - 1)), TM if mi == NMI - 1 else None)
                tpj_prev = tpj
                P.wait("scalar", TM, tpj)
                if pj_spill is not None:
                    P.wait("scalar", LD, pj_spill)
                spj = P.op("scalar", lambda e: e.activation(xpjt[:], PS[4][0:33, :], AF.Copy), SC)
                spj_prev = spj
                P.wait("sync", SC, spj)
                pj_spill = P.dma("sync", lambda e, c0=c0: e.dma_start(
                    out=d_xpj.ap()[:, c0:c0 + TC], in_=xpjt[:]))
            P.barrier()

        art = P.coll("AllReduce", ALU.add, [d_xpj.ap().opt()], [d_xpjr.ap().opt()],
                     wait_ld=P.LD.n)
        P.barrier()

        rs_done = [None] * NJ
        with ExitStack() as s2b:
            E3 = s2b.enter_context
            Wob = E3(nc.sbuf_tensor("Wob", [128, NMI, 1024], BF16))
            xct3 = [E3(nc.sbuf_tensor(f"xct3{i}", [128, NMI, TC], F32)) for i in range(2)]
            zsl = [E3(nc.sbuf_tensor(f"zsl{i}", [128, NMI, TC], F32)) for i in range(2)]
            dtt = E3(nc.sbuf_tensor("dtt", [128, NMI, TC], F32))
            ut = E3(nc.sbuf_tensor("ut", [128, NMI, TC], BF16))
            Bxt = E3(nc.sbuf_tensor("Bxt", [16, TC], F32))
            Cxt = E3(nc.sbuf_tensor("Cxt", [16, TC], F32))
            dtr = E3(nc.sbuf_tensor("dtr", [1, TC], F32))
            Bxb = E3(nc.sbuf_tensor("Bxb", [16, TC], BF16))
            Cxb = E3(nc.sbuf_tensor("Cxb", [16, TC], BF16))
            Cb = [E3(nc.sbuf_tensor(f"Cb{i}", [128, TC], BF16)) for i in range(2)]
            Bbb = [E3(nc.sbuf_tensor(f"Bbb{i}", [128, TC], BF16)) for i in range(2)]
            ab = [E3(nc.sbuf_tensor(f"ab{i}", [128, NMI * TC], F32)) for i in range(2)]
            bb3 = E3(nc.sbuf_tensor("bb3", [128, NMI * TC], BF16))
            hta = E3(nc.sbuf_tensor("hta", [128, NMI, TC], F32))
            tmpb = [E3(nc.sbuf_tensor(f"tmpb{i}", [128, NMI * TC], BF16)) for i in range(2)]
            yt = [E3(nc.sbuf_tensor(f"yt{i}", [128, NMI, TC], BF16)) for i in range(2)]
            gt = [E3(nc.sbuf_tensor(f"gt{i}", [128, NMI, TC], BF16)) for i in range(2)]
            eb2b = E3(nc.sbuf_tensor("eb2b", [128, NMI * TC], F32))
            rb2b = E3(nc.sbuf_tensor("rb2b", [128, NMI * TC], F32))
            wot = [E3(nc.sbuf_tensor(f"wot{i}", [128, TC], F32)) for i in range(2)]

            lw = P.dma("sync", lambda e: e.dma_start(
                out=Wob[:], in_=d_wob.ap().rearrange("(k p) m -> p k m", p=128)))
            P.op("vector", lambda e: e.memset(st[:], 0.0), VC)
            P.wait("tensor", LD, lw)

            roll = {"a3": {}, "dve": {}, "psB_mm": {}, "psC_ev": {}, "tmp": {}, "tmp_rd": {}}
            vg_k = [None, None]
            wo_last_k = [None, None]
            wo_bank_ev = {0: None, 1: None}
            dt_vsp = None
            u_done = None
            u_done_k = [None, None]
            b3_last = None
            bxb_done = None
            dtb_tm = [None, None]
            prev_tdt = None
            gl_n = 0
            for k in range(NCH):
                c0 = TC * k
                kt = slice(c0, c0 + TC)
                pk = k % 2
                if vg_k[pk] is not None:
                    P.wait("sync", VC, vg_k[pk])
                    P.wait("sync", PC, u_done_k[pk])
                l1 = P.dma("sync", lambda e, kt=kt, pk=pk: e.dma_start(out=xct3[pk][:], in_=d_xc_r[:, :, kt]))
                l2 = P.dma("sync", lambda e, kt=kt, pk=pk: e.dma_start(out=zsl[pk][:], in_=d_z_r[:, :, kt]))
                if bxb_done is not None:
                    P.wait("sync", SC, bxb_done)
                if prev_tdt is not None:
                    P.wait("sync", TM, prev_tdt)
                lB = P.dma("sync", lambda e, kt=kt: e.dma_start(out=Bxt[:], in_=d_xpjr.ap()[0:16, kt]))
                lC = P.dma("sync", lambda e, kt=kt: e.dma_start(out=Cxt[:], in_=d_xpjr.ap()[16:32, kt]))
                lD = P.dma("sync", lambda e, kt=kt: e.dma_start(out=dtr[:], in_=d_xpjr.ap()[32:33, kt]))
                P.wait("tensor", LD, lD)
                if dtb_tm[pk] is not None:
                    P.wait("tensor", VC, dt_vsp)
                tdt = P.op("tensor", lambda e, pk=pk: e.matmul(
                    PS[6 + pk][:], ones1t[:], dtr[:], start=True, stop=True), TM)
                dtb_tm[pk] = tdt
                prev_tdt = tdt
                P.wait("vector", TM, tdt)
                for mi in range(NMI):
                    P.op("vector", lambda e, mi=mi, pk=pk: e.tensor_scalar(
                        dtt[:, mi, :], PS[6 + pk][:],
                        vect[:, 4 * mi + 2:4 * mi + 3], vect[:, 4 * mi + 1:4 * mi + 2],
                        ALU.mult, ALU.add))
                vrl = P.op("vector", lambda e: e.tensor_scalar(
                    rb2b[:], dtt[:].rearrange("p m t -> p (m t)"), 0.0, None, ALU.max), VC)
                P.wait("scalar", VC, vrl)
                P.op("scalar", lambda e: e.activation(eb2b[:], dtt[:].rearrange("p m t -> p (m t)"), AF.Abs))
                P.op("scalar", lambda e: e.activation(eb2b[:], eb2b[:], AF.Exp, scale=-1.0))
                ssp = P.op("scalar", lambda e: e.activation(eb2b[:], eb2b[:], AF.Ln, bias=one_t[:, 0:1]), SC)
                P.wait("vector", SC, ssp)
                dt_vsp = P.op("vector", lambda e: e.tensor_tensor(
                    dtt[:].rearrange("p m t -> p (m t)"), rb2b[:], eb2b[:], ALU.add), VC)
                P.wait("gpsimd", VC, dt_vsp)
                P.wait("gpsimd", LD, l1)
                if b3_last is not None:
                    P.wait("gpsimd", VC, b3_last)
                u_done = P.op("gpsimd", lambda e, pk=pk: e.tensor_tensor(
                    ut[:].rearrange("p m t -> p (m t)"), dtt[:].rearrange("p m t -> p (m t)"),
                    xct3[pk][:].rearrange("p m t -> p (m t)"), ALU.mult), PC)
                u_done_k[pk] = u_done
                P.wait("scalar", LD, lC)
                if gl_n > 0:
                    P.wait("scalar", TM, roll["psB_mm"][gl_n - 1])
                P.op("scalar", lambda e: e.activation(Bxb[:], Bxt[:], AF.Copy))
                bxb_done = P.op("scalar", lambda e: e.activation(Cxb[:], Cxt[:], AF.Copy), SC)
                if debug and "dty" in dbg_parts and k == 0:
                    P.wait("sync", VC, dt_vsp)
                    P.dma("sync", lambda e: e.dma_start(out=dbg["dbg_dt"], in_=dtt[:].rearrange("p m t -> p (m t)")))
                first_v = True
                for n in range(NS):
                    g = gl_n
                    gl_n += 1
                    pp = g % 2
                    psB = PS[0 + 2 * pp]
                    psC = PS[1 + 2 * pp]
                    P.wait("tensor", SC, bxb_done)
                    if roll["psC_ev"].get(g - 2) is not None:
                        P.wait("tensor", SC, roll["psC_ev"][g - 2])
                    P.op("tensor", lambda e, n=n, psB=psB: e.matmul(
                        psB[:], selb[:, 128 * n:128 * (n + 1)], Bxb[:], start=True, stop=True))
                    tbc = P.op("tensor", lambda e, n=n, psC=psC: e.matmul(
                        psC[:], selb[:, 128 * n:128 * (n + 1)], Cxb[:], start=True, stop=True), TM)
                    roll["psB_mm"][g] = tbc
                    P.wait("scalar", TM, tbc)
                    if roll["tmp"].get(g - 2) is not None:
                        P.wait("scalar", PC, roll["tmp"][g - 2])
                    if roll["dve"].get(g - 2) is not None:
                        P.wait("scalar", VC, roll["dve"][g - 2])
                    P.op("scalar", lambda e, pp=pp, psB=psB: e.activation(Bbb[pp][:], psB[:], AF.Copy))
                    cev = P.op("scalar", lambda e, pp=pp, psC=psC: e.activation(Cb[pp][:], psC[:], AF.Copy), SC)
                    roll["psC_ev"][g] = cev
                    if roll["dve"].get(g - 2) is not None:
                        P.wait("scalar", VC, roll["dve"][g - 2])
                    if n == 0:
                        P.wait("scalar", VC, dt_vsp)
                    sa = None
                    for mi in range(NMI):
                        sa = P.op("scalar", lambda e, mi=mi, n=n, pp=pp: e.activation(
                            ab[pp][:, TC * mi:TC * (mi + 1)], dtt[:, mi, :], AF.Exp,
                            scale=Aneg[:, 16 * mi + n:16 * mi + n + 1]),
                            SC if mi == NMI - 1 else None)
                    roll["a3"][g] = sa
                    if n == 0:
                        P.wait("vector", PC, u_done)
                    P.wait("vector", SC, cev)
                    if roll["tmp"].get(g - 1) is not None:
                        P.wait("vector", PC, roll["tmp"][g - 1])
                    b3m = P.op("vector", lambda e, pp=pp: e.tensor_tensor(
                        bb3[:].rearrange("p (m t) -> p m t", m=NMI),
                        ut[:], bc_mid(Bbb[pp][:], NMI), ALU.mult), VC)
                    if n == NS - 1:
                        b3_last = b3m
                    P.wait("vector", SC, roll["a3"][g])
                    for mi in range(NMI):
                        P.op("vector", lambda e, mi=mi, n=n, pp=pp: e.tensor_tensor_scan(
                            hta[:, mi, :],
                            ab[pp][:, TC * mi:TC * (mi + 1)],
                            bb3[:, TC * mi:TC * (mi + 1)],
                            st[:, mi, n:n + 1], ALU.mult, ALU.add))
                    vsc = P.op("vector", lambda e, n=n: e.tensor_copy(
                        st[:, :, n:n + 1], hta[:, :, TC - 1:TC]), VC)
                    roll["dve"][g] = vsc
                    P.wait("gpsimd", VC, vsc)
                    P.wait("gpsimd", SC, cev)
                    if roll["tmp_rd"].get(g - 2) is not None:
                        P.wait("gpsimd", VC, roll["tmp_rd"][g - 2])
                    tm_ = P.op("gpsimd", lambda e, pp=pp: e.tensor_tensor(
                        tmpb[pp][:].rearrange("p (m t) -> p m t", m=NMI),
                        hta[:], bc_mid(Cb[pp][:], NMI), ALU.mult), PC)
                    roll["tmp"][g] = tm_
                    P.wait("vector", PC, tm_)
                    if n == 0:
                        yadd = P.op("vector", lambda e, pp=pp, pk=pk: e.tensor_scalar(
                            yt[pk][:].rearrange("p m t -> p (m t)"),
                            tmpb[pp][:], 0.0, None, ALU.add), VC)
                    else:
                        yadd = P.op("vector", lambda e, pp=pp, pk=pk: e.tensor_tensor(
                            yt[pk][:].rearrange("p m t -> p (m t)"),
                            yt[pk][:].rearrange("p m t -> p (m t)"),
                            tmpb[pp][:], ALU.add), VC)
                    roll["tmp_rd"][g] = yadd
                    if debug and "scan" in dbg_parts and k == 0 and n == 0:
                        P.wait("sync", PC, u_done)
                        du = P.dma("sync", lambda e: e.dma_start(out=dbg["dbg_u"], in_=ut[:].rearrange("p m t -> p (m t)")))
                        P.wait("sync", SC, roll["a3"][g])
                        da = P.dma("sync", lambda e, pp=pp: e.dma_start(out=dbg["dbg_a0"], in_=ab[pp][:]))
                        P.wait("sync", VC, roll["dve"][g])
                        dbm = P.dma("sync", lambda e: e.dma_start(out=dbg["dbg_b0"], in_=bb3[:]))
                        dh = P.dma("sync", lambda e: e.dma_start(out=dbg["dbg_h0"], in_=hta[:].rearrange("p m t -> p (m t)")))
                        P.wait("sync", VC, yadd)
                        dy0 = P.dma("sync", lambda e, pk=pk: e.dma_start(out=dbg["dbg_y0"], in_=yt[pk][:].rearrange("p m t -> p (m t)")))
                        P.wait("sync", LD, dy0)
                        P.barrier()
                P.wait("vector", LD, l1)
                for mi in range(NMI):
                    P.op("vector", lambda e, mi=mi, pk=pk: e.scalar_tensor_tensor(
                        yt[pk][:, mi, :], xct3[pk][:, mi, :],
                        vect[:, 4 * mi + 3:4 * mi + 4], yt[pk][:, mi, :], ALU.mult, ALU.add))
                P.wait("vector", LD, l2)
                if wo_last_k[pk] is not None:
                    P.wait("vector", TM, wo_last_k[pk])
                vg = P.op("vector", lambda e, pk=pk: e.tensor_tensor(
                    gt[pk][:].rearrange("p m t -> p (m t)"), yt[pk][:].rearrange("p m t -> p (m t)"),
                    zsl[pk][:].rearrange("p m t -> p (m t)"), ALU.mult), VC)
                vg_k[pk] = vg
                if debug and "dty" in dbg_parts and k == 0:
                    P.wait("sync", VC, vg)
                    P.dma("sync", lambda e: e.dma_start(out=dbg["dbg_y"], in_=yt[0][:].rearrange("p m t -> p (m t)")))
                slot = k
                P.wait("tensor", VC, vg)
                wo_dma = {}
                for mo in range(KD):
                    pw = mo % 2
                    psW = PS[4 + pw]
                    if wo_bank_ev[pw] is not None:
                        P.wait("tensor", SC, wo_bank_ev[pw])
                    two = None
                    for kd in range(NMI):
                        two = P.op("tensor", lambda e, kd=kd, mo=mo, pk=pk, psW=psW: e.matmul(
                            psW[:], Wob[:, kd, 128 * mo:128 * (mo + 1)], gt[pk][:, kd, :],
                            start=(kd == 0), stop=(kd == NMI - 1)), TM if kd == NMI - 1 else None)
                    if mo == KD - 1:
                        wo_last_k[pk] = two
                    P.wait("scalar", TM, two)
                    if wo_dma.get(mo - 2) is not None:
                        P.wait("scalar", LD, wo_dma[mo - 2])
                    swo = P.op("scalar", lambda e, pw=pw, psW=psW: e.activation(wot[pw][:], psW[:], AF.Copy), SC)
                    wo_bank_ev[pw] = swo
                    P.wait("sync", SC, swo)
                    wo_dma[mo] = P.dma("sync", lambda e, k=k, mo=mo, pw=pw: e.dma_start(
                        out=d_wop_j[k // 4].ap()[1024 * (k % 4) + 128 * mo:1024 * (k % 4) + 128 * (mo + 1), :],
                        in_=wot[pw][:]))
                if debug and "wop" in dbg_parts and k == 3:
                    P.wait("sync", LD, P.LD.n)
                    P.dma("sync", lambda e: e.dma_start(out=dbg["dbg_wop"], in_=d_wop_j[0].ap()))
            P.barrier()
        if debug and "wop" in dbg_parts:
            P.dma("sync", lambda e: e.dma_start(out=dbg["dbg_wop2"], in_=d_wop_j[0].ap()))
            P.barrier()
        for j in range(NJ):
            rs_done[j] = P.coll(
                "ReduceScatter", ALU.add,
                [d_wop_j[j].ap().opt()], [d_att_j[j].ap().opt()])
        P.barrier()
        if debug and "rs" in dbg_parts:
            P.wait("sync", CD, rs_done[0])
            P.dma("sync", lambda e: e.dma_start(out=dbg["dbg_wop"][0:D, :], in_=d_att_j[0].ap()))
            P.barrier()

        with ExitStack() as sB:
            T = alloc_ffn_scope(sB, "b")
            for j in range(NJ):
                jt = slice(TC * j, TC * (j + 1))
                l1 = P.dma("sync", lambda e, jt=jt: e.dma_start(out=T["xs"][:], in_=d_x1_r[:, :, jt]))
                l2 = P.dma("sync", lambda e, jt=jt: e.dma_start(out=T["ln2t"][:], in_=d_ln2_r[:, :, jt]))
                P.wait("sync", CD, rs_done[j])
                l3 = P.dma("sync", lambda e, j=j: e.dma_start(
                    out=T["cen"][:],
                    in_=d_att_j[j].ap().rearrange("(k p) t -> p k t", p=128)))
                P.wait("vector", LD, l3)
                P.op("vector", lambda e: e.tensor_tensor(
                    T["xs"][:].rearrange("p k t -> p (k t)"), T["xs"][:].rearrange("p k t -> p (k t)"),
                    T["ln2t"][:].rearrange("p k t -> p (k t)"), ALU.add))
                vx2 = P.op("vector", lambda e: e.tensor_tensor(
                    T["xs"][:].rearrange("p k t -> p (k t)"), T["xs"][:].rearrange("p k t -> p (k t)"),
                    T["cen"][:].rearrange("p k t -> p (k t)"), ALU.add), VC)
                if debug and "p3" in dbg_parts:
                    P.wait("sync", LD, l3)
                    P.dma("sync", lambda e, j=j: e.dma_start(
                        out=dbg["dbg_att"][D * j:D * (j + 1), :].rearrange("(k p) t -> p k t", p=128),
                        in_=T["cen"][:]))
                s_r = ln_stats(T, T["xs"], wait_vc=vx2)
                P.wait("vector", SC, s_r)
                vx = ln_apply_plain(T, T["xs"], T["xnb"])
                P.wait("tensor", VC, vx)
                xv = ffn(T)
                P.wait("sync", VC, xv)
                P.dma("sync", lambda e, jt=jt: e.dma_start(out=o_y_r[:, :, jt], in_=T["xs"][:]))
                P.barrier()

        P.wait("sync", LD, P.LD.n)

        block = E(nc.Block())
        P.emit(block)

    return nc


_DEBUG = False
_NC_CACHE = {}


_DBG_PARTS = ("p3",)


def _get_nc():
    key = (bool(_DEBUG), tuple(_DBG_PARTS))
    if key not in _NC_CACHE:
        _NC_CACHE[key] = build_program(debug=key[0], dbg_parts=key[1])
    return _NC_CACHE[key]


def kernel(L0_attn_x, L0_attn_ln_g, L0_attn_ln_b, L0_attn_W_in, L0_attn_conv_w,
           L0_attn_conv_b, L0_attn_W_xp, L0_attn_W_dt, L0_attn_b_dt, L0_attn_A_log,
           L0_attn_D, L0_attn_W_out, L0_ffn_w_gate, L0_ffn_w_up, L0_ffn_w_down):
    f32 = np.float32
    x = np.asarray(L0_attn_x, f32)
    W_in = np.asarray(L0_attn_W_in, f32)
    W_out = np.asarray(L0_attn_W_out, f32)
    W_xp = np.asarray(L0_attn_W_xp, f32)
    conv_w = np.asarray(L0_attn_conv_w, f32)
    conv_b = np.asarray(L0_attn_conv_b, f32)
    W_dt = np.asarray(L0_attn_W_dt, f32)
    b_dt = np.asarray(L0_attn_b_dt, f32)
    A_log = np.asarray(L0_attn_A_log, f32)
    Dvec = np.asarray(L0_attn_D, f32)
    ln_g = np.asarray(L0_attn_ln_g, f32)
    ln_b = np.asarray(L0_attn_ln_b, f32)
    wg = np.ascontiguousarray(np.asarray(L0_ffn_w_gate, np.int32).T)
    wu = np.ascontiguousarray(np.asarray(L0_ffn_w_up, np.int32).T)
    wd = np.ascontiguousarray(np.asarray(L0_ffn_w_down, np.int32).T)

    sel = np.zeros((16, 2048), f32)
    for n in range(16):
        sel[n, 128 * n:128 * (n + 1)] = 1.0
    ones128 = np.ones((128, 128), f32)
    ones1 = np.ones((1, 128), f32)
    lngb = np.zeros((128, 16), f32)
    for kb in range(KD):
        lngb[:, 2 * kb] = ln_g[128 * kb:128 * (kb + 1)]
        lngb[:, 2 * kb + 1] = ln_b[128 * kb:128 * (kb + 1)]

    in_maps = []
    for c in range(8):
        b, r = divmod(c, 4)
        sl = slice(512 * r, 512 * (r + 1))
        xc = np.concatenate(
            [x[b, TC * (4 * j + r):TC * (4 * j + r + 1), :].T for j in range(NJ)], axis=1)
        winc_c = np.concatenate([W_in[:, sl], W_in[:, 2048 + sl.start:2048 + sl.stop]], axis=1)
        cwc = conv_w[sl].reshape(4, 128, 4).transpose(1, 0, 2).reshape(128, 16)
        avals = A_log[sl].reshape(4, 128, 16).transpose(1, 0, 2).reshape(128, 64)
        vv = np.stack([conv_b[sl], b_dt[sl], W_dt[0, sl], Dvec[sl]], axis=-1)
        vv = vv.reshape(4, 128, 4).transpose(1, 0, 2).reshape(128, 16)
        in_maps.append(dict(
            x_fm=np.ascontiguousarray(xc),
            winc=np.ascontiguousarray(winc_c),
            woc=np.ascontiguousarray(W_out[sl]),
            wxpc=np.ascontiguousarray(W_xp[sl]),
            wgT=wg, wuT=wu, wdT=wd,
            cwc=np.ascontiguousarray(cwc),
            avals=np.ascontiguousarray(avals),
            vecs=np.ascontiguousarray(vv),
            lngb=lngb, sel=sel, ones128=ones128, ones1=ones1,
        ))

    nc = _get_nc()
    res = run_bass_kernel_spmd(nc, in_maps, core_ids=list(range(8)))

    out = np.empty((B, L, D), f32)
    for c in range(8):
        b, r = divmod(c, 4)
        oy = np.asarray(res.results[c]["o_y"], f32)
        for j in range(NJ):
            out[b, TC * (4 * j + r):TC * (4 * j + r + 1), :] = oy[:, TC * j:TC * (j + 1)].T
    if _DEBUG:
        kernel._dbg = res.results
    return out


# version 22
# speedup vs baseline: 1.1282x; 1.0432x over previous
import dataclasses
from contextlib import ExitStack

import numpy as np

import concourse.bass as bass
import concourse.mybir as mybir
from concourse.bass_utils import run_bass_kernel_spmd

F32 = mybir.dt.float32
BF16 = mybir.dt.bfloat16
I32 = mybir.dt.int32
AF = mybir.ActivationFunctionType
ALU = mybir.AluOpType

B, L, D, DIN, H, NS = 2, 8192, 1024, 2048, 3072, 16
TC = 512
NCH = 16
NJ = 4
KD = 8
NMI = 4
HM = H // 128
EPS = 1e-5
GROUPS = [[0, 1, 2, 3], [4, 5, 6, 7]]
ENGINES = ("sync", "vector", "scalar", "tensor", "gpsimd")


def bc_mid(ap, rep):
    assert len(ap.ap) == 2
    return dataclasses.replace(ap, ap=[ap.ap[0], [0, rep], ap.ap[1]])


class Ctr:
    def __init__(self, h):
        self.h = h
        self.n = 0


class Bld:
    def __init__(self, nc, ctx):
        self.nc = nc
        self.ops = {e: [] for e in ENGINES}
        E = ctx.enter_context
        self.LD = Ctr(E(nc.semaphore("LD")))
        self.CC = Ctr(E(nc.semaphore("CC")))
        self.VC = Ctr(E(nc.semaphore("VC")))
        self.SC = Ctr(E(nc.semaphore("SC")))
        self.TM = Ctr(E(nc.semaphore("TM")))
        self.PC = Ctr(E(nc.semaphore("PC")))
        self.CD = Ctr(E(nc.semaphore("CD")))
        self.WQ = Ctr(E(nc.semaphore("WQ")))
        self.WO = Ctr(E(nc.semaphore("WO")))
        self.XN = Ctr(E(nc.semaphore("XN")))
        self.BAR = Ctr(E(nc.semaphore("BAR")))

    def op(self, eng, build, ctr=None):
        if ctr is None:
            self.ops[eng].append(build)
            return None
        ctr.n += 1
        tgt = ctr.n
        self.ops[eng].append(lambda e, b=build, c=ctr: b(e).then_inc(c.h, 1))
        return tgt

    def dma(self, eng, build, ctr=None):
        ctr = ctr or self.LD
        ctr.n += 16
        tgt = ctr.n
        self.ops[eng].append(lambda e, b=build, c=ctr: b(e).then_inc(c.h, 16))
        return tgt

    def wait(self, eng, ctr, val):
        if val is None:
            return
        self.ops[eng].append(lambda e, c=ctr, v=val: e.wait_ge(c.h, v))

    def barrier(self):
        self.wait("sync", self.LD, self.LD.n)
        self.wait("sync", self.WQ, self.WQ.n)
        self.wait("sync", self.WO, self.WO.n)
        self.wait("sync", self.XN, self.XN.n)
        tgt = self.BAR.n + len(ENGINES)
        for e in ENGINES:
            self.ops[e].append(lambda E: E.sem_inc(self.BAR.h, 1))
        self.BAR.n = tgt
        for e in ENGINES:
            self.ops[e].append(lambda E, v=tgt: E.wait_ge(self.BAR.h, v))

    def coll(self, kind, op_, ins, outs, wait_ld=None):
        if wait_ld is not None:
            self.wait("gpsimd", self.LD, wait_ld)
        self.CC.n += 1
        tgt = self.CC.n
        self.ops["gpsimd"].append(lambda e, k=kind, o=op_, i=ins, u=outs: e.collective_compute(
            k, o, GROUPS, i, u).then_inc(self.CC.h, 1))
        self.wait("gpsimd", self.CC, tgt)
        self.ops["gpsimd"].append(lambda e: e.drain(fusable=False))
        self.CD.n += 1
        cdt = self.CD.n
        self.ops["gpsimd"].append(lambda e, v=cdt: e.sem_inc(self.CD.h, 1))
        return cdt

    def emit(self, block):
        b = self
        @block.sync
        def _(eng):
            for f in b.ops["sync"]:
                f(eng)
        @block.vector
        def _(eng):
            for f in b.ops["vector"]:
                f(eng)
        @block.scalar
        def _(eng):
            for f in b.ops["scalar"]:
                f(eng)
        @block.tensor
        def _(eng):
            for f in b.ops["tensor"]:
                f(eng)
        @block.gpsimd
        def _(eng):
            for f in b.ops["gpsimd"]:
                f(eng)


def build_program(debug=False, dbg_parts=("p1","scan","wop","rs","p3","dty")):
    nc = bass.Bass("TRN2", target_bir_lowering=False)

    def din(name, shape, dt=F32):
        return nc.dram_tensor(name, shape, dt, kind="ExternalInput").ap()

    x_fm = din("x_fm", [D, NJ * TC])
    winc = din("winc", [D, 1024])
    woc = din("woc", [512, D])
    wxpc = din("wxpc", [512, 33])
    wgT = din("wgT", [D, H], I32)
    wuT = din("wuT", [D, H], I32)
    wdT = din("wdT", [H, D], I32)
    cwc = din("cwc", [128, 16])
    avals = din("avals", [128, 64])
    vecs = din("vecs", [128, 16])
    lngb = din("lngb", [128, 16])
    sel_i = din("sel", [16, 2048])
    ones128 = din("ones128", [128, 128])
    ones1 = din("ones1", [1, 128])

    o_y = nc.dram_tensor("o_y", [D, NJ * TC], F32, kind="ExternalOutput").ap()

    def dint(name, shape, dt=F32):
        return nc.dram_tensor(name, shape, dt, kind="Internal")

    d_wg = dint("d_wg", [D, H], BF16)
    d_wu = dint("d_wu", [D, H], BF16)
    d_wd = dint("d_wd", [H, D], BF16)
    d_winb = dint("d_winb", [D, 1024], BF16)
    d_wob = dint("d_wob", [512, D], BF16)
    d_x1 = dint("d_x1", [D, NJ * TC], F32)
    d_ln2 = dint("d_ln2", [D, NJ * TC], F32)
    d_xn = dint("d_xn", [NJ * D, TC], BF16)
    d_xng = dint("d_xng", [NJ * 4 * D, TC], BF16)
    d_xc = dint("d_xc", [512, L], F32)
    d_z = dint("d_z", [512, L], F32)
    d_xpj = dint("d_xpj", [33, L], F32)
    d_xpjr = dint("d_xpjr", [33, L], F32)
    d_wop_j = [dint(f"d_wop{j}", [4 * D, TC], F32) for j in range(NJ)]
    d_att_j = [dint(f"d_att{j}", [D, TC], F32) for j in range(NJ)]

    dbg = {}
    if debug:
        for name, shape, dt in [
            ("dbg_xn", [D, NJ * TC], BF16), ("dbg_x1", [D, NJ * TC], F32),
            ("dbg_dt", [128, NMI * TC], F32),
            ("dbg_y", [128, NMI * TC], F32), ("dbg_att", [NJ * D, TC], F32),
            ("dbg_u", [128, NMI * TC], F32), ("dbg_a0", [128, NMI * TC], F32),
            ("dbg_b0", [128, NMI * TC], F32), ("dbg_h0", [128, NMI * TC], F32),
            ("dbg_y0", [128, NMI * TC], F32), ("dbg_wop", [4 * D, TC], F32),
            ("dbg_wop2", [4 * D, TC], F32),
        ]:
            dbg[name] = nc.dram_tensor(name, shape, dt, kind="ExternalOutput").ap()

    x_fm_r = x_fm.rearrange("(k p) t -> p k t", p=128)
    d_x1_r = d_x1.ap().rearrange("(k p) t -> p k t", p=128)
    d_ln2_r = d_ln2.ap().rearrange("(k p) t -> p k t", p=128)
    d_xc_r = d_xc.ap().rearrange("(m p) t -> p m t", p=128)
    d_z_r = d_z.ap().rearrange("(m p) t -> p m t", p=128)
    o_y_r = o_y.rearrange("(k p) t -> p k t", p=128)

    with ExitStack() as ctx:
        E = ctx.enter_context
        P = Bld(nc, ctx)
        LD, CC, VC, SC, TM, PC, CD, WQ, WO, XN = P.LD, P.CC, P.VC, P.SC, P.TM, P.PC, P.CD, P.WQ, P.WO, P.XN

        def sb(name, shape, dt=F32):
            return E(nc.sbuf_tensor(name, shape, dt))

        mu_t = sb("mu_t", [128, TC])
        ms_t = sb("ms_t", [128, TC])
        var_t = sb("var_t", [128, TC])
        rstd_t = sb("rstd_t", [128, TC])
        f1buf = sb("f1buf", [128, TC])
        cwt = sb("cwt", [128, 16])
        Aneg = sb("Aneg", [128, 64])
        vect = sb("vect", [128, 16])
        lngbt = sb("lngbt", [128, 16])
        onesb = sb("onesb", [128, 128], BF16)
        ones1t = sb("ones1t", [1, 128])
        eps_t = sb("eps_t", [128, 1])
        one_t = sb("one_t", [128, 1])
        selb = sb("selb", [16, 2048], BF16)
        st = sb("st", [128, NMI, NS])

        PS = [E(nc.psum_tensor(f"ps{i}", [128, TC], F32)) for i in range(8)]

        with nc.sbuf_tensor("deqi0", [128, H], I32) as deqi0, \
             nc.sbuf_tensor("deqi1", [128, H], I32) as deqi1, \
             nc.sbuf_tensor("deqb0", [128, H], BF16) as deqb0, \
             nc.sbuf_tensor("deqb1", [128, H], BF16) as deqb1, \
             nc.sbuf_tensor("wfl", [128, 1024], F32) as wfl, \
             nc.sbuf_tensor("p0sel", [16, 2048], F32) as p0sel:
            deqi = [deqi0, deqi1]
            deqb = [deqb0, deqb1]

            P.op("vector", lambda e: e.memset(eps_t[:], float(EPS)))
            vcn = P.op("vector", lambda e: e.memset(one_t[:], 1.0), VC)
            P.wait("scalar", VC, vcn)
            P.dma("sync", lambda e: e.dma_start(out=cwt[:], in_=cwc))
            P.dma("sync", lambda e: e.dma_start(out=vect[:], in_=vecs))
            P.dma("sync", lambda e: e.dma_start(out=lngbt[:], in_=lngb))
            P.dma("sync", lambda e: e.dma_start(out=ones1t[:], in_=ones1))
            t5 = P.dma("sync", lambda e: e.dma_start(out=mu_t[:, 0:64], in_=avals))
            P.wait("scalar", LD, t5)
            s1 = P.op("scalar", lambda e: e.activation(ms_t[:, 0:64], mu_t[:, 0:64], AF.Exp), SC)
            P.wait("vector", SC, s1)
            P.op("vector", lambda e: e.tensor_scalar(Aneg[:], ms_t[:, 0:64], -1.0, None, ALU.mult), VC)
            t6 = P.dma("sync", lambda e: e.dma_start(out=rstd_t[:, 0:128], in_=ones128))
            P.wait("scalar", LD, t6)
            P.op("scalar", lambda e: e.activation(onesb[:], rstd_t[:, 0:128], AF.Copy), SC)
            t7 = P.dma("sync", lambda e: e.dma_start(out=p0sel[:], in_=sel_i))
            P.wait("scalar", LD, t7)
            P.op("scalar", lambda e: e.activation(selb[:], p0sel[:], AF.Copy), SC)

            def dequant(src_ap, dst_t, nblocks, width):
                vvs = [None] * nblocks
                dmas = [None] * nblocks
                for i in range(nblocks):
                    bb = i % 2
                    if i >= 2:
                        P.wait("sync", VC, vvs[i - 2])
                    tl = P.dma("sync", lambda e, i=i, bb=bb, s=src_ap, w=width: e.dma_start(
                        out=deqi[bb][:, 0:w], in_=s[128 * i:128 * (i + 1), 0:w]))
                    P.wait("vector", LD, tl)
                    if i >= 2:
                        P.wait("vector", LD, dmas[i - 2])
                    vvs[i] = P.op("vector", lambda e, bb=bb, w=width: e.tensor_scalar(
                        deqb[bb][:, 0:w], deqi[bb][:, 0:w], 1.0 / 127.0, None, ALU.mult), VC)
                    P.wait("sync", VC, vvs[i])
                    dmas[i] = P.dma("sync", lambda e, i=i, bb=bb, d=dst_t, w=width: e.dma_start(
                        out=d.ap()[128 * i:128 * (i + 1), 0:w], in_=deqb[bb][:, 0:w]))

            dequant(wgT, d_wg, KD, H)
            dequant(wuT, d_wu, KD, H)
            dequant(wdT, d_wd, HM, D)

            def stage(src, dst, nblocks):
                prev = {"sv": None, "dma": None}
                for i in range(nblocks):
                    if prev["sv"] is not None:
                        P.wait("sync", SC, prev["sv"])
                    tl = P.dma("sync", lambda e, i=i, s=src: e.dma_start(
                        out=wfl[:], in_=s[128 * i:128 * (i + 1), :]))
                    P.wait("scalar", LD, tl)
                    if prev["dma"] is not None:
                        P.wait("scalar", LD, prev["dma"])
                    sv = P.op("scalar", lambda e: e.activation(deqb[0][:, 0:1024], wfl[:], AF.Copy), SC)
                    prev["sv"] = sv
                    P.wait("sync", SC, sv)
                    prev["dma"] = P.dma("sync", lambda e, i=i, d=dst: e.dma_start(
                        out=d.ap()[128 * i:128 * (i + 1), :], in_=deqb[0][:, 0:1024]))
            stage(winc, d_winb, KD)
            stage(woc, d_wob, NMI)
            P.barrier()

        def ln_stats(T, src, wait_vc=None):
            if wait_vc is not None:
                P.wait("scalar", VC, wait_vc)
            P.op("scalar", lambda e: e.activation(T["xbb"][:], src[:], AF.Copy))
            c2 = P.op("scalar", lambda e: e.activation(T["xnb"][:], src[:], AF.Square), SC)
            P.wait("tensor", SC, c2)
            for kb in range(KD):
                P.op("tensor", lambda e, kb=kb: e.matmul(
                    PS[4][:], onesb[:], T["xbb"][:, kb, :], start=(kb == 0), stop=(kb == KD - 1)),
                    TM if kb == KD - 1 else None)
            tm2 = None
            for kb in range(KD):
                tm2 = P.op("tensor", lambda e, kb=kb: e.matmul(
                    PS[5][:], onesb[:], T["xnb"][:, kb, :], start=(kb == 0), stop=(kb == KD - 1)),
                    TM if kb == KD - 1 else None)
            P.wait("vector", TM, tm2)
            P.op("vector", lambda e: e.tensor_scalar(mu_t[:], PS[4][:], 1.0 / D, None, ALU.mult))
            P.op("vector", lambda e: e.tensor_scalar(ms_t[:], PS[5][:], 1.0 / D, None, ALU.mult))
            P.op("vector", lambda e: e.tensor_tensor(var_t[:], mu_t[:], mu_t[:], ALU.mult))
            v1 = P.op("vector", lambda e: e.tensor_tensor(var_t[:], ms_t[:], var_t[:], ALU.subtract), VC)
            P.wait("scalar", VC, v1)
            P.op("scalar", lambda e: e.activation(rstd_t[:], var_t[:], AF.Ln, bias=eps_t[:, 0:1]))
            s2 = P.op("scalar", lambda e: e.activation(rstd_t[:], rstd_t[:], AF.Exp, scale=-0.5), SC)
            return s2, c2

        def ffn(T, par, prev):
            xs, xnb, fbuf = T["xs"][par], T["xnb"], T["fbuf"]
            wgt_s, wut_s, wdt_s = T["wgt_s"], T["wut_s"], T["wdt_s"]
            if prev.get("tds7") is not None:
                P.wait("sync", TM, prev["tds7"])
            tus = [None] * HM
            fvs = [None] * HM
            f1s = [None] * HM
            for mh in range(HM):
                bw = mh % 2
                pg = PS[0 + 2 * bw]
                pu = PS[1 + 2 * bw]
                if mh >= 2:
                    P.wait("sync", TM, tus[mh - 2])
                P.dma("sync", lambda e, mh=mh, bw=bw: e.dma_start(
                    out=wgt_s[bw][:],
                    in_=d_wg.ap()[:, 128 * mh:128 * (mh + 1)].rearrange("(k p) m -> p k m", p=128)), ctr=WQ)
                lw2 = P.dma("sync", lambda e, mh=mh, bw=bw: e.dma_start(
                    out=wut_s[bw][:],
                    in_=d_wu.ap()[:, 128 * mh:128 * (mh + 1)].rearrange("(k p) m -> p k m", p=128)), ctr=WQ)
                P.wait("tensor", WQ, lw2)
                if mh >= 2:
                    P.wait("tensor", SC, f1s[mh - 2])
                    P.wait("tensor", VC, fvs[mh - 2])
                elif prev.get("f1s7") is not None:
                    P.wait("tensor", SC, prev["f1s7"])
                    P.wait("tensor", VC, prev["fvs7"])
                tg = None
                for kb in range(KD):
                    tg = P.op("tensor", lambda e, kb=kb, bw=bw, pg=pg: e.matmul(
                        pg[:], wgt_s[bw][:, kb, :], xnb[:, kb, :],
                        start=(kb == 0), stop=(kb == KD - 1)), TM if kb == KD - 1 else None)
                for kb in range(KD):
                    tus[mh] = P.op("tensor", lambda e, kb=kb, bw=bw, pu=pu: e.matmul(
                        pu[:], wut_s[bw][:, kb, :], xnb[:, kb, :],
                        start=(kb == 0), stop=(kb == KD - 1)), TM if kb == KD - 1 else None)
                P.wait("scalar", TM, tg)
                if mh >= 1:
                    P.wait("scalar", VC, fvs[mh - 1])
                elif prev.get("fvs7") is not None:
                    P.wait("scalar", VC, prev["fvs7"])
                f1s[mh] = P.op("scalar", lambda e, pg=pg: e.activation(f1buf[:], pg[:], AF.Silu), SC)
                P.wait("vector", SC, f1s[mh])
                P.wait("vector", TM, tus[mh])
                if mh == 0 and prev.get("tds7") is not None:
                    P.wait("vector", TM, prev["tds7"])
                fvs[mh] = P.op("vector", lambda e, mh=mh, pu=pu: e.tensor_tensor(
                    fbuf[:, mh, :], f1buf[:], pu[:], ALU.mult), VC)
            xv = None
            tds = [None] * KD
            for mo in range(KD):
                bw = mo % 2
                pd = PS[6 + bw]
                if mo >= 2:
                    P.wait("sync", TM, tds[mo - 2])
                lw = P.dma("sync", lambda e, mo=mo, bw=bw: e.dma_start(
                    out=wdt_s[bw][:],
                    in_=d_wd.ap()[:, 128 * mo:128 * (mo + 1)].rearrange("(k p) m -> p k m", p=128)), ctr=WQ)
                P.wait("tensor", WQ, lw)
                if mo == 0:
                    P.wait("tensor", VC, fvs[-1])
                if mo >= 2:
                    P.wait("tensor", VC, xv)
                elif prev.get("xv7") is not None:
                    P.wait("tensor", VC, prev["xv7"])
                for kh in range(HM):
                    tds[mo] = P.op("tensor", lambda e, kh=kh, bw=bw, pd=pd: e.matmul(
                        pd[:], wdt_s[bw][:, kh, :], fbuf[:, kh, :],
                        start=(kh == 0), stop=(kh == HM - 1)), TM if kh == HM - 1 else None)
                P.wait("vector", TM, tds[mo])
                xv = P.op("vector", lambda e, mo=mo, pd=pd, xs=xs: e.scalar_tensor_tensor(
                    xs[:, mo, :], pd[:], 0.5, xs[:, mo, :], ALU.mult, ALU.add), VC)
            return dict(xv=xv, tds7=tds[-1], f1s7=f1s[-1], fvs7=fvs[-1], xv7=xv)

        def ln_apply_plain(T, src, dst_bf):
            P.op("vector", lambda e, src=src: e.tensor_tensor(
                T["cen"][:], src[:], bc_mid(mu_t[:], KD), ALU.subtract))
            return P.op("vector", lambda e, dst_bf=dst_bf: e.tensor_tensor(
                dst_bf[:], T["cen"][:], bc_mid(rstd_t[:], KD), ALU.mult), VC)

        def alloc_ffn_scope(stack, tag):
            T = {}
            ES = stack.enter_context
            T["xs"] = [ES(nc.sbuf_tensor(f"xs{tag}{i}", [128, KD, TC], F32)) for i in range(2)]
            T["ln2t"] = ES(nc.sbuf_tensor(f"ln2t{tag}", [128, KD, TC], F32))
            T["cen"] = ES(nc.sbuf_tensor(f"cen{tag}", [128, KD, TC], F32))
            T["xbb"] = ES(nc.sbuf_tensor(f"xbb{tag}", [128, KD, TC], BF16))
            T["xnb"] = ES(nc.sbuf_tensor(f"xnb{tag}", [128, KD, TC], BF16))
            T["fbuf"] = ES(nc.sbuf_tensor(f"fbuf{tag}", [128, HM, TC], BF16))
            T["wgt_s"] = [ES(nc.sbuf_tensor(f"wgt{tag}{i}", [128, KD, 128], BF16)) for i in range(2)]
            T["wut_s"] = [ES(nc.sbuf_tensor(f"wut{tag}{i}", [128, KD, 128], BF16)) for i in range(2)]
            T["wdt_s"] = [ES(nc.sbuf_tensor(f"wdt{tag}{i}", [128, HM, 128], BF16)) for i in range(2)]
            return T

        with ExitStack() as sA:
            T = alloc_ffn_scope(sA, "a")
            prev = {}
            x1sp = [None, None]
            xs_vc = [None, None]
            xs_sc = [None, None]
            for j in range(NJ):
                par = j % 2
                jt = slice(TC * j, TC * (j + 1))
                xs = T["xs"][par]
                if j >= 2:
                    P.wait("sync", VC, xs_vc[par])
                    P.wait("sync", SC, xs_sc[par])
                    P.wait("sync", LD, x1sp[par])
                lx = P.dma("sync", lambda e, jt=jt, xs=xs: e.dma_start(out=xs[:], in_=x_fm_r[:, :, jt]))
                P.wait("scalar", LD, lx)
                P.wait("vector", LD, lx)
                s_r, _ = ln_stats(T, xs)
                P.wait("vector", SC, s_r)
                vx = ln_apply_plain(T, xs, T["xnb"])
                P.wait("tensor", VC, vx)
                prev = ffn(T, par, prev)
                xv = prev["xv"]
                P.wait("sync", VC, xv)
                x1sp[par] = P.dma("sync", lambda e, jt=jt, xs=xs: e.dma_start(
                    out=d_x1_r[:, :, jt], in_=xs[:]))
                s_r, c2m = ln_stats(T, xs, wait_vc=xv)
                xs_sc[par] = c2m
                P.wait("vector", SC, s_r)
                xs_vc[par] = P.op("vector", lambda e, xs=xs: e.tensor_tensor(
                    T["ln2t"][:], xs[:], bc_mid(mu_t[:], KD), ALU.subtract), VC)
                v2 = P.op("vector", lambda e: e.tensor_tensor(
                    T["ln2t"][:], T["ln2t"][:], bc_mid(rstd_t[:], KD), ALU.mult), VC)
                P.wait("sync", VC, v2)
                P.dma("sync", lambda e, jt=jt: e.dma_start(out=d_ln2_r[:, :, jt], in_=T["ln2t"][:]))
                s_r, _ = ln_stats(T, T["ln2t"], wait_vc=v2)
                P.wait("vector", SC, s_r)
                P.op("vector", lambda e: e.tensor_tensor(
                    T["cen"][:], T["ln2t"][:], bc_mid(mu_t[:], KD), ALU.subtract))
                P.op("vector", lambda e: e.tensor_tensor(
                    T["cen"][:], T["cen"][:], bc_mid(rstd_t[:], KD), ALU.mult))
                v3 = None
                for kb in range(KD):
                    v3 = P.op("vector", lambda e, kb=kb: e.tensor_scalar(
                        T["xnb"][:, kb, :], T["cen"][:, kb, :],
                        lngbt[:, 2 * kb:2 * kb + 1], lngbt[:, 2 * kb + 1:2 * kb + 2],
                        ALU.mult, ALU.add), VC)
                P.wait("sync", VC, v3)
                P.dma("sync", lambda e, j=j: e.dma_start(
                    out=d_xn.ap()[D * j:D * (j + 1), :].rearrange("(k p) t -> p k t", p=128),
                    in_=T["xnb"][:]), ctr=XN)
                if debug and "p1" in dbg_parts:
                    P.dma("sync", lambda e, jt=jt: e.dma_start(
                        out=dbg["dbg_xn"].rearrange("(k p) t -> p k t", p=128)[:, :, jt], in_=T["xnb"][:]))
                    P.dma("sync", lambda e, jt=jt, xs=xs: e.dma_start(
                        out=dbg["dbg_x1"].rearrange("(k p) t -> p k t", p=128)[:, :, jt], in_=xs[:]))
            P.barrier()
        ag_done = [None] * NJ
        for j in range(NJ):
            ag_done[j] = P.coll(
                "AllGather", ALU.bypass,
                [d_xn.ap()[D * j:D * (j + 1), :].opt()],
                [d_xng.ap()[4 * D * j:4 * D * (j + 1), :].opt()])

        with ExitStack() as s2a:
            E2 = s2a.enter_context
            Winb = E2(nc.sbuf_tensor("Winb", [128, KD, 1024], BF16))
            xng0 = E2(nc.sbuf_tensor("xng0", [128, KD, TC], BF16))
            xng1 = E2(nc.sbuf_tensor("xng1", [128, KD, TC], BF16))
            xhh = E2(nc.sbuf_tensor("xhh", [128, NMI, TC + 8], F32))
            zst = E2(nc.sbuf_tensor("zst", [128, NMI, TC], F32))
            xct2 = E2(nc.sbuf_tensor("xct2", [128, NMI, TC], F32))
            xcb2 = E2(nc.sbuf_tensor("xcb2", [128, NMI, TC], BF16))
            cvac = E2(nc.sbuf_tensor("cvac", [128, NMI, TC], F32))
            Wxpb = E2(nc.sbuf_tensor("Wxpb", [128, NMI, 33], BF16))
            xpjt = E2(nc.sbuf_tensor("xpjt", [33, TC], F32))
            xng = [xng0, xng1]
            lw = P.dma("sync", lambda e: e.dma_start(
                out=Winb[:], in_=d_winb.ap().rearrange("(k p) m -> p k m", p=128)))
            for mi in range(NMI):
                tw = P.dma("sync", lambda e, mi=mi: e.dma_start(
                    out=var_t[:, 0:33], in_=wxpc[128 * mi:128 * (mi + 1), :]))
                P.wait("scalar", LD, tw)
                sw = P.op("scalar", lambda e, mi=mi: e.activation(Wxpb[:, mi, :], var_t[:, 0:33], AF.Copy), SC)
                P.wait("sync", SC, sw)
            P.wait("tensor", LD, lw)
            P.op("gpsimd", lambda e: e.memset(xhh[:, :, 0:3], 0.0), PC)
            P.wait("scalar", PC, PC.n)
            conv_done = None
            sil_done = None
            cast_done = None
            tpj_prev = None
            spj_prev = None
            zs_spill = None
            xc_spill = None
            pj_spill = None
            bank_ev = {}
            mm_last = [None, None]
            for k in range(NCH):
                bk = k % 2
                r0 = 4 * D * (k // 4) + D * (k % 4)
                c0 = TC * k
                if mm_last[bk] is not None:
                    P.wait("sync", TM, mm_last[bk])
                P.wait("sync", CD, ag_done[k // 4])
                lx = P.dma("sync", lambda e, r0=r0, bk=bk: e.dma_start(
                    out=xng[bk][:],
                    in_=d_xng.ap()[r0:r0 + D, :].rearrange("(k p) t -> p k t", p=128)))
                if k > 0:
                    P.wait("scalar", VC, conv_done)
                    P.op("scalar", lambda e: e.activation(xhh[:, :, 0:3], xhh[:, :, TC:TC + 3], AF.Copy))
                P.wait("tensor", LD, lx)
                ev = []
                for mi in range(KD):
                    psn = mi % 4
                    if bank_ev.get(psn) is not None:
                        P.wait("tensor", SC, bank_ev[psn])
                    tmm = None
                    for kb in range(KD):
                        tmm = P.op("tensor", lambda e, kb=kb, mi=mi, bk=bk, psn=psn: e.matmul(
                            PS[psn][:], Winb[:, kb, 128 * mi:128 * (mi + 1)], xng[bk][:, kb, :],
                            start=(kb == 0), stop=(kb == KD - 1)), TM if kb == KD - 1 else None)
                    if mi == KD - 1:
                        mm_last[bk] = tmm
                    P.wait("scalar", TM, tmm)
                    if mi < NMI:
                        evv = P.op("scalar", lambda e, mi=mi, psn=psn: e.activation(
                            xhh[:, mi, 3:3 + TC], PS[psn][:], AF.Copy), SC)
                    else:
                        if zs_spill is not None and mi == NMI:
                            P.wait("scalar", LD, zs_spill)
                        evv = P.op("scalar", lambda e, mi=mi, psn=psn: e.activation(
                            zst[:, mi - NMI, :], PS[psn][:], AF.Silu), SC)
                    ev.append(evv)
                    bank_ev[psn] = evv
                P.wait("sync", SC, ev[-1])
                zs_spill = P.dma("sync", lambda e, c0=c0: e.dma_start(
                    out=d_z_r[:, :, c0:c0 + TC], in_=zst[:]))
                P.wait("vector", SC, ev[NMI - 1])
                if sil_done is not None:
                    P.wait("vector", SC, sil_done)
                cv = None
                for mi in range(NMI):
                    P.op("vector", lambda e, mi=mi: e.tensor_scalar(
                        cvac[:, mi, :], xhh[:, mi, 0:TC],
                        cwt[:, 4 * mi + 0:4 * mi + 1], vect[:, 4 * mi + 0:4 * mi + 1],
                        ALU.mult, ALU.add))
                    for kk in range(1, 4):
                        cv = P.op("vector", lambda e, mi=mi, kk=kk: e.scalar_tensor_tensor(
                            cvac[:, mi, :], xhh[:, mi, kk:kk + TC],
                            cwt[:, 4 * mi + kk:4 * mi + kk + 1], cvac[:, mi, :],
                            ALU.mult, ALU.add), VC if (mi == NMI - 1 and kk == 3) else None)
                conv_done = cv
                P.wait("scalar", VC, cv)
                if xc_spill is not None:
                    P.wait("scalar", LD, xc_spill)
                sil_done = P.op("scalar", lambda e: e.activation(
                    xct2[:].rearrange("p m t -> p (m t)"), cvac[:].rearrange("p m t -> p (m t)"),
                    AF.Silu), SC)
                if tpj_prev is not None:
                    P.wait("scalar", TM, tpj_prev)
                cast_done = P.op("scalar", lambda e: e.activation(
                    xcb2[:].rearrange("p m t -> p (m t)"), xct2[:].rearrange("p m t -> p (m t)"),
                    AF.Copy), SC)
                P.wait("sync", SC, sil_done)
                xc_spill = P.dma("sync", lambda e, c0=c0: e.dma_start(
                    out=d_xc_r[:, :, c0:c0 + TC], in_=xct2[:]))
                P.wait("tensor", SC, cast_done)
                if spj_prev is not None:
                    P.wait("tensor", SC, spj_prev)
                tpj = None
                for mi in range(NMI):
                    tpj = P.op("tensor", lambda e, mi=mi: e.matmul(
                        PS[4][0:33, :], Wxpb[:, mi, :], xcb2[:, mi, :],
                        start=(mi == 0), stop=(mi == NMI - 1)), TM if mi == NMI - 1 else None)
                tpj_prev = tpj
                P.wait("scalar", TM, tpj)
                if pj_spill is not None:
                    P.wait("scalar", LD, pj_spill)
                spj = P.op("scalar", lambda e: e.activation(xpjt[:], PS[4][0:33, :], AF.Copy), SC)
                spj_prev = spj
                P.wait("sync", SC, spj)
                pj_spill = P.dma("sync", lambda e, c0=c0: e.dma_start(
                    out=d_xpj.ap()[:, c0:c0 + TC], in_=xpjt[:]))
            P.barrier()

        art = P.coll("AllReduce", ALU.add, [d_xpj.ap().opt()], [d_xpjr.ap().opt()],
                     wait_ld=P.LD.n)
        P.barrier()

        rs_done = [None] * NJ
        with ExitStack() as s2b:
            E3 = s2b.enter_context
            Wob = E3(nc.sbuf_tensor("Wob", [128, NMI, 1024], BF16))
            xct3 = [E3(nc.sbuf_tensor(f"xct3{i}", [128, NMI, TC], F32)) for i in range(2)]
            zsl = [E3(nc.sbuf_tensor(f"zsl{i}", [128, NMI, TC], F32)) for i in range(2)]
            dtt = E3(nc.sbuf_tensor("dtt", [128, NMI, TC], F32))
            ut = E3(nc.sbuf_tensor("ut", [128, NMI, TC], BF16))
            Bxt = E3(nc.sbuf_tensor("Bxt", [16, TC], F32))
            Cxt = E3(nc.sbuf_tensor("Cxt", [16, TC], F32))
            dtr = E3(nc.sbuf_tensor("dtr", [1, TC], F32))
            Bxb = E3(nc.sbuf_tensor("Bxb", [16, TC], BF16))
            Cxb = E3(nc.sbuf_tensor("Cxb", [16, TC], BF16))
            Cb = [E3(nc.sbuf_tensor(f"Cb{i}", [128, TC], BF16)) for i in range(2)]
            Bbb = [E3(nc.sbuf_tensor(f"Bbb{i}", [128, TC], BF16)) for i in range(2)]
            ab = [E3(nc.sbuf_tensor(f"ab{i}", [128, NMI * TC], F32)) for i in range(2)]
            bb3 = E3(nc.sbuf_tensor("bb3", [128, NMI * TC], BF16))
            hta = E3(nc.sbuf_tensor("hta", [128, NMI, TC], F32))
            tmpb = [E3(nc.sbuf_tensor(f"tmpb{i}", [128, NMI * TC], BF16)) for i in range(2)]
            yt = [E3(nc.sbuf_tensor(f"yt{i}", [128, NMI, TC], BF16)) for i in range(2)]
            gt = [E3(nc.sbuf_tensor(f"gt{i}", [128, NMI, TC], BF16)) for i in range(2)]
            eb2b = E3(nc.sbuf_tensor("eb2b", [128, NMI * TC], F32))
            rb2b = E3(nc.sbuf_tensor("rb2b", [128, NMI * TC], F32))
            wot = [E3(nc.sbuf_tensor(f"wot{i}", [128, TC], F32)) for i in range(2)]

            lw = P.dma("sync", lambda e: e.dma_start(
                out=Wob[:], in_=d_wob.ap().rearrange("(k p) m -> p k m", p=128)))
            P.op("vector", lambda e: e.memset(st[:], 0.0), VC)
            P.wait("tensor", LD, lw)

            roll = {"a3": {}, "dve": {}, "psB_mm": {}, "psC_ev": {}, "tmp": {}, "tmp_rd": {}}
            vg_k = [None, None]
            wo_last_k = [None, None]
            wo_bank_ev = {0: None, 1: None}
            dt_vsp = None
            u_done = None
            u_done_k = [None, None]
            b3_last = None
            bxb_done = None
            dtb_tm = [None, None]
            prev_tdt = None
            gl_n = 0
            for k in range(NCH):
                c0 = TC * k
                kt = slice(c0, c0 + TC)
                pk = k % 2
                if vg_k[pk] is not None:
                    P.wait("sync", VC, vg_k[pk])
                    P.wait("sync", PC, u_done_k[pk])
                l1 = P.dma("sync", lambda e, kt=kt, pk=pk: e.dma_start(out=xct3[pk][:], in_=d_xc_r[:, :, kt]))
                l2 = P.dma("sync", lambda e, kt=kt, pk=pk: e.dma_start(out=zsl[pk][:], in_=d_z_r[:, :, kt]))
                if bxb_done is not None:
                    P.wait("sync", SC, bxb_done)
                if prev_tdt is not None:
                    P.wait("sync", TM, prev_tdt)
                lB = P.dma("sync", lambda e, kt=kt: e.dma_start(out=Bxt[:], in_=d_xpjr.ap()[0:16, kt]))
                lC = P.dma("sync", lambda e, kt=kt: e.dma_start(out=Cxt[:], in_=d_xpjr.ap()[16:32, kt]))
                lD = P.dma("sync", lambda e, kt=kt: e.dma_start(out=dtr[:], in_=d_xpjr.ap()[32:33, kt]))
                P.wait("tensor", LD, lD)
                if dtb_tm[pk] is not None:
                    P.wait("tensor", VC, dt_vsp)
                tdt = P.op("tensor", lambda e, pk=pk: e.matmul(
                    PS[6 + pk][:], ones1t[:], dtr[:], start=True, stop=True), TM)
                dtb_tm[pk] = tdt
                prev_tdt = tdt
                P.wait("vector", TM, tdt)
                for mi in range(NMI):
                    P.op("vector", lambda e, mi=mi, pk=pk: e.tensor_scalar(
                        dtt[:, mi, :], PS[6 + pk][:],
                        vect[:, 4 * mi + 2:4 * mi + 3], vect[:, 4 * mi + 1:4 * mi + 2],
                        ALU.mult, ALU.add))
                vrl = P.op("vector", lambda e: e.tensor_scalar(
                    rb2b[:], dtt[:].rearrange("p m t -> p (m t)"), 0.0, None, ALU.max), VC)
                P.wait("scalar", VC, vrl)
                P.op("scalar", lambda e: e.activation(eb2b[:], dtt[:].rearrange("p m t -> p (m t)"), AF.Abs))
                P.op("scalar", lambda e: e.activation(eb2b[:], eb2b[:], AF.Exp, scale=-1.0))
                ssp = P.op("scalar", lambda e: e.activation(eb2b[:], eb2b[:], AF.Ln, bias=one_t[:, 0:1]), SC)
                P.wait("vector", SC, ssp)
                dt_vsp = P.op("vector", lambda e: e.tensor_tensor(
                    dtt[:].rearrange("p m t -> p (m t)"), rb2b[:], eb2b[:], ALU.add), VC)
                P.wait("gpsimd", VC, dt_vsp)
                P.wait("gpsimd", LD, l1)
                if b3_last is not None:
                    P.wait("gpsimd", VC, b3_last)
                u_done = P.op("gpsimd", lambda e, pk=pk: e.tensor_tensor(
                    ut[:].rearrange("p m t -> p (m t)"), dtt[:].rearrange("p m t -> p (m t)"),
                    xct3[pk][:].rearrange("p m t -> p (m t)"), ALU.mult), PC)
                u_done_k[pk] = u_done
                P.wait("scalar", LD, lC)
                if gl_n > 0:
                    P.wait("scalar", TM, roll["psB_mm"][gl_n - 1])
                P.op("scalar", lambda e: e.activation(Bxb[:], Bxt[:], AF.Copy))
                bxb_done = P.op("scalar", lambda e: e.activation(Cxb[:], Cxt[:], AF.Copy), SC)
                if debug and "dty" in dbg_parts and k == 0:
                    P.wait("sync", VC, dt_vsp)
                    P.dma("sync", lambda e: e.dma_start(out=dbg["dbg_dt"], in_=dtt[:].rearrange("p m t -> p (m t)")))
                first_v = True
                for n in range(NS):
                    g = gl_n
                    gl_n += 1
                    pp = g % 2
                    psB = PS[0 + 2 * pp]
                    psC = PS[1 + 2 * pp]
                    P.wait("tensor", SC, bxb_done)
                    if roll["psC_ev"].get(g - 2) is not None:
                        P.wait("tensor", SC, roll["psC_ev"][g - 2])
                    P.op("tensor", lambda e, n=n, psB=psB: e.matmul(
                        psB[:], selb[:, 128 * n:128 * (n + 1)], Bxb[:], start=True, stop=True))
                    tbc = P.op("tensor", lambda e, n=n, psC=psC: e.matmul(
                        psC[:], selb[:, 128 * n:128 * (n + 1)], Cxb[:], start=True, stop=True), TM)
                    roll["psB_mm"][g] = tbc
                    P.wait("scalar", TM, tbc)
                    if roll["tmp"].get(g - 2) is not None:
                        P.wait("scalar", PC, roll["tmp"][g - 2])
                    if roll["dve"].get(g - 2) is not None:
                        P.wait("scalar", VC, roll["dve"][g - 2])
                    P.op("scalar", lambda e, pp=pp, psB=psB: e.activation(Bbb[pp][:], psB[:], AF.Copy))
                    cev = P.op("scalar", lambda e, pp=pp, psC=psC: e.activation(Cb[pp][:], psC[:], AF.Copy), SC)
                    roll["psC_ev"][g] = cev
                    if roll["dve"].get(g - 2) is not None:
                        P.wait("scalar", VC, roll["dve"][g - 2])
                    if n == 0:
                        P.wait("scalar", VC, dt_vsp)
                    sa = None
                    for mi in range(NMI):
                        sa = P.op("scalar", lambda e, mi=mi, n=n, pp=pp: e.activation(
                            ab[pp][:, TC * mi:TC * (mi + 1)], dtt[:, mi, :], AF.Exp,
                            scale=Aneg[:, 16 * mi + n:16 * mi + n + 1]),
                            SC if mi == NMI - 1 else None)
                    roll["a3"][g] = sa
                    if n == 0:
                        P.wait("vector", PC, u_done)
                    P.wait("vector", SC, cev)
                    if roll["tmp"].get(g - 1) is not None:
                        P.wait("vector", PC, roll["tmp"][g - 1])
                    b3m = P.op("vector", lambda e, pp=pp: e.tensor_tensor(
                        bb3[:].rearrange("p (m t) -> p m t", m=NMI),
                        ut[:], bc_mid(Bbb[pp][:], NMI), ALU.mult), VC)
                    if n == NS - 1:
                        b3_last = b3m
                    P.wait("vector", SC, roll["a3"][g])
                    for mi in range(NMI):
                        P.op("vector", lambda e, mi=mi, n=n, pp=pp: e.tensor_tensor_scan(
                            hta[:, mi, :],
                            ab[pp][:, TC * mi:TC * (mi + 1)],
                            bb3[:, TC * mi:TC * (mi + 1)],
                            st[:, mi, n:n + 1], ALU.mult, ALU.add))
                    vsc = P.op("vector", lambda e, n=n: e.tensor_copy(
                        st[:, :, n:n + 1], hta[:, :, TC - 1:TC]), VC)
                    roll["dve"][g] = vsc
                    P.wait("gpsimd", VC, vsc)
                    P.wait("gpsimd", SC, cev)
                    if roll["tmp_rd"].get(g - 2) is not None:
                        P.wait("gpsimd", VC, roll["tmp_rd"][g - 2])
                    tm_ = P.op("gpsimd", lambda e, pp=pp: e.tensor_tensor(
                        tmpb[pp][:].rearrange("p (m t) -> p m t", m=NMI),
                        hta[:], bc_mid(Cb[pp][:], NMI), ALU.mult), PC)
                    roll["tmp"][g] = tm_
                    P.wait("vector", PC, tm_)
                    if n == 0:
                        yadd = P.op("vector", lambda e, pp=pp, pk=pk: e.tensor_scalar(
                            yt[pk][:].rearrange("p m t -> p (m t)"),
                            tmpb[pp][:], 0.0, None, ALU.add), VC)
                    else:
                        yadd = P.op("vector", lambda e, pp=pp, pk=pk: e.tensor_tensor(
                            yt[pk][:].rearrange("p m t -> p (m t)"),
                            yt[pk][:].rearrange("p m t -> p (m t)"),
                            tmpb[pp][:], ALU.add), VC)
                    roll["tmp_rd"][g] = yadd
                    if debug and "scan" in dbg_parts and k == 0 and n == 0:
                        P.wait("sync", PC, u_done)
                        du = P.dma("sync", lambda e: e.dma_start(out=dbg["dbg_u"], in_=ut[:].rearrange("p m t -> p (m t)")))
                        P.wait("sync", SC, roll["a3"][g])
                        da = P.dma("sync", lambda e, pp=pp: e.dma_start(out=dbg["dbg_a0"], in_=ab[pp][:]))
                        P.wait("sync", VC, roll["dve"][g])
                        dbm = P.dma("sync", lambda e: e.dma_start(out=dbg["dbg_b0"], in_=bb3[:]))
                        dh = P.dma("sync", lambda e: e.dma_start(out=dbg["dbg_h0"], in_=hta[:].rearrange("p m t -> p (m t)")))
                        P.wait("sync", VC, yadd)
                        dy0 = P.dma("sync", lambda e, pk=pk: e.dma_start(out=dbg["dbg_y0"], in_=yt[pk][:].rearrange("p m t -> p (m t)")))
                        P.wait("sync", LD, dy0)
                        P.barrier()
                P.wait("vector", LD, l1)
                for mi in range(NMI):
                    P.op("vector", lambda e, mi=mi, pk=pk: e.scalar_tensor_tensor(
                        yt[pk][:, mi, :], xct3[pk][:, mi, :],
                        vect[:, 4 * mi + 3:4 * mi + 4], yt[pk][:, mi, :], ALU.mult, ALU.add))
                P.wait("vector", LD, l2)
                if wo_last_k[pk] is not None:
                    P.wait("vector", TM, wo_last_k[pk])
                vg = P.op("vector", lambda e, pk=pk: e.tensor_tensor(
                    gt[pk][:].rearrange("p m t -> p (m t)"), yt[pk][:].rearrange("p m t -> p (m t)"),
                    zsl[pk][:].rearrange("p m t -> p (m t)"), ALU.mult), VC)
                vg_k[pk] = vg
                if debug and "dty" in dbg_parts and k == 0:
                    P.wait("sync", VC, vg)
                    P.dma("sync", lambda e: e.dma_start(out=dbg["dbg_y"], in_=yt[0][:].rearrange("p m t -> p (m t)")))
                slot = k
                P.wait("tensor", VC, vg)
                wo_dma = {}
                for mo in range(KD):
                    pw = mo % 2
                    psW = PS[4 + pw]
                    if wo_bank_ev[pw] is not None:
                        P.wait("tensor", SC, wo_bank_ev[pw])
                    two = None
                    for kd in range(NMI):
                        two = P.op("tensor", lambda e, kd=kd, mo=mo, pk=pk, psW=psW: e.matmul(
                            psW[:], Wob[:, kd, 128 * mo:128 * (mo + 1)], gt[pk][:, kd, :],
                            start=(kd == 0), stop=(kd == NMI - 1)), TM if kd == NMI - 1 else None)
                    if mo == KD - 1:
                        wo_last_k[pk] = two
                    P.wait("scalar", TM, two)
                    if wo_dma.get(mo - 2) is not None:
                        P.wait("scalar", WO, wo_dma[mo - 2])
                    swo = P.op("scalar", lambda e, pw=pw, psW=psW: e.activation(wot[pw][:], psW[:], AF.Copy), SC)
                    wo_bank_ev[pw] = swo
                    P.wait("sync", SC, swo)
                    wo_dma[mo] = P.dma("sync", lambda e, k=k, mo=mo, pw=pw: e.dma_start(
                        out=d_wop_j[k // 4].ap()[1024 * (k % 4) + 128 * mo:1024 * (k % 4) + 128 * (mo + 1), :],
                        in_=wot[pw][:]), ctr=WO)
                if k % 4 == 3:
                    P.wait("gpsimd", WO, 16 * KD * (k + 1))
                    CC.n += 1
                    P.op("gpsimd", lambda e, j=k // 4: e.collective_compute(
                        "ReduceScatter", ALU.add, GROUPS,
                        [d_wop_j[j].ap().opt()], [d_att_j[j].ap().opt()]).then_inc(CC.h, 1))
                if debug and "wop" in dbg_parts and k == 3:
                    P.wait("sync", LD, P.LD.n)
                    P.dma("sync", lambda e: e.dma_start(out=dbg["dbg_wop"], in_=d_wop_j[0].ap()))
            P.wait("gpsimd", CC, CC.n)
            P.op("gpsimd", lambda e: e.drain(fusable=False))
            P.barrier()

        with ExitStack() as sB:
            T = alloc_ffn_scope(sB, "b")
            prev = {}
            oysp = [None, None]
            xs_vc = [None, None]
            xs_sc = [None, None]
            ln2_rd = None
            cen_rd = None
            for j in range(NJ):
                par = j % 2
                jt = slice(TC * j, TC * (j + 1))
                xs = T["xs"][par]
                if j >= 2:
                    P.wait("sync", VC, xs_vc[par])
                    P.wait("sync", SC, xs_sc[par])
                    P.wait("sync", LD, oysp[par])
                l1 = P.dma("sync", lambda e, jt=jt, xs=xs: e.dma_start(out=xs[:], in_=d_x1_r[:, :, jt]))
                if ln2_rd is not None:
                    P.wait("sync", VC, ln2_rd)
                l2 = P.dma("sync", lambda e, jt=jt: e.dma_start(out=T["ln2t"][:], in_=d_ln2_r[:, :, jt]))
                if cen_rd is not None:
                    P.wait("sync", VC, cen_rd)
                l3 = P.dma("sync", lambda e, j=j: e.dma_start(
                    out=T["cen"][:],
                    in_=d_att_j[j].ap().rearrange("(k p) t -> p k t", p=128)))
                P.wait("vector", LD, l3)
                ln2_rd = P.op("vector", lambda e, xs=xs: e.tensor_tensor(
                    xs[:].rearrange("p k t -> p (k t)"), xs[:].rearrange("p k t -> p (k t)"),
                    T["ln2t"][:].rearrange("p k t -> p (k t)"), ALU.add), VC)
                vx2 = P.op("vector", lambda e, xs=xs: e.tensor_tensor(
                    xs[:].rearrange("p k t -> p (k t)"), xs[:].rearrange("p k t -> p (k t)"),
                    T["cen"][:].rearrange("p k t -> p (k t)"), ALU.add), VC)
                if debug and "p3" in dbg_parts:
                    P.wait("sync", LD, l3)
                    P.dma("sync", lambda e, j=j: e.dma_start(
                        out=dbg["dbg_att"][D * j:D * (j + 1), :].rearrange("(k p) t -> p k t", p=128),
                        in_=T["cen"][:]))
                s_r, c2m = ln_stats(T, xs, wait_vc=vx2)
                xs_sc[par] = c2m
                P.wait("vector", SC, s_r)
                vx = ln_apply_plain(T, xs, T["xnb"])
                cen_rd = vx
                xs_vc[par] = vx
                P.wait("tensor", VC, vx)
                prev = ffn(T, par, prev)
                xv = prev["xv"]
                P.wait("sync", VC, xv)
                oysp[par] = P.dma("sync", lambda e, jt=jt, xs=xs: e.dma_start(
                    out=o_y_r[:, :, jt], in_=xs[:]))

        P.wait("sync", LD, P.LD.n)

        block = E(nc.Block())
        P.emit(block)

    return nc


_DEBUG = False
_NC_CACHE = {}


_DBG_PARTS = ("p3",)


def _get_nc():
    key = (bool(_DEBUG), tuple(_DBG_PARTS))
    if key not in _NC_CACHE:
        _NC_CACHE[key] = build_program(debug=key[0], dbg_parts=key[1])
    return _NC_CACHE[key]


def kernel(L0_attn_x, L0_attn_ln_g, L0_attn_ln_b, L0_attn_W_in, L0_attn_conv_w,
           L0_attn_conv_b, L0_attn_W_xp, L0_attn_W_dt, L0_attn_b_dt, L0_attn_A_log,
           L0_attn_D, L0_attn_W_out, L0_ffn_w_gate, L0_ffn_w_up, L0_ffn_w_down):
    f32 = np.float32
    x = np.asarray(L0_attn_x, f32)
    W_in = np.asarray(L0_attn_W_in, f32)
    W_out = np.asarray(L0_attn_W_out, f32)
    W_xp = np.asarray(L0_attn_W_xp, f32)
    conv_w = np.asarray(L0_attn_conv_w, f32)
    conv_b = np.asarray(L0_attn_conv_b, f32)
    W_dt = np.asarray(L0_attn_W_dt, f32)
    b_dt = np.asarray(L0_attn_b_dt, f32)
    A_log = np.asarray(L0_attn_A_log, f32)
    Dvec = np.asarray(L0_attn_D, f32)
    ln_g = np.asarray(L0_attn_ln_g, f32)
    ln_b = np.asarray(L0_attn_ln_b, f32)
    wg = np.ascontiguousarray(np.asarray(L0_ffn_w_gate, np.int32).T)
    wu = np.ascontiguousarray(np.asarray(L0_ffn_w_up, np.int32).T)
    wd = np.ascontiguousarray(np.asarray(L0_ffn_w_down, np.int32).T)

    sel = np.zeros((16, 2048), f32)
    for n in range(16):
        sel[n, 128 * n:128 * (n + 1)] = 1.0
    ones128 = np.ones((128, 128), f32)
    ones1 = np.ones((1, 128), f32)
    lngb = np.zeros((128, 16), f32)
    for kb in range(KD):
        lngb[:, 2 * kb] = ln_g[128 * kb:128 * (kb + 1)]
        lngb[:, 2 * kb + 1] = ln_b[128 * kb:128 * (kb + 1)]

    in_maps = []
    for c in range(8):
        b, r = divmod(c, 4)
        sl = slice(512 * r, 512 * (r + 1))
        xc = np.concatenate(
            [x[b, TC * (4 * j + r):TC * (4 * j + r + 1), :].T for j in range(NJ)], axis=1)
        winc_c = np.concatenate([W_in[:, sl], W_in[:, 2048 + sl.start:2048 + sl.stop]], axis=1)
        cwc = conv_w[sl].reshape(4, 128, 4).transpose(1, 0, 2).reshape(128, 16)
        avals = A_log[sl].reshape(4, 128, 16).transpose(1, 0, 2).reshape(128, 64)
        vv = np.stack([conv_b[sl], b_dt[sl], W_dt[0, sl], Dvec[sl]], axis=-1)
        vv = vv.reshape(4, 128, 4).transpose(1, 0, 2).reshape(128, 16)
        in_maps.append(dict(
            x_fm=np.ascontiguousarray(xc),
            winc=np.ascontiguousarray(winc_c),
            woc=np.ascontiguousarray(W_out[sl]),
            wxpc=np.ascontiguousarray(W_xp[sl]),
            wgT=wg, wuT=wu, wdT=wd,
            cwc=np.ascontiguousarray(cwc),
            avals=np.ascontiguousarray(avals),
            vecs=np.ascontiguousarray(vv),
            lngb=lngb, sel=sel, ones128=ones128, ones1=ones1,
        ))

    nc = _get_nc()
    res = run_bass_kernel_spmd(nc, in_maps, core_ids=list(range(8)))

    out = np.empty((B, L, D), f32)
    for c in range(8):
        b, r = divmod(c, 4)
        oy = np.asarray(res.results[c]["o_y"], f32)
        for j in range(NJ):
            out[b, TC * (4 * j + r):TC * (4 * j + r + 1), :] = oy[:, TC * j:TC * (j + 1)].T
    if _DEBUG:
        kernel._dbg = res.results
    return out


# version 23
# speedup vs baseline: 1.1334x; 1.0046x over previous
import dataclasses
from contextlib import ExitStack

import numpy as np

import concourse.bass as bass
import concourse.mybir as mybir
from concourse.bass_utils import run_bass_kernel_spmd

F32 = mybir.dt.float32
BF16 = mybir.dt.bfloat16
I32 = mybir.dt.int32
AF = mybir.ActivationFunctionType
ALU = mybir.AluOpType

B, L, D, DIN, H, NS = 2, 8192, 1024, 2048, 3072, 16
TC = 512
NCH = 16
NJ = 4
KD = 8
NMI = 4
HM = H // 128
EPS = 1e-5
GROUPS = [[0, 1, 2, 3], [4, 5, 6, 7]]
ENGINES = ("sync", "vector", "scalar", "tensor", "gpsimd")


def bc_mid(ap, rep):
    assert len(ap.ap) == 2
    return dataclasses.replace(ap, ap=[ap.ap[0], [0, rep], ap.ap[1]])


class Ctr:
    def __init__(self, h):
        self.h = h
        self.n = 0


class Bld:
    def __init__(self, nc, ctx):
        self.nc = nc
        self.ops = {e: [] for e in ENGINES}
        E = ctx.enter_context
        self.LD = Ctr(E(nc.semaphore("LD")))
        self.CC = Ctr(E(nc.semaphore("CC")))
        self.VC = Ctr(E(nc.semaphore("VC")))
        self.SC = Ctr(E(nc.semaphore("SC")))
        self.TM = Ctr(E(nc.semaphore("TM")))
        self.PC = Ctr(E(nc.semaphore("PC")))
        self.CD = Ctr(E(nc.semaphore("CD")))
        self.WQ = Ctr(E(nc.semaphore("WQ")))
        self.WO = Ctr(E(nc.semaphore("WO")))
        self.XN = Ctr(E(nc.semaphore("XN")))
        self.BAR = Ctr(E(nc.semaphore("BAR")))

    def op(self, eng, build, ctr=None):
        if ctr is None:
            self.ops[eng].append(build)
            return None
        ctr.n += 1
        tgt = ctr.n
        self.ops[eng].append(lambda e, b=build, c=ctr: b(e).then_inc(c.h, 1))
        return tgt

    def dma(self, eng, build, ctr=None):
        ctr = ctr or self.LD
        ctr.n += 16
        tgt = ctr.n
        self.ops[eng].append(lambda e, b=build, c=ctr: b(e).then_inc(c.h, 16))
        return tgt

    def wait(self, eng, ctr, val):
        if val is None:
            return
        self.ops[eng].append(lambda e, c=ctr, v=val: e.wait_ge(c.h, v))

    def barrier(self):
        self.wait("sync", self.LD, self.LD.n)
        self.wait("sync", self.WQ, self.WQ.n)
        self.wait("sync", self.WO, self.WO.n)
        self.wait("sync", self.XN, self.XN.n)
        tgt = self.BAR.n + len(ENGINES)
        for e in ENGINES:
            self.ops[e].append(lambda E: E.sem_inc(self.BAR.h, 1))
        self.BAR.n = tgt
        for e in ENGINES:
            self.ops[e].append(lambda E, v=tgt: E.wait_ge(self.BAR.h, v))

    def coll(self, kind, op_, ins, outs, wait_ld=None):
        if wait_ld is not None:
            self.wait("gpsimd", self.LD, wait_ld)
        self.CC.n += 1
        tgt = self.CC.n
        self.ops["gpsimd"].append(lambda e, k=kind, o=op_, i=ins, u=outs: e.collective_compute(
            k, o, GROUPS, i, u).then_inc(self.CC.h, 1))
        self.wait("gpsimd", self.CC, tgt)
        self.ops["gpsimd"].append(lambda e: e.drain(fusable=False))
        self.CD.n += 1
        cdt = self.CD.n
        self.ops["gpsimd"].append(lambda e, v=cdt: e.sem_inc(self.CD.h, 1))
        return cdt

    def emit(self, block):
        b = self
        @block.sync
        def _(eng):
            for f in b.ops["sync"]:
                f(eng)
        @block.vector
        def _(eng):
            for f in b.ops["vector"]:
                f(eng)
        @block.scalar
        def _(eng):
            for f in b.ops["scalar"]:
                f(eng)
        @block.tensor
        def _(eng):
            for f in b.ops["tensor"]:
                f(eng)
        @block.gpsimd
        def _(eng):
            for f in b.ops["gpsimd"]:
                f(eng)


def build_program(debug=False, dbg_parts=("p1","scan","wop","rs","p3","dty")):
    nc = bass.Bass("TRN2", target_bir_lowering=False, detect_race_conditions=False)

    def din(name, shape, dt=F32):
        return nc.dram_tensor(name, shape, dt, kind="ExternalInput").ap()

    x_fm = din("x_fm", [D, NJ * TC])
    winc = din("winc", [D, 1024])
    woc = din("woc", [512, D])
    wxpc = din("wxpc", [512, 33])
    wgT = din("wgT", [D, H], I32)
    wuT = din("wuT", [D, H], I32)
    wdT = din("wdT", [H, D], I32)
    cwc = din("cwc", [128, 16])
    avals = din("avals", [128, 64])
    vecs = din("vecs", [128, 16])
    lngb = din("lngb", [128, 16])
    sel_i = din("sel", [16, 2048])
    ones128 = din("ones128", [128, 128])
    ones1 = din("ones1", [1, 128])

    o_y = nc.dram_tensor("o_y", [D, NJ * TC], F32, kind="ExternalOutput").ap()

    def dint(name, shape, dt=F32):
        return nc.dram_tensor(name, shape, dt, kind="Internal")

    d_wg = dint("d_wg", [D, H], BF16)
    d_wu = dint("d_wu", [D, H], BF16)
    d_wd = dint("d_wd", [H, D], BF16)
    d_winb = dint("d_winb", [D, 1024], BF16)
    d_wob = dint("d_wob", [512, D], BF16)
    d_x1 = dint("d_x1", [D, NJ * TC], F32)
    d_ln2 = dint("d_ln2", [D, NJ * TC], F32)
    d_xn = dint("d_xn", [NJ * D, TC], BF16)
    d_xng = dint("d_xng", [NJ * 4 * D, TC], BF16)
    d_xc = dint("d_xc", [512, L], F32)
    d_z = dint("d_z", [512, L], F32)
    d_xpj = dint("d_xpj", [33, L], F32)
    d_xpjr = dint("d_xpjr", [33, L], F32)
    d_wop_j = [dint(f"d_wop{j}", [4 * D, TC], F32) for j in range(NJ)]
    d_att_j = [dint(f"d_att{j}", [D, TC], F32) for j in range(NJ)]

    dbg = {}
    if debug:
        for name, shape, dt in [
            ("dbg_xn", [D, NJ * TC], BF16), ("dbg_x1", [D, NJ * TC], F32),
            ("dbg_dt", [128, NMI * TC], F32),
            ("dbg_y", [128, NMI * TC], F32), ("dbg_att", [NJ * D, TC], F32),
            ("dbg_u", [128, NMI * TC], F32), ("dbg_a0", [128, NMI * TC], F32),
            ("dbg_b0", [128, NMI * TC], F32), ("dbg_h0", [128, NMI * TC], F32),
            ("dbg_y0", [128, NMI * TC], F32), ("dbg_wop", [4 * D, TC], F32),
            ("dbg_wop2", [4 * D, TC], F32),
        ]:
            dbg[name] = nc.dram_tensor(name, shape, dt, kind="ExternalOutput").ap()

    x_fm_r = x_fm.rearrange("(k p) t -> p k t", p=128)
    d_x1_r = d_x1.ap().rearrange("(k p) t -> p k t", p=128)
    d_ln2_r = d_ln2.ap().rearrange("(k p) t -> p k t", p=128)
    d_xc_r = d_xc.ap().rearrange("(m p) t -> p m t", p=128)
    d_z_r = d_z.ap().rearrange("(m p) t -> p m t", p=128)
    o_y_r = o_y.rearrange("(k p) t -> p k t", p=128)

    with ExitStack() as ctx:
        E = ctx.enter_context
        P = Bld(nc, ctx)
        LD, CC, VC, SC, TM, PC, CD, WQ, WO, XN = P.LD, P.CC, P.VC, P.SC, P.TM, P.PC, P.CD, P.WQ, P.WO, P.XN

        def sb(name, shape, dt=F32):
            return E(nc.sbuf_tensor(name, shape, dt))

        mu_t = sb("mu_t", [128, TC])
        ms_t = sb("ms_t", [128, TC])
        var_t = sb("var_t", [128, TC])
        rstd_t = sb("rstd_t", [128, TC])
        f1buf = sb("f1buf", [128, TC])
        cwt = sb("cwt", [128, 16])
        Aneg = sb("Aneg", [128, 64])
        vect = sb("vect", [128, 16])
        lngbt = sb("lngbt", [128, 16])
        onesb = sb("onesb", [128, 128], BF16)
        ones1t = sb("ones1t", [1, 128])
        eps_t = sb("eps_t", [128, 1])
        one_t = sb("one_t", [128, 1])
        selb = sb("selb", [16, 2048], BF16)
        st = sb("st", [128, NMI, NS])

        PS = [E(nc.psum_tensor(f"ps{i}", [128, TC], F32)) for i in range(8)]

        with nc.sbuf_tensor("deqi0", [128, H], I32) as deqi0, \
             nc.sbuf_tensor("deqi1", [128, H], I32) as deqi1, \
             nc.sbuf_tensor("deqb0", [128, H], BF16) as deqb0, \
             nc.sbuf_tensor("deqb1", [128, H], BF16) as deqb1, \
             nc.sbuf_tensor("wfl", [128, 1024], F32) as wfl, \
             nc.sbuf_tensor("p0sel", [16, 2048], F32) as p0sel:
            deqi = [deqi0, deqi1]
            deqb = [deqb0, deqb1]

            P.op("vector", lambda e: e.memset(eps_t[:], float(EPS)))
            vcn = P.op("vector", lambda e: e.memset(one_t[:], 1.0), VC)
            P.wait("scalar", VC, vcn)
            P.dma("sync", lambda e: e.dma_start(out=cwt[:], in_=cwc))
            P.dma("sync", lambda e: e.dma_start(out=vect[:], in_=vecs))
            P.dma("sync", lambda e: e.dma_start(out=lngbt[:], in_=lngb))
            P.dma("sync", lambda e: e.dma_start(out=ones1t[:], in_=ones1))
            t5 = P.dma("sync", lambda e: e.dma_start(out=mu_t[:, 0:64], in_=avals))
            P.wait("scalar", LD, t5)
            s1 = P.op("scalar", lambda e: e.activation(ms_t[:, 0:64], mu_t[:, 0:64], AF.Exp), SC)
            P.wait("vector", SC, s1)
            P.op("vector", lambda e: e.tensor_scalar(Aneg[:], ms_t[:, 0:64], -1.0, None, ALU.mult), VC)
            t6 = P.dma("sync", lambda e: e.dma_start(out=rstd_t[:, 0:128], in_=ones128))
            P.wait("scalar", LD, t6)
            P.op("scalar", lambda e: e.activation(onesb[:], rstd_t[:, 0:128], AF.Copy), SC)
            t7 = P.dma("sync", lambda e: e.dma_start(out=p0sel[:], in_=sel_i))
            P.wait("scalar", LD, t7)
            P.op("scalar", lambda e: e.activation(selb[:], p0sel[:], AF.Copy), SC)

            def dequant(src_ap, dst_t, nblocks, width):
                vvs = [None] * nblocks
                dmas = [None] * nblocks
                for i in range(nblocks):
                    bb = i % 2
                    if i >= 2:
                        P.wait("sync", VC, vvs[i - 2])
                    tl = P.dma("sync", lambda e, i=i, bb=bb, s=src_ap, w=width: e.dma_start(
                        out=deqi[bb][:, 0:w], in_=s[128 * i:128 * (i + 1), 0:w]))
                    P.wait("vector", LD, tl)
                    if i >= 2:
                        P.wait("vector", LD, dmas[i - 2])
                    vvs[i] = P.op("vector", lambda e, bb=bb, w=width: e.tensor_scalar(
                        deqb[bb][:, 0:w], deqi[bb][:, 0:w], 1.0 / 127.0, None, ALU.mult), VC)
                    P.wait("sync", VC, vvs[i])
                    dmas[i] = P.dma("sync", lambda e, i=i, bb=bb, d=dst_t, w=width: e.dma_start(
                        out=d.ap()[128 * i:128 * (i + 1), 0:w], in_=deqb[bb][:, 0:w]))

            dequant(wgT, d_wg, KD, H)
            dequant(wuT, d_wu, KD, H)
            dequant(wdT, d_wd, HM, D)

            def stage(src, dst, nblocks):
                prev = {"sv": None, "dma": None}
                for i in range(nblocks):
                    if prev["sv"] is not None:
                        P.wait("sync", SC, prev["sv"])
                    tl = P.dma("sync", lambda e, i=i, s=src: e.dma_start(
                        out=wfl[:], in_=s[128 * i:128 * (i + 1), :]))
                    P.wait("scalar", LD, tl)
                    if prev["dma"] is not None:
                        P.wait("scalar", LD, prev["dma"])
                    sv = P.op("scalar", lambda e: e.activation(deqb[0][:, 0:1024], wfl[:], AF.Copy), SC)
                    prev["sv"] = sv
                    P.wait("sync", SC, sv)
                    prev["dma"] = P.dma("sync", lambda e, i=i, d=dst: e.dma_start(
                        out=d.ap()[128 * i:128 * (i + 1), :], in_=deqb[0][:, 0:1024]))
            stage(winc, d_winb, KD)
            stage(woc, d_wob, NMI)
            P.barrier()

        def ln_stats(T, src, wait_vc=None):
            if wait_vc is not None:
                P.wait("scalar", VC, wait_vc)
            P.op("scalar", lambda e: e.activation(T["xbb"][:], src[:], AF.Copy))
            c2 = P.op("scalar", lambda e: e.activation(T["xnb"][:], src[:], AF.Square), SC)
            P.wait("tensor", SC, c2)
            for kb in range(KD):
                P.op("tensor", lambda e, kb=kb: e.matmul(
                    PS[4][:], onesb[:], T["xbb"][:, kb, :], start=(kb == 0), stop=(kb == KD - 1)),
                    TM if kb == KD - 1 else None)
            tm2 = None
            for kb in range(KD):
                tm2 = P.op("tensor", lambda e, kb=kb: e.matmul(
                    PS[5][:], onesb[:], T["xnb"][:, kb, :], start=(kb == 0), stop=(kb == KD - 1)),
                    TM if kb == KD - 1 else None)
            P.wait("vector", TM, tm2)
            P.op("vector", lambda e: e.tensor_scalar(mu_t[:], PS[4][:], 1.0 / D, None, ALU.mult))
            P.op("vector", lambda e: e.tensor_scalar(ms_t[:], PS[5][:], 1.0 / D, None, ALU.mult))
            P.op("vector", lambda e: e.tensor_tensor(var_t[:], mu_t[:], mu_t[:], ALU.mult))
            v1 = P.op("vector", lambda e: e.tensor_tensor(var_t[:], ms_t[:], var_t[:], ALU.subtract), VC)
            P.wait("scalar", VC, v1)
            P.op("scalar", lambda e: e.activation(rstd_t[:], var_t[:], AF.Ln, bias=eps_t[:, 0:1]))
            s2 = P.op("scalar", lambda e: e.activation(rstd_t[:], rstd_t[:], AF.Exp, scale=-0.5), SC)
            return s2, c2

        def ffn(T, par, prev):
            xs, xnb, fbuf = T["xs"][par], T["xnb"], T["fbuf"]
            wgt_s, wut_s, wdt_s = T["wgt_s"], T["wut_s"], T["wdt_s"]
            if prev.get("tds7") is not None:
                P.wait("sync", TM, prev["tds7"])
            tus = [None] * HM
            fvs = [None] * HM
            f1s = [None] * HM
            for mh in range(HM):
                bw = mh % 2
                pg = PS[0 + 2 * bw]
                pu = PS[1 + 2 * bw]
                if mh >= 2:
                    P.wait("sync", TM, tus[mh - 2])
                P.dma("sync", lambda e, mh=mh, bw=bw: e.dma_start(
                    out=wgt_s[bw][:],
                    in_=d_wg.ap()[:, 128 * mh:128 * (mh + 1)].rearrange("(k p) m -> p k m", p=128)), ctr=WQ)
                lw2 = P.dma("sync", lambda e, mh=mh, bw=bw: e.dma_start(
                    out=wut_s[bw][:],
                    in_=d_wu.ap()[:, 128 * mh:128 * (mh + 1)].rearrange("(k p) m -> p k m", p=128)), ctr=WQ)
                P.wait("tensor", WQ, lw2)
                if mh >= 2:
                    P.wait("tensor", SC, f1s[mh - 2])
                    P.wait("tensor", VC, fvs[mh - 2])
                elif prev.get("f1s7") is not None:
                    P.wait("tensor", SC, prev["f1s7"])
                    P.wait("tensor", VC, prev["fvs7"])
                tg = None
                for kb in range(KD):
                    tg = P.op("tensor", lambda e, kb=kb, bw=bw, pg=pg: e.matmul(
                        pg[:], wgt_s[bw][:, kb, :], xnb[:, kb, :],
                        start=(kb == 0), stop=(kb == KD - 1)), TM if kb == KD - 1 else None)
                for kb in range(KD):
                    tus[mh] = P.op("tensor", lambda e, kb=kb, bw=bw, pu=pu: e.matmul(
                        pu[:], wut_s[bw][:, kb, :], xnb[:, kb, :],
                        start=(kb == 0), stop=(kb == KD - 1)), TM if kb == KD - 1 else None)
                P.wait("scalar", TM, tg)
                if mh >= 1:
                    P.wait("scalar", VC, fvs[mh - 1])
                elif prev.get("fvs7") is not None:
                    P.wait("scalar", VC, prev["fvs7"])
                f1s[mh] = P.op("scalar", lambda e, pg=pg: e.activation(f1buf[:], pg[:], AF.Silu), SC)
                P.wait("vector", SC, f1s[mh])
                P.wait("vector", TM, tus[mh])
                if mh == 0 and prev.get("tds7") is not None:
                    P.wait("vector", TM, prev["tds7"])
                fvs[mh] = P.op("vector", lambda e, mh=mh, pu=pu: e.tensor_tensor(
                    fbuf[:, mh, :], f1buf[:], pu[:], ALU.mult), VC)
            xv = None
            tds = [None] * KD
            for mo in range(KD):
                bw = mo % 2
                pd = PS[6 + bw]
                if mo >= 2:
                    P.wait("sync", TM, tds[mo - 2])
                lw = P.dma("sync", lambda e, mo=mo, bw=bw: e.dma_start(
                    out=wdt_s[bw][:],
                    in_=d_wd.ap()[:, 128 * mo:128 * (mo + 1)].rearrange("(k p) m -> p k m", p=128)), ctr=WQ)
                P.wait("tensor", WQ, lw)
                if mo == 0:
                    P.wait("tensor", VC, fvs[-1])
                if mo >= 2:
                    P.wait("tensor", VC, xv)
                elif prev.get("xv7") is not None:
                    P.wait("tensor", VC, prev["xv7"])
                for kh in range(HM):
                    tds[mo] = P.op("tensor", lambda e, kh=kh, bw=bw, pd=pd: e.matmul(
                        pd[:], wdt_s[bw][:, kh, :], fbuf[:, kh, :],
                        start=(kh == 0), stop=(kh == HM - 1)), TM if kh == HM - 1 else None)
                P.wait("vector", TM, tds[mo])
                xv = P.op("vector", lambda e, mo=mo, pd=pd, xs=xs: e.scalar_tensor_tensor(
                    xs[:, mo, :], pd[:], 0.5, xs[:, mo, :], ALU.mult, ALU.add), VC)
            return dict(xv=xv, tds7=tds[-1], f1s7=f1s[-1], fvs7=fvs[-1], xv7=xv)

        def ln_apply_plain(T, src, dst_bf):
            P.op("vector", lambda e, src=src: e.tensor_tensor(
                T["cen"][:], src[:], bc_mid(mu_t[:], KD), ALU.subtract))
            return P.op("vector", lambda e, dst_bf=dst_bf: e.tensor_tensor(
                dst_bf[:], T["cen"][:], bc_mid(rstd_t[:], KD), ALU.mult), VC)

        def alloc_ffn_scope(stack, tag):
            T = {}
            ES = stack.enter_context
            T["xs"] = [ES(nc.sbuf_tensor(f"xs{tag}{i}", [128, KD, TC], F32)) for i in range(2)]
            T["ln2t"] = ES(nc.sbuf_tensor(f"ln2t{tag}", [128, KD, TC], F32))
            T["cen"] = ES(nc.sbuf_tensor(f"cen{tag}", [128, KD, TC], F32))
            T["xbb"] = ES(nc.sbuf_tensor(f"xbb{tag}", [128, KD, TC], BF16))
            T["xnb"] = ES(nc.sbuf_tensor(f"xnb{tag}", [128, KD, TC], BF16))
            T["fbuf"] = ES(nc.sbuf_tensor(f"fbuf{tag}", [128, HM, TC], BF16))
            T["wgt_s"] = [ES(nc.sbuf_tensor(f"wgt{tag}{i}", [128, KD, 128], BF16)) for i in range(2)]
            T["wut_s"] = [ES(nc.sbuf_tensor(f"wut{tag}{i}", [128, KD, 128], BF16)) for i in range(2)]
            T["wdt_s"] = [ES(nc.sbuf_tensor(f"wdt{tag}{i}", [128, HM, 128], BF16)) for i in range(2)]
            return T

        with ExitStack() as sA:
            T = alloc_ffn_scope(sA, "a")
            prev = {}
            x1sp = [None, None]
            xs_vc = [None, None]
            xs_sc = [None, None]
            for j in range(NJ):
                par = j % 2
                jt = slice(TC * j, TC * (j + 1))
                xs = T["xs"][par]
                if j >= 2:
                    P.wait("sync", VC, xs_vc[par])
                    P.wait("sync", SC, xs_sc[par])
                    P.wait("sync", LD, x1sp[par])
                lx = P.dma("sync", lambda e, jt=jt, xs=xs: e.dma_start(out=xs[:], in_=x_fm_r[:, :, jt]))
                P.wait("scalar", LD, lx)
                P.wait("vector", LD, lx)
                s_r, _ = ln_stats(T, xs)
                P.wait("vector", SC, s_r)
                vx = ln_apply_plain(T, xs, T["xnb"])
                P.wait("tensor", VC, vx)
                prev = ffn(T, par, prev)
                xv = prev["xv"]
                P.wait("sync", VC, xv)
                x1sp[par] = P.dma("sync", lambda e, jt=jt, xs=xs: e.dma_start(
                    out=d_x1_r[:, :, jt], in_=xs[:]))
                s_r, c2m = ln_stats(T, xs, wait_vc=xv)
                xs_sc[par] = c2m
                P.wait("vector", SC, s_r)
                xs_vc[par] = P.op("vector", lambda e, xs=xs: e.tensor_tensor(
                    T["ln2t"][:], xs[:], bc_mid(mu_t[:], KD), ALU.subtract), VC)
                v2 = P.op("vector", lambda e: e.tensor_tensor(
                    T["ln2t"][:], T["ln2t"][:], bc_mid(rstd_t[:], KD), ALU.mult), VC)
                P.wait("sync", VC, v2)
                P.dma("sync", lambda e, jt=jt: e.dma_start(out=d_ln2_r[:, :, jt], in_=T["ln2t"][:]))
                s_r, _ = ln_stats(T, T["ln2t"], wait_vc=v2)
                P.wait("vector", SC, s_r)
                P.op("vector", lambda e: e.tensor_tensor(
                    T["cen"][:], T["ln2t"][:], bc_mid(mu_t[:], KD), ALU.subtract))
                P.op("vector", lambda e: e.tensor_tensor(
                    T["cen"][:], T["cen"][:], bc_mid(rstd_t[:], KD), ALU.mult))
                v3 = None
                for kb in range(KD):
                    v3 = P.op("vector", lambda e, kb=kb: e.tensor_scalar(
                        T["xnb"][:, kb, :], T["cen"][:, kb, :],
                        lngbt[:, 2 * kb:2 * kb + 1], lngbt[:, 2 * kb + 1:2 * kb + 2],
                        ALU.mult, ALU.add), VC)
                P.wait("sync", VC, v3)
                P.dma("sync", lambda e, j=j: e.dma_start(
                    out=d_xn.ap()[D * j:D * (j + 1), :].rearrange("(k p) t -> p k t", p=128),
                    in_=T["xnb"][:]), ctr=XN)
                if debug and "p1" in dbg_parts:
                    P.dma("sync", lambda e, jt=jt: e.dma_start(
                        out=dbg["dbg_xn"].rearrange("(k p) t -> p k t", p=128)[:, :, jt], in_=T["xnb"][:]))
                    P.dma("sync", lambda e, jt=jt, xs=xs: e.dma_start(
                        out=dbg["dbg_x1"].rearrange("(k p) t -> p k t", p=128)[:, :, jt], in_=xs[:]))
            P.barrier()
        ag_done = [None] * NJ
        for j in range(NJ):
            ag_done[j] = P.coll(
                "AllGather", ALU.bypass,
                [d_xn.ap()[D * j:D * (j + 1), :].opt()],
                [d_xng.ap()[4 * D * j:4 * D * (j + 1), :].opt()])

        with ExitStack() as s2a:
            E2 = s2a.enter_context
            Winb = E2(nc.sbuf_tensor("Winb", [128, KD, 1024], BF16))
            xng0 = E2(nc.sbuf_tensor("xng0", [128, KD, TC], BF16))
            xng1 = E2(nc.sbuf_tensor("xng1", [128, KD, TC], BF16))
            xhh = E2(nc.sbuf_tensor("xhh", [128, NMI, TC + 8], F32))
            zst = E2(nc.sbuf_tensor("zst", [128, NMI, TC], F32))
            xct2 = E2(nc.sbuf_tensor("xct2", [128, NMI, TC], F32))
            xcb2 = E2(nc.sbuf_tensor("xcb2", [128, NMI, TC], BF16))
            cvac = E2(nc.sbuf_tensor("cvac", [128, NMI, TC], F32))
            Wxpb = E2(nc.sbuf_tensor("Wxpb", [128, NMI, 33], BF16))
            xpjt = E2(nc.sbuf_tensor("xpjt", [33, TC], F32))
            xng = [xng0, xng1]
            lw = P.dma("sync", lambda e: e.dma_start(
                out=Winb[:], in_=d_winb.ap().rearrange("(k p) m -> p k m", p=128)))
            for mi in range(NMI):
                tw = P.dma("sync", lambda e, mi=mi: e.dma_start(
                    out=var_t[:, 0:33], in_=wxpc[128 * mi:128 * (mi + 1), :]))
                P.wait("scalar", LD, tw)
                sw = P.op("scalar", lambda e, mi=mi: e.activation(Wxpb[:, mi, :], var_t[:, 0:33], AF.Copy), SC)
                P.wait("sync", SC, sw)
            P.wait("tensor", LD, lw)
            P.op("gpsimd", lambda e: e.memset(xhh[:, :, 0:3], 0.0), PC)
            P.wait("scalar", PC, PC.n)
            conv_done = None
            sil_done = None
            cast_done = None
            tpj_prev = None
            spj_prev = None
            zs_spill = None
            xc_spill = None
            pj_spill = None
            bank_ev = {}
            mm_last = [None, None]
            for k in range(NCH):
                bk = k % 2
                r0 = 4 * D * (k // 4) + D * (k % 4)
                c0 = TC * k
                if mm_last[bk] is not None:
                    P.wait("sync", TM, mm_last[bk])
                P.wait("sync", CD, ag_done[k // 4])
                lx = P.dma("sync", lambda e, r0=r0, bk=bk: e.dma_start(
                    out=xng[bk][:],
                    in_=d_xng.ap()[r0:r0 + D, :].rearrange("(k p) t -> p k t", p=128)))
                if k > 0:
                    P.wait("scalar", VC, conv_done)
                    P.op("scalar", lambda e: e.activation(xhh[:, :, 0:3], xhh[:, :, TC:TC + 3], AF.Copy))
                P.wait("tensor", LD, lx)
                ev = []
                for mi in range(KD):
                    psn = mi % 4
                    if bank_ev.get(psn) is not None:
                        P.wait("tensor", SC, bank_ev[psn])
                    tmm = None
                    for kb in range(KD):
                        tmm = P.op("tensor", lambda e, kb=kb, mi=mi, bk=bk, psn=psn: e.matmul(
                            PS[psn][:], Winb[:, kb, 128 * mi:128 * (mi + 1)], xng[bk][:, kb, :],
                            start=(kb == 0), stop=(kb == KD - 1)), TM if kb == KD - 1 else None)
                    if mi == KD - 1:
                        mm_last[bk] = tmm
                    P.wait("scalar", TM, tmm)
                    if mi < NMI:
                        evv = P.op("scalar", lambda e, mi=mi, psn=psn: e.activation(
                            xhh[:, mi, 3:3 + TC], PS[psn][:], AF.Copy), SC)
                    else:
                        if zs_spill is not None and mi == NMI:
                            P.wait("scalar", LD, zs_spill)
                        evv = P.op("scalar", lambda e, mi=mi, psn=psn: e.activation(
                            zst[:, mi - NMI, :], PS[psn][:], AF.Silu), SC)
                    ev.append(evv)
                    bank_ev[psn] = evv
                P.wait("sync", SC, ev[-1])
                zs_spill = P.dma("sync", lambda e, c0=c0: e.dma_start(
                    out=d_z_r[:, :, c0:c0 + TC], in_=zst[:]))
                P.wait("vector", SC, ev[NMI - 1])
                if sil_done is not None:
                    P.wait("vector", SC, sil_done)
                cv = None
                for mi in range(NMI):
                    P.op("vector", lambda e, mi=mi: e.tensor_scalar(
                        cvac[:, mi, :], xhh[:, mi, 0:TC],
                        cwt[:, 4 * mi + 0:4 * mi + 1], vect[:, 4 * mi + 0:4 * mi + 1],
                        ALU.mult, ALU.add))
                    for kk in range(1, 4):
                        cv = P.op("vector", lambda e, mi=mi, kk=kk: e.scalar_tensor_tensor(
                            cvac[:, mi, :], xhh[:, mi, kk:kk + TC],
                            cwt[:, 4 * mi + kk:4 * mi + kk + 1], cvac[:, mi, :],
                            ALU.mult, ALU.add), VC if (mi == NMI - 1 and kk == 3) else None)
                conv_done = cv
                P.wait("scalar", VC, cv)
                if xc_spill is not None:
                    P.wait("scalar", LD, xc_spill)
                sil_done = P.op("scalar", lambda e: e.activation(
                    xct2[:].rearrange("p m t -> p (m t)"), cvac[:].rearrange("p m t -> p (m t)"),
                    AF.Silu), SC)
                if tpj_prev is not None:
                    P.wait("scalar", TM, tpj_prev)
                cast_done = P.op("scalar", lambda e: e.activation(
                    xcb2[:].rearrange("p m t -> p (m t)"), xct2[:].rearrange("p m t -> p (m t)"),
                    AF.Copy), SC)
                P.wait("sync", SC, sil_done)
                xc_spill = P.dma("sync", lambda e, c0=c0: e.dma_start(
                    out=d_xc_r[:, :, c0:c0 + TC], in_=xct2[:]))
                P.wait("tensor", SC, cast_done)
                if spj_prev is not None:
                    P.wait("tensor", SC, spj_prev)
                tpj = None
                for mi in range(NMI):
                    tpj = P.op("tensor", lambda e, mi=mi: e.matmul(
                        PS[4][0:33, :], Wxpb[:, mi, :], xcb2[:, mi, :],
                        start=(mi == 0), stop=(mi == NMI - 1)), TM if mi == NMI - 1 else None)
                tpj_prev = tpj
                P.wait("scalar", TM, tpj)
                if pj_spill is not None:
                    P.wait("scalar", LD, pj_spill)
                spj = P.op("scalar", lambda e: e.activation(xpjt[:], PS[4][0:33, :], AF.Copy), SC)
                spj_prev = spj
                P.wait("sync", SC, spj)
                pj_spill = P.dma("sync", lambda e, c0=c0: e.dma_start(
                    out=d_xpj.ap()[:, c0:c0 + TC], in_=xpjt[:]))
            P.barrier()

        art = P.coll("AllReduce", ALU.add, [d_xpj.ap().opt()], [d_xpjr.ap().opt()],
                     wait_ld=P.LD.n)
        P.barrier()

        rs_done = [None] * NJ
        with ExitStack() as s2b:
            E3 = s2b.enter_context
            Wob = E3(nc.sbuf_tensor("Wob", [128, NMI, 1024], BF16))
            xct3 = [E3(nc.sbuf_tensor(f"xct3{i}", [128, NMI, TC], F32)) for i in range(2)]
            zsl = [E3(nc.sbuf_tensor(f"zsl{i}", [128, NMI, TC], F32)) for i in range(2)]
            dtt = E3(nc.sbuf_tensor("dtt", [128, NMI, TC], F32))
            ut = E3(nc.sbuf_tensor("ut", [128, NMI, TC], BF16))
            Bxt = E3(nc.sbuf_tensor("Bxt", [16, TC], F32))
            Cxt = E3(nc.sbuf_tensor("Cxt", [16, TC], F32))
            dtr = E3(nc.sbuf_tensor("dtr", [1, TC], F32))
            Bxb = E3(nc.sbuf_tensor("Bxb", [16, TC], BF16))
            Cxb = E3(nc.sbuf_tensor("Cxb", [16, TC], BF16))
            Cb = [E3(nc.sbuf_tensor(f"Cb{i}", [128, TC], BF16)) for i in range(2)]
            Bbb = [E3(nc.sbuf_tensor(f"Bbb{i}", [128, TC], BF16)) for i in range(2)]
            ab = [E3(nc.sbuf_tensor(f"ab{i}", [128, NMI * TC], F32)) for i in range(2)]
            bb3 = E3(nc.sbuf_tensor("bb3", [128, NMI * TC], BF16))
            hta = E3(nc.sbuf_tensor("hta", [128, NMI, TC], F32))
            tmpb = [E3(nc.sbuf_tensor(f"tmpb{i}", [128, NMI * TC], BF16)) for i in range(2)]
            yt = [E3(nc.sbuf_tensor(f"yt{i}", [128, NMI, TC], BF16)) for i in range(2)]
            gt = [E3(nc.sbuf_tensor(f"gt{i}", [128, NMI, TC], BF16)) for i in range(2)]
            eb2b = E3(nc.sbuf_tensor("eb2b", [128, NMI * TC], F32))
            rb2b = E3(nc.sbuf_tensor("rb2b", [128, NMI * TC], F32))
            wot = [E3(nc.sbuf_tensor(f"wot{i}", [128, TC], F32)) for i in range(2)]

            lw = P.dma("sync", lambda e: e.dma_start(
                out=Wob[:], in_=d_wob.ap().rearrange("(k p) m -> p k m", p=128)))
            P.op("vector", lambda e: e.memset(st[:], 0.0), VC)
            P.wait("tensor", LD, lw)

            roll = {"a3": {}, "dve": {}, "psB_mm": {}, "psC_ev": {}, "tmp": {}, "tmp_rd": {}}
            vg_k = [None, None]
            wo_last_k = [None, None]
            wo_bank_ev = {0: None, 1: None}
            dt_vsp = None
            u_done = None
            u_done_k = [None, None]
            b3_last = None
            bxb_done = None
            dtb_tm = [None, None]
            prev_tdt = None
            gl_n = 0
            for k in range(NCH):
                c0 = TC * k
                kt = slice(c0, c0 + TC)
                pk = k % 2
                if vg_k[pk] is not None:
                    P.wait("sync", VC, vg_k[pk])
                    P.wait("sync", PC, u_done_k[pk])
                l1 = P.dma("sync", lambda e, kt=kt, pk=pk: e.dma_start(out=xct3[pk][:], in_=d_xc_r[:, :, kt]))
                l2 = P.dma("sync", lambda e, kt=kt, pk=pk: e.dma_start(out=zsl[pk][:], in_=d_z_r[:, :, kt]))
                if bxb_done is not None:
                    P.wait("sync", SC, bxb_done)
                if prev_tdt is not None:
                    P.wait("sync", TM, prev_tdt)
                lB = P.dma("sync", lambda e, kt=kt: e.dma_start(out=Bxt[:], in_=d_xpjr.ap()[0:16, kt]))
                lC = P.dma("sync", lambda e, kt=kt: e.dma_start(out=Cxt[:], in_=d_xpjr.ap()[16:32, kt]))
                lD = P.dma("sync", lambda e, kt=kt: e.dma_start(out=dtr[:], in_=d_xpjr.ap()[32:33, kt]))
                P.wait("tensor", LD, lD)
                if dtb_tm[pk] is not None:
                    P.wait("tensor", VC, dt_vsp)
                tdt = P.op("tensor", lambda e, pk=pk: e.matmul(
                    PS[6 + pk][:], ones1t[:], dtr[:], start=True, stop=True), TM)
                dtb_tm[pk] = tdt
                prev_tdt = tdt
                P.wait("vector", TM, tdt)
                for mi in range(NMI):
                    P.op("vector", lambda e, mi=mi, pk=pk: e.tensor_scalar(
                        dtt[:, mi, :], PS[6 + pk][:],
                        vect[:, 4 * mi + 2:4 * mi + 3], vect[:, 4 * mi + 1:4 * mi + 2],
                        ALU.mult, ALU.add))
                vrl = P.op("vector", lambda e: e.tensor_scalar(
                    rb2b[:], dtt[:].rearrange("p m t -> p (m t)"), 0.0, None, ALU.max), VC)
                P.wait("scalar", VC, vrl)
                P.op("scalar", lambda e: e.activation(eb2b[:], dtt[:].rearrange("p m t -> p (m t)"), AF.Abs))
                P.op("scalar", lambda e: e.activation(eb2b[:], eb2b[:], AF.Exp, scale=-1.0))
                ssp = P.op("scalar", lambda e: e.activation(eb2b[:], eb2b[:], AF.Ln, bias=one_t[:, 0:1]), SC)
                P.wait("vector", SC, ssp)
                dt_vsp = P.op("vector", lambda e: e.tensor_tensor(
                    dtt[:].rearrange("p m t -> p (m t)"), rb2b[:], eb2b[:], ALU.add), VC)
                P.wait("gpsimd", VC, dt_vsp)
                P.wait("gpsimd", LD, l1)
                if b3_last is not None:
                    P.wait("gpsimd", VC, b3_last)
                u_done = P.op("gpsimd", lambda e, pk=pk: e.tensor_tensor(
                    ut[:].rearrange("p m t -> p (m t)"), dtt[:].rearrange("p m t -> p (m t)"),
                    xct3[pk][:].rearrange("p m t -> p (m t)"), ALU.mult), PC)
                u_done_k[pk] = u_done
                P.wait("scalar", LD, lC)
                if gl_n > 0:
                    P.wait("scalar", TM, roll["psB_mm"][gl_n - 1])
                P.op("scalar", lambda e: e.activation(Bxb[:], Bxt[:], AF.Copy))
                bxb_done = P.op("scalar", lambda e: e.activation(Cxb[:], Cxt[:], AF.Copy), SC)
                if debug and "dty" in dbg_parts and k == 0:
                    P.wait("sync", VC, dt_vsp)
                    P.dma("sync", lambda e: e.dma_start(out=dbg["dbg_dt"], in_=dtt[:].rearrange("p m t -> p (m t)")))
                first_v = True
                for n in range(NS):
                    g = gl_n
                    gl_n += 1
                    pp = g % 2
                    psB = PS[0 + 2 * pp]
                    psC = PS[1 + 2 * pp]
                    P.wait("tensor", SC, bxb_done)
                    if roll["psC_ev"].get(g - 2) is not None:
                        P.wait("tensor", SC, roll["psC_ev"][g - 2])
                    P.op("tensor", lambda e, n=n, psB=psB: e.matmul(
                        psB[:], selb[:, 128 * n:128 * (n + 1)], Bxb[:], start=True, stop=True))
                    tbc = P.op("tensor", lambda e, n=n, psC=psC: e.matmul(
                        psC[:], selb[:, 128 * n:128 * (n + 1)], Cxb[:], start=True, stop=True), TM)
                    roll["psB_mm"][g] = tbc
                    P.wait("scalar", TM, tbc)
                    if roll["tmp"].get(g - 2) is not None:
                        P.wait("scalar", PC, roll["tmp"][g - 2])
                    if roll["dve"].get(g - 2) is not None:
                        P.wait("scalar", VC, roll["dve"][g - 2])
                    P.op("scalar", lambda e, pp=pp, psB=psB: e.activation(Bbb[pp][:], psB[:], AF.Copy))
                    cev = P.op("scalar", lambda e, pp=pp, psC=psC: e.activation(Cb[pp][:], psC[:], AF.Copy), SC)
                    roll["psC_ev"][g] = cev
                    if roll["dve"].get(g - 2) is not None:
                        P.wait("scalar", VC, roll["dve"][g - 2])
                    if n == 0:
                        P.wait("scalar", VC, dt_vsp)
                    sa = None
                    for mi in range(NMI):
                        sa = P.op("scalar", lambda e, mi=mi, n=n, pp=pp: e.activation(
                            ab[pp][:, TC * mi:TC * (mi + 1)], dtt[:, mi, :], AF.Exp,
                            scale=Aneg[:, 16 * mi + n:16 * mi + n + 1]),
                            SC if mi == NMI - 1 else None)
                    roll["a3"][g] = sa
                    if n == 0:
                        P.wait("vector", PC, u_done)
                    P.wait("vector", SC, cev)
                    if roll["tmp"].get(g - 1) is not None:
                        P.wait("vector", PC, roll["tmp"][g - 1])
                    b3m = P.op("vector", lambda e, pp=pp: e.tensor_tensor(
                        bb3[:].rearrange("p (m t) -> p m t", m=NMI),
                        ut[:], bc_mid(Bbb[pp][:], NMI), ALU.mult), VC)
                    if n == NS - 1:
                        b3_last = b3m
                    P.wait("vector", SC, roll["a3"][g])
                    for mi in range(NMI):
                        P.op("vector", lambda e, mi=mi, n=n, pp=pp: e.tensor_tensor_scan(
                            hta[:, mi, :],
                            ab[pp][:, TC * mi:TC * (mi + 1)],
                            bb3[:, TC * mi:TC * (mi + 1)],
                            st[:, mi, n:n + 1], ALU.mult, ALU.add))
                    vsc = P.op("vector", lambda e, n=n: e.tensor_copy(
                        st[:, :, n:n + 1], hta[:, :, TC - 1:TC]), VC)
                    roll["dve"][g] = vsc
                    P.wait("gpsimd", VC, vsc)
                    P.wait("gpsimd", SC, cev)
                    if roll["tmp_rd"].get(g - 2) is not None:
                        P.wait("gpsimd", VC, roll["tmp_rd"][g - 2])
                    tm_ = P.op("gpsimd", lambda e, pp=pp: e.tensor_tensor(
                        tmpb[pp][:].rearrange("p (m t) -> p m t", m=NMI),
                        hta[:], bc_mid(Cb[pp][:], NMI), ALU.mult), PC)
                    roll["tmp"][g] = tm_
                    P.wait("vector", PC, tm_)
                    if n == 0:
                        yadd = P.op("vector", lambda e, pp=pp, pk=pk: e.tensor_scalar(
                            yt[pk][:].rearrange("p m t -> p (m t)"),
                            tmpb[pp][:], 0.0, None, ALU.add), VC)
                    else:
                        yadd = P.op("vector", lambda e, pp=pp, pk=pk: e.tensor_tensor(
                            yt[pk][:].rearrange("p m t -> p (m t)"),
                            yt[pk][:].rearrange("p m t -> p (m t)"),
                            tmpb[pp][:], ALU.add), VC)
                    roll["tmp_rd"][g] = yadd
                    if debug and "scan" in dbg_parts and k == 0 and n == 0:
                        P.wait("sync", PC, u_done)
                        du = P.dma("sync", lambda e: e.dma_start(out=dbg["dbg_u"], in_=ut[:].rearrange("p m t -> p (m t)")))
                        P.wait("sync", SC, roll["a3"][g])
                        da = P.dma("sync", lambda e, pp=pp: e.dma_start(out=dbg["dbg_a0"], in_=ab[pp][:]))
                        P.wait("sync", VC, roll["dve"][g])
                        dbm = P.dma("sync", lambda e: e.dma_start(out=dbg["dbg_b0"], in_=bb3[:]))
                        dh = P.dma("sync", lambda e: e.dma_start(out=dbg["dbg_h0"], in_=hta[:].rearrange("p m t -> p (m t)")))
                        P.wait("sync", VC, yadd)
                        dy0 = P.dma("sync", lambda e, pk=pk: e.dma_start(out=dbg["dbg_y0"], in_=yt[pk][:].rearrange("p m t -> p (m t)")))
                        P.wait("sync", LD, dy0)
                        P.barrier()
                P.wait("vector", LD, l1)
                for mi in range(NMI):
                    P.op("vector", lambda e, mi=mi, pk=pk: e.scalar_tensor_tensor(
                        yt[pk][:, mi, :], xct3[pk][:, mi, :],
                        vect[:, 4 * mi + 3:4 * mi + 4], yt[pk][:, mi, :], ALU.mult, ALU.add))
                P.wait("vector", LD, l2)
                if wo_last_k[pk] is not None:
                    P.wait("vector", TM, wo_last_k[pk])
                vg = P.op("vector", lambda e, pk=pk: e.tensor_tensor(
                    gt[pk][:].rearrange("p m t -> p (m t)"), yt[pk][:].rearrange("p m t -> p (m t)"),
                    zsl[pk][:].rearrange("p m t -> p (m t)"), ALU.mult), VC)
                vg_k[pk] = vg
                if debug and "dty" in dbg_parts and k == 0:
                    P.wait("sync", VC, vg)
                    P.dma("sync", lambda e: e.dma_start(out=dbg["dbg_y"], in_=yt[0][:].rearrange("p m t -> p (m t)")))
                slot = k
                P.wait("tensor", VC, vg)
                wo_dma = {}
                for mo in range(KD):
                    pw = mo % 2
                    psW = PS[4 + pw]
                    if wo_bank_ev[pw] is not None:
                        P.wait("tensor", SC, wo_bank_ev[pw])
                    two = None
                    for kd in range(NMI):
                        two = P.op("tensor", lambda e, kd=kd, mo=mo, pk=pk, psW=psW: e.matmul(
                            psW[:], Wob[:, kd, 128 * mo:128 * (mo + 1)], gt[pk][:, kd, :],
                            start=(kd == 0), stop=(kd == NMI - 1)), TM if kd == NMI - 1 else None)
                    if mo == KD - 1:
                        wo_last_k[pk] = two
                    P.wait("scalar", TM, two)
                    if wo_dma.get(mo - 2) is not None:
                        P.wait("scalar", WO, wo_dma[mo - 2])
                    swo = P.op("scalar", lambda e, pw=pw, psW=psW: e.activation(wot[pw][:], psW[:], AF.Copy), SC)
                    wo_bank_ev[pw] = swo
                    P.wait("sync", SC, swo)
                    wo_dma[mo] = P.dma("sync", lambda e, k=k, mo=mo, pw=pw: e.dma_start(
                        out=d_wop_j[k // 4].ap()[1024 * (k % 4) + 128 * mo:1024 * (k % 4) + 128 * (mo + 1), :],
                        in_=wot[pw][:]), ctr=WO)
                if k % 4 == 3:
                    P.wait("gpsimd", WO, 16 * KD * (k + 1))
                    CC.n += 1
                    P.op("gpsimd", lambda e, j=k // 4: e.collective_compute(
                        "ReduceScatter", ALU.add, GROUPS,
                        [d_wop_j[j].ap().opt()], [d_att_j[j].ap().opt()]).then_inc(CC.h, 1))
                if debug and "wop" in dbg_parts and k == 3:
                    P.wait("sync", LD, P.LD.n)
                    P.dma("sync", lambda e: e.dma_start(out=dbg["dbg_wop"], in_=d_wop_j[0].ap()))
            P.wait("gpsimd", CC, CC.n)
            P.op("gpsimd", lambda e: e.drain(fusable=False))
            P.barrier()

        with ExitStack() as sB:
            T = alloc_ffn_scope(sB, "b")
            prev = {}
            oysp = [None, None]
            xs_vc = [None, None]
            xs_sc = [None, None]
            ln2_rd = None
            cen_rd = None
            for j in range(NJ):
                par = j % 2
                jt = slice(TC * j, TC * (j + 1))
                xs = T["xs"][par]
                if j >= 2:
                    P.wait("sync", VC, xs_vc[par])
                    P.wait("sync", SC, xs_sc[par])
                    P.wait("sync", LD, oysp[par])
                l1 = P.dma("sync", lambda e, jt=jt, xs=xs: e.dma_start(out=xs[:], in_=d_x1_r[:, :, jt]))
                if ln2_rd is not None:
                    P.wait("sync", VC, ln2_rd)
                l2 = P.dma("sync", lambda e, jt=jt: e.dma_start(out=T["ln2t"][:], in_=d_ln2_r[:, :, jt]))
                if cen_rd is not None:
                    P.wait("sync", VC, cen_rd)
                l3 = P.dma("sync", lambda e, j=j: e.dma_start(
                    out=T["cen"][:],
                    in_=d_att_j[j].ap().rearrange("(k p) t -> p k t", p=128)))
                P.wait("vector", LD, l3)
                ln2_rd = P.op("vector", lambda e, xs=xs: e.tensor_tensor(
                    xs[:].rearrange("p k t -> p (k t)"), xs[:].rearrange("p k t -> p (k t)"),
                    T["ln2t"][:].rearrange("p k t -> p (k t)"), ALU.add), VC)
                vx2 = P.op("vector", lambda e, xs=xs: e.tensor_tensor(
                    xs[:].rearrange("p k t -> p (k t)"), xs[:].rearrange("p k t -> p (k t)"),
                    T["cen"][:].rearrange("p k t -> p (k t)"), ALU.add), VC)
                if debug and "p3" in dbg_parts:
                    P.wait("sync", LD, l3)
                    P.dma("sync", lambda e, j=j: e.dma_start(
                        out=dbg["dbg_att"][D * j:D * (j + 1), :].rearrange("(k p) t -> p k t", p=128),
                        in_=T["cen"][:]))
                s_r, c2m = ln_stats(T, xs, wait_vc=vx2)
                xs_sc[par] = c2m
                P.wait("vector", SC, s_r)
                vx = ln_apply_plain(T, xs, T["xnb"])
                cen_rd = vx
                xs_vc[par] = vx
                P.wait("tensor", VC, vx)
                prev = ffn(T, par, prev)
                xv = prev["xv"]
                P.wait("sync", VC, xv)
                oysp[par] = P.dma("sync", lambda e, jt=jt, xs=xs: e.dma_start(
                    out=o_y_r[:, :, jt], in_=xs[:]))

        P.wait("sync", LD, P.LD.n)

        block = E(nc.Block())
        P.emit(block)

    return nc


_DEBUG = False
_NC_CACHE = {}


_DBG_PARTS = ("p3",)


def _get_nc():
    key = (bool(_DEBUG), tuple(_DBG_PARTS))
    if key not in _NC_CACHE:
        _NC_CACHE[key] = build_program(debug=key[0], dbg_parts=key[1])
    return _NC_CACHE[key]


def kernel(L0_attn_x, L0_attn_ln_g, L0_attn_ln_b, L0_attn_W_in, L0_attn_conv_w,
           L0_attn_conv_b, L0_attn_W_xp, L0_attn_W_dt, L0_attn_b_dt, L0_attn_A_log,
           L0_attn_D, L0_attn_W_out, L0_ffn_w_gate, L0_ffn_w_up, L0_ffn_w_down):
    f32 = np.float32
    x = np.asarray(L0_attn_x, f32)
    W_in = np.asarray(L0_attn_W_in, f32)
    W_out = np.asarray(L0_attn_W_out, f32)
    W_xp = np.asarray(L0_attn_W_xp, f32)
    conv_w = np.asarray(L0_attn_conv_w, f32)
    conv_b = np.asarray(L0_attn_conv_b, f32)
    W_dt = np.asarray(L0_attn_W_dt, f32)
    b_dt = np.asarray(L0_attn_b_dt, f32)
    A_log = np.asarray(L0_attn_A_log, f32)
    Dvec = np.asarray(L0_attn_D, f32)
    ln_g = np.asarray(L0_attn_ln_g, f32)
    ln_b = np.asarray(L0_attn_ln_b, f32)
    wg = np.ascontiguousarray(np.asarray(L0_ffn_w_gate, np.int32).T)
    wu = np.ascontiguousarray(np.asarray(L0_ffn_w_up, np.int32).T)
    wd = np.ascontiguousarray(np.asarray(L0_ffn_w_down, np.int32).T)

    sel = np.zeros((16, 2048), f32)
    for n in range(16):
        sel[n, 128 * n:128 * (n + 1)] = 1.0
    ones128 = np.ones((128, 128), f32)
    ones1 = np.ones((1, 128), f32)
    lngb = np.zeros((128, 16), f32)
    for kb in range(KD):
        lngb[:, 2 * kb] = ln_g[128 * kb:128 * (kb + 1)]
        lngb[:, 2 * kb + 1] = ln_b[128 * kb:128 * (kb + 1)]

    in_maps = []
    for c in range(8):
        b, r = divmod(c, 4)
        sl = slice(512 * r, 512 * (r + 1))
        xc = np.concatenate(
            [x[b, TC * (4 * j + r):TC * (4 * j + r + 1), :].T for j in range(NJ)], axis=1)
        winc_c = np.concatenate([W_in[:, sl], W_in[:, 2048 + sl.start:2048 + sl.stop]], axis=1)
        cwc = conv_w[sl].reshape(4, 128, 4).transpose(1, 0, 2).reshape(128, 16)
        avals = A_log[sl].reshape(4, 128, 16).transpose(1, 0, 2).reshape(128, 64)
        vv = np.stack([conv_b[sl], b_dt[sl], W_dt[0, sl], Dvec[sl]], axis=-1)
        vv = vv.reshape(4, 128, 4).transpose(1, 0, 2).reshape(128, 16)
        in_maps.append(dict(
            x_fm=np.ascontiguousarray(xc),
            winc=np.ascontiguousarray(winc_c),
            woc=np.ascontiguousarray(W_out[sl]),
            wxpc=np.ascontiguousarray(W_xp[sl]),
            wgT=wg, wuT=wu, wdT=wd,
            cwc=np.ascontiguousarray(cwc),
            avals=np.ascontiguousarray(avals),
            vecs=np.ascontiguousarray(vv),
            lngb=lngb, sel=sel, ones128=ones128, ones1=ones1,
        ))

    nc = _get_nc()
    res = run_bass_kernel_spmd(nc, in_maps, core_ids=list(range(8)))

    out = np.empty((B, L, D), f32)
    for c in range(8):
        b, r = divmod(c, 4)
        oy = np.asarray(res.results[c]["o_y"], f32)
        for j in range(NJ):
            out[b, TC * (4 * j + r):TC * (4 * j + r + 1), :] = oy[:, TC * j:TC * (j + 1)].T
    if _DEBUG:
        kernel._dbg = res.results
    return out
